# version 02
# baseline (speedup 1.0000x reference)
import contextlib
import numpy as np
import ml_dtypes
import concourse.bass as bass
import concourse.mybir as mybir
from concourse.bass_utils import run_bass_kernel_spmd

F32 = mybir.dt.float32
BF16 = mybir.dt.bfloat16
I32 = mybir.dt.int32
AF = mybir.ActivationFunctionType
OP = mybir.AluOpType
AX = mybir.AxisListType.X
bfnp = ml_dtypes.bfloat16

NCORES = 8
T = 2048
TP = 2176
HALO = 3
D = 2048
NCH = 16
NTT = 17
Q = 128
NQ = 16
H_A, P_A, N_A = 32, 64, 32
PROJ_P = 4352
NOT_P = 34
CH_BC, CH_DT, CH_G0 = 16, 17, 18
LN_EPS = 1e-5
RMS_EPS = 1e-6
SLICES = [(0, 512), (512, 512), (1024, 512), (1536, 512), (2048, 128)]


def rep_cols(apv, nseg, nrep):
    pstep, pcnt = apv.ap[0]
    fstep = apv.ap[1][0]
    return bass.AP(tensor=apv.tensor, offset=apv.offset,
                   ap=[[pstep, pcnt], [fstep, nseg], [0, nrep]])


def rep_tile(apv, nrep, seglen):
    pstep, pcnt = apv.ap[0]
    fstep = apv.ap[1][0]
    return bass.AP(tensor=apv.tensor, offset=apv.offset,
                   ap=[[pstep, pcnt], [0, nrep], [fstep, seglen]])


def strided_ap(apv, start, step, count):
    pstep, pcnt = apv.ap[0]
    fstep = apv.ap[1][0]
    return bass.AP(tensor=apv.tensor, offset=apv.offset + start * fstep,
                   ap=[[pstep, pcnt], [fstep * step, count]])


class Env:

    def __init__(self, nc, es):
        self.nc = nc
        self.es = es
        self.sph = es.enter_context(nc.semaphore("ph"))
        self.total = 0
        self.pending = 0
        self.w = 0
        self.B = None

    def sbt(self, n, s, d):
        return self.es.enter_context(self.nc.sbuf_tensor(n, s, d))

    def pst(self, n, s, d=F32):
        return self.es.enter_context(self.nc.psum_tensor(n, s, d))

    def begin(self):
        assert self.pending == 0
        self.w = self.total

    def inc(self, n):
        self.pending += n

    def end(self):
        self.total += self.pending
        self.pending = 0

    def fin(self, e, n=1):
        e.drain()
        e.nop().then_inc(self.sph, n)
        self.inc(n)

    def setup_consts(self, values):
        vals = sorted(set(float(v) for v in values))
        self.cvals = vals
        self.ctile = self.sbt("cconsts", [128, max(1, len(vals))], F32)
        for i, v in enumerate(vals):
            self.nc.const_aps.register(F32, v, self.ctile[:, i:i + 1])

    def emit_const_memsets(self, e):
        for i, v in enumerate(self.cvals):
            e.memset(self.ctile[:, i:i + 1], v)



def emit_ln_transpose(env, src_dram, ntiles, xlnT, psT, tiles, add_dram=None,
                      eps=LN_EPS, out_dram=None, xt_out=None, xt_shift=0, xt_cols=None,
                      psF=None, idf_s=None, xstg=None):
    B, sph = env.B, env.sph
    xtok, xlt = tiles["xtok"], tiles["xlt"]
    tsc, stats, stats2 = tiles["tsc"], tiles["stats"], tiles["stats2"]
    tscf = (lambda j: tsc[j % 2][:]) if isinstance(tsc, list) else (lambda j: tsc[:])
    mcol, rcol, vcol = tiles["mcol"], tiles["rcol"], tiles["vcol"]
    idb_s = tiles["idb_s"]
    atok = tiles.get("atok")

    for j in range(ntiles + 1):
        env.begin()
        if j < ntiles:
            @B.sync
            def _(e, j=j):
                e.wait_ge(sph, env.w)
                e.dma_start(out=xtok[j % 2][:], in_=src_dram[128 * j:128 * (j + 1), :]).then_inc(sph, 16)
                env.inc(16)
            if add_dram is not None:
                @B.scalar
                def _(e, j=j):
                    e.wait_ge(sph, env.w)
                    e.dma_start(out=atok[j % 2][:], in_=add_dram[128 * j:128 * (j + 1), :]).then_inc(sph, 16)
                    env.inc(16)
        if j > 0:
            if add_dram is not None:
                @B.vector
                def _(e, j=j):
                    e.wait_ge(sph, env.w)
                    e.tensor_add(xtok[(j - 1) % 2][:], xtok[(j - 1) % 2][:], atok[(j - 1) % 2][:])
                    e.drain()
                    e.reduce_sum(stats[:, j - 1:j], xtok[(j - 1) % 2][:], AX)
                    env.fin(e)
                if out_dram is not None:
                    env.end(); env.begin()
                    @B.sync
                    def _(e, j=j):
                        e.wait_ge(sph, env.w)
                        e.dma_start(out=out_dram[128 * (j - 1):128 * j, :],
                                    in_=xtok[(j - 1) % 2][:]).then_inc(sph, 16)
                        env.inc(16)
                    @B.scalar
                    def _(e, j=j):
                        e.wait_ge(sph, env.w)
                        e.activation(out=tscf(j - 1), in_=xtok[(j - 1) % 2][:], func=AF.Square,
                                     accum_out=stats2[:, j - 1:j])
                        env.fin(e)
            else:
                @B.scalar
                def _(e, j=j):
                    e.wait_ge(sph, env.w)
                    e.activation(out=tscf(j - 1), in_=xtok[(j - 1) % 2][:], func=AF.Square,
                                 accum_out=stats2[:, j - 1:j])
                    env.fin(e)
                @B.vector
                def _(e, j=j):
                    e.wait_ge(sph, env.w)
                    e.reduce_sum(stats[:, j - 1:j], xtok[(j - 1) % 2][:], AX)
                    env.fin(e)
        if add_dram is not None and j > 0 and out_dram is None:
            @B.scalar
            def _(e, j=j):
                e.wait_ge(sph, env.w)
                e.activation(out=tscf(j - 1), in_=xtok[(j - 1) % 2][:], func=AF.Square,
                             accum_out=stats2[:, j - 1:j])
                env.fin(e)
        env.end()

    env.begin()
    @B.vector
    def _(e):
        e.wait_ge(sph, env.w)
        e.tensor_scalar_mul(mcol[:, 0:ntiles], stats[:, 0:ntiles], 1.0 / D)
        e.drain()
        e.tensor_mul(vcol[:, 0:ntiles], mcol[:, 0:ntiles], mcol[:, 0:ntiles])
        e.drain()
        e.tensor_scalar_mul(vcol[:, 0:ntiles], vcol[:, 0:ntiles], -1.0)
        e.drain()
        e.scalar_tensor_tensor(vcol[:, 0:ntiles], stats2[:, 0:ntiles], 1.0 / D,
                               vcol[:, 0:ntiles], OP.mult, OP.add)
        env.fin(e)
    env.end()
    env.begin()
    @B.scalar
    def _(e):
        e.wait_ge(sph, env.w)
        e.activation(out=rcol[:, 0:ntiles], in_=vcol[:, 0:ntiles], func=AF.Ln, bias=eps)
        e.drain()
        e.activation(out=rcol[:, 0:ntiles], in_=rcol[:, 0:ntiles], func=AF.Exp, scale=-0.5)
        env.fin(e)
    env.end()

    for j in range(ntiles + 1):
        env.begin()
        if j < ntiles:
            @B.sync
            def _(e, j=j):
                e.wait_ge(sph, env.w)
                e.dma_start(out=xtok[j % 2][:], in_=src_dram[128 * j:128 * (j + 1), :]).then_inc(sph, 16)
                env.inc(16)
            if add_dram is not None:
                @B.scalar
                def _(e, j=j):
                    e.wait_ge(sph, env.w)
                    e.dma_start(out=atok[j % 2][:], in_=add_dram[128 * j:128 * (j + 1), :]).then_inc(sph, 16)
                    env.inc(16)
        env.end()
        env.begin()
        if j < ntiles:
            @B.vector
            def _(e, j=j):
                e.wait_ge(sph, env.w)
                if add_dram is not None:
                    e.tensor_add(xtok[j % 2][:], xtok[j % 2][:], atok[j % 2][:])
                    e.drain()
                e.tensor_scalar(xlt[j % 2][:], xtok[j % 2][:], mcol[:, j:j + 1],
                                rcol[:, j:j + 1], OP.subtract, OP.mult)
                env.fin(e)
        if j > 0 and xt_out is not None:
            jj = j - 1
            tcol0 = 128 * jj - xt_shift
            for grp in range(4):
                @B.tensor
                def _(e, jj=jj, grp=grp):
                    e.wait_ge(sph, env.w)
                    for u in range(4):
                        k = grp * 4 + u
                        e.transpose(psF[:, 128 * u:128 * (u + 1)],
                                    xtok[jj % 2][:, 128 * k:128 * (k + 1)], idf_s[:])
                    env.fin(e)
                env.end()
                env.begin()
                @B.vector
                def _(e, grp=grp):
                    e.wait_ge(sph, env.w)
                    e.tensor_copy(xstg[:], psF[:, 0:512])
                    env.fin(e)
                env.end()
                env.begin()
                @B.sync
                def _(e, jj=jj, grp=grp, tcol0=tcol0):
                    e.wait_ge(sph, env.w)
                    a = max(0, -tcol0)
                    bnd = xt_cols - tcol0
                    b_ = min(128, bnd)
                    if b_ > a:
                        for u in range(4):
                            k = grp * 4 + u
                            e.dma_start(out=xt_out[128 * k:128 * (k + 1), tcol0 + a:tcol0 + b_],
                                        in_=xstg[:, 128 * u + a:128 * u + b_]).then_inc(sph, 16)
                            env.inc(16)
                env.end()
                env.begin()
        if j > 0:
            for half in range(2):
                @B.tensor
                def _(e, j=j, half=half):
                    e.wait_ge(sph, env.w)
                    for u in range(8):
                        k = half * 8 + u
                        e.transpose(psT[:, 1024 * half + 128 * u:1024 * half + 128 * (u + 1)],
                                    xlt[(j - 1) % 2][:, 128 * k:128 * (k + 1)], idb_s[:])
                    env.fin(e)
                env.end()
                env.begin()
                @B.scalar
                def _(e, j=j, half=half):
                    e.wait_ge(sph, env.w)
                    for u in range(8):
                        k = half * 8 + u
                        e.copy(xlnT[k][:, 128 * (j - 1):128 * j],
                               psT[:, 1024 * half + 128 * u:1024 * half + 128 * (u + 1)])
                    env.fin(e)
        env.end()


def emit_proj(env, wdram, wrows, out_chunks, xlnT, tok_slices, psum4, wbuf,
              evac_fn, dma_fn=None, rhs_fn=None, pre_fn=None):
    B, sph = env.B, env.sph
    KCH = wrows // 128
    NOG = out_chunks // 2
    iters = [(g, si) for g in range(NOG) for si in range(len(tok_slices))]
    NIT = len(iters)
    if rhs_fn is None:
        rhs_fn = lambda k, c0, cwid: xlnT[k][:, c0:c0 + cwid]
    ncols = out_chunks * 128

    env.begin()
    @B.gpsimd
    def _(e):
        e.wait_ge(sph, env.w)
        src = bass.AP(tensor=wdram.tensor, offset=wdram.offset + 0,
                      ap=[[ncols, 128], [ncols * 128, KCH], [1, 256]])
        e.dma_start(out=wbuf[0][:, 0:KCH * 256], in_=src).then_inc(sph, 16)
        env.inc(16)
    env.end()

    for i in range(NIT + 2):
        env.begin()
        if i < NIT:
            g, si = iters[i]
            c0, cwid = tok_slices[si]
            if si == 0 and g + 1 < NOG:
                @B.gpsimd
                def _(e, g=g):
                    e.wait_ge(sph, env.w)
                    src = bass.AP(tensor=wdram.tensor, offset=wdram.offset + (g + 1) * 256,
                                  ap=[[ncols, 128], [ncols * 128, KCH], [1, 256]])
                    e.dma_start(out=wbuf[(g + 1) % 2][:, 0:KCH * 256], in_=src).then_inc(sph, 16)
                    env.inc(16)
            @B.tensor
            def _(e, g=g, c0=c0, cwid=cwid, i=i):
                e.wait_ge(sph, env.w)
                pa = psum4[2 * (i % 2)]
                pb = psum4[2 * (i % 2) + 1]
                for k in range(KCH):
                    e.matmul(pa[:, 0:cwid], wbuf[g % 2][:, k * 256:k * 256 + 128],
                             rhs_fn(k, c0, cwid), start=(k == 0), stop=(k == KCH - 1))
                for k in range(KCH):
                    e.matmul(pb[:, 0:cwid], wbuf[g % 2][:, k * 256 + 128:k * 256 + 256],
                             rhs_fn(k, c0, cwid), start=(k == 0), stop=(k == KCH - 1))
                env.fin(e)
        if 0 < i <= NIT:
            gp, sip = iters[i - 1]
            c0p, cwp = tok_slices[sip]
            @B.vector
            def _(e, gp=gp, c0p=c0p, cwp=cwp, i=i):
                e.wait_ge(sph, env.w)
                pa = psum4[2 * ((i - 1) % 2)]
                pb = psum4[2 * ((i - 1) % 2) + 1]
                evac_fn(e, 2 * gp, c0p, cwp, pa, i - 1)
                evac_fn(e, 2 * gp + 1, c0p, cwp, pb, i - 1)
                env.fin(e)
        if (i >= 2 and dma_fn is not None) or (i < NIT and pre_fn is not None):
            @B.sync
            def _(e, i=i):
                e.wait_ge(sph, env.w)
                if i < NIT and pre_fn is not None:
                    pre_fn(e, i)
                if i >= 2 and dma_fn is not None:
                    gq, siq = iters[i - 2]
                    c0q, cwq = tok_slices[siq]
                    dma_fn(e, 2 * gq, c0q, cwq, i - 2)
                    dma_fn(e, 2 * gq + 1, c0q, cwq, i - 2)
        env.end()



def build_L1():
    nc = bass.Bass("TRN2", target_bir_lowering=False)
    xh = nc.dram_tensor("xh", [TP, D], F32, kind="ExternalInput").ap()
    Wip = nc.dram_tensor("Wip", [D, PROJ_P], F32, kind="ExternalInput").ap()
    bip = nc.dram_tensor("bip", [PROJ_P], F32, kind="ExternalInput").ap()
    cw = nc.dram_tensor("cw", [2112, 4], F32, kind="ExternalInput").ap()
    cb = nc.dram_tensor("cb", [2112], F32, kind="ExternalInput").ap()
    alog = nc.dram_tensor("alog", [H_A], F32, kind="ExternalInput").ap()
    dtb = nc.dram_tensor("dtb", [H_A], F32, kind="ExternalInput").ap()
    hmask = nc.dram_tensor("hmask", [128, HALO], F32, kind="ExternalInput").ap()
    idb = nc.dram_tensor("idb", [128, 128], BF16, kind="ExternalInput").ap()
    idf = nc.dram_tensor("idf", [128, 128], F32, kind="ExternalInput").ap()

    xTo = nc.dram_tensor("xTo", [D, T], F32, kind="ExternalOutput").ap()
    gateT = nc.dram_tensor("gateT", [D, TP], BF16, kind="ExternalOutput").ap()
    convT = nc.dram_tensor("convT", [2112, TP], BF16, kind="ExternalOutput").ap()
    dtT = nc.dram_tensor("dtT", [H_A, TP], F32, kind="ExternalOutput").ap()
    schk = nc.dram_tensor("schk", [NQ, N_A, D], F32, kind="ExternalOutput").ap()
    score = nc.dram_tensor("score", [N_A, D], F32, kind="ExternalOutput").ap()
    ltotc = nc.dram_tensor("ltotc", [H_A, NQ], F32, kind="ExternalOutput").ap()

    es = contextlib.ExitStack()
    with es:
        env = Env(nc, es)
        sbt, pst = env.sbt, env.pst

        hbc = [sbt(f"hbc{k}", [128, TP], BF16) for k in range(17)]
        dtt = sbt("dtt", [H_A, TP], F32)
        lbuf = sbt("lbuf", [H_A, Q * NQ], F32)
        cacc = [sbt(f"cacc{i}", [128, TP], F32) for i in range(2)]
        cwS = sbt("cwS", [128, 4 * 17], F32)
        cbS = sbt("cbS", [128, 17], F32)
        bipS = sbt("bipS", [128, NOT_P], F32)
        hm = sbt("hm", [128, HALO], F32)
        idb_s = sbt("idb_s", [128, 128], BF16)
        idf_s = sbt("idf_s", [128, 128], F32)
        onesr = sbt("onesr", [1, 128], F32)
        sA = sbt("sA", [H_A, 1], F32)
        sdtb = sbt("sdtb", [H_A, 1], F32)
        stats = sbt("stats", [128, NTT], F32)
        stats2 = sbt("stats2", [128, NTT], F32)
        mcol = sbt("mcol", [128, NTT], F32)
        rcol = sbt("rcol", [128, NTT], F32)
        vcol = sbt("vcol", [128, NTT], F32)
        ltrow = sbt("ltrow", [1, H_A], F32)
        erow = sbt("erow", [1, H_A], F32)

        xtok = [cacc[i][:, 0:D] for i in range(2)]
        env.setup_consts([LN_EPS, RMS_EPS, 1.0, -1.0, -0.5, 0.5, 2.0, 0.0, 1.0 / D, -32.0, 32.0])

        psum4 = [pst(f"ps{i}", [128, 512]) for i in range(4)]
        psT = pst("psT", [128, 2048], BF16)
        psF = pst("psF", [128, 512])
        psTb = pst("psTb", [128, 512], BF16)

        B = es.enter_context(nc.Block())
        env.B = B
        sph = env.sph

        esB = contextlib.ExitStack()
        sbB = lambda n, s, d: esB.enter_context(nc.sbuf_tensor(n, s, d))
        xlnT = [sbB(f"xlnT{k}", [128, TP], BF16) for k in range(NCH)]
        wbuf = [sbB(f"wbuf{i}", [128, 16 * 256], BF16) for i in range(2)]
        xlt = [sbB(f"xlt{i}", [128, D], BF16) for i in range(2)]
        gst = [[sbB(f"gst{i}_{h}", [128, 512], BF16) for h in range(2)] for i in range(2)]

        env.begin()
        @B.sync
        def _(e):
            e.dma_start(out=idb_s[:], in_=idb[:]).then_inc(sph, 16)
            e.dma_start(out=idf_s[:], in_=idf[:]).then_inc(sph, 16)
            e.dma_start(out=hm[:], in_=hmask[:]).then_inc(sph, 16)
            e.dma_start(out=sdtb[:, 0], in_=dtb[:]).then_inc(sph, 16)
            e.dma_start(out=sA[:, 0], in_=alog[:]).then_inc(sph, 16)
            for j in range(16):
                e.dma_start(out=cwS[:, 4 * j:4 * j + 4], in_=cw[128 * j:128 * (j + 1), :]).then_inc(sph, 16)
                e.dma_start(out=cbS[:, j:j + 1], in_=cb[128 * j:128 * (j + 1)]).then_inc(sph, 16)
            e.dma_start(out=cwS[0:32, 64:68], in_=cw[2048:2080, :]).then_inc(sph, 16)
            e.dma_start(out=cwS[32:64, 64:68], in_=cw[2080:2112, :]).then_inc(sph, 16)
            e.dma_start(out=cbS[0:32, 16:17], in_=cb[2048:2080]).then_inc(sph, 16)
            e.dma_start(out=cbS[32:64, 16:17], in_=cb[2080:2112]).then_inc(sph, 16)
            env.inc(16 * 41)
        @B.scalar
        def _(e):
            for ot in range(NOT_P):
                e.dma_start(out=bipS[:, ot:ot + 1], in_=bip[128 * ot:128 * (ot + 1)]).then_inc(sph, 16)
            env.inc(16 * NOT_P)
        @B.vector
        def _(e):
            e.memset(onesr[:], 1.0)
            e.drain()
            env.emit_const_memsets(e)
            env.fin(e)
        env.end()

        env.begin()
        @B.scalar
        def _(e):
            e.wait_ge(sph, env.w)
            e.activation(out=sA[:], in_=sA[:], func=AF.Exp)
            e.drain()
            e.mul(sA[:], sA[:], -1.0)
            env.fin(e)
        env.end()

        xstg = sbB("xstg", [128, 512], F32)
        ln_tiles = dict(xtok=xtok, xlt=xlt, tsc=xlt, stats=stats, stats2=stats2,
                        mcol=mcol, rcol=rcol, vcol=vcol, idb_s=idb_s)
        emit_ln_transpose(env, xh, NTT, xlnT, psT, ln_tiles,
                          xt_out=xTo, xt_shift=HALO, xt_cols=T,
                          psF=psF, idf_s=idf_s, xstg=xstg)

        def evac(e, ot, c0, cwid, ps, it):
            if ot <= 15:
                e.tensor_scalar(hbc[ot][:, c0:c0 + cwid], ps[:, 0:cwid],
                                bipS[:, ot:ot + 1], None, OP.add)
            elif ot == CH_BC:
                e.tensor_scalar(hbc[16][0:64, c0:c0 + cwid], ps[0:64, 0:cwid],
                                bipS[0:64, ot:ot + 1], None, OP.add)
            elif ot == CH_DT:
                e.tensor_scalar(dtt[0:32, c0:c0 + cwid], ps[0:32, 0:cwid],
                                bipS[0:32, ot:ot + 1], None, OP.add)
            else:
                e.tensor_scalar(gst[it % 2][ot % 2][:, 0:cwid], ps[:, 0:cwid],
                                bipS[:, ot:ot + 1], None, OP.add)

        def gdma(e, ot, c0, cwid, it):
            if ot >= CH_G0:
                e.dma_start(out=gateT[128 * (ot - CH_G0):128 * (ot - CH_G0 + 1), c0:c0 + cwid],
                            in_=gst[it % 2][ot % 2][:, 0:cwid]).then_inc(sph, 16)
                env.inc(16)

        emit_proj(env, Wip, D, NOT_P, xlnT, SLICES, psum4, wbuf, evac, dma_fn=gdma)

        esB.close()

        esC = contextlib.ExitStack()
        sbC = lambda n, s, d: esC.enter_context(nc.sbuf_tensor(n, s, d))
        dl = sbC("dl", [H_A, TP], F32)
        sctA = sbC("sctA", [H_A, TP], F32)
        sctB = sbC("sctB", [H_A, TP], F32)
        lcol = sbC("lcol", [128, H_A], F32)
        dtcol = sbC("dtcol", [128, H_A], F32)
        w1c = sbC("w1c", [128, H_A], F32)
        btokb = sbC("btokb", [128, N_A], BF16)
        xtk = sbC("xtk", [128, D], BF16)
        xw1 = sbC("xw1", [128, D], BF16)
        sloc = sbC("sloc", [N_A, D], F32)
        stmp = sbC("stmp", [N_A, D], F32)

        env.begin()
        @B.vector
        def _(e):
            e.wait_ge(sph, env.w)
            for k in range(16):
                e.tensor_mul(hbc[k][:, 0:HALO], hbc[k][:, 0:HALO], hm[:, 0:HALO])
            e.tensor_mul(hbc[16][0:64, 0:HALO], hbc[16][0:64, 0:HALO], hm[0:64, 0:HALO])
            e.drain()
            e.memset(sloc[:], 0.0)
            env.fin(e)
        env.end()

        conv_units = [(k, 128, k) for k in range(16)] + [(16, 64, 16)]
        for uj in range(len(conv_units) + 1):
            env.begin()
            if uj < len(conv_units):
                k, rows, wi = conv_units[uj]
                @B.vector
                def _(e, k=k, rows=rows, wi=wi, uj=uj):
                    e.wait_ge(sph, env.w)
                    a = cacc[uj % 2]
                    W_ = TP - HALO
                    e.tensor_scalar(a[0:rows, HALO:TP], hbc[k][0:rows, 0:W_],
                                    cwS[0:rows, 4 * wi:4 * wi + 1], None, OP.mult)
                    for tap in range(1, 4):
                        e.drain()
                        e.scalar_tensor_tensor(a[0:rows, HALO:TP], hbc[k][0:rows, tap:W_ + tap],
                                               cwS[0:rows, 4 * wi + tap:4 * wi + tap + 1],
                                               a[0:rows, HALO:TP], OP.mult, OP.add)
                    env.fin(e)
            if uj > 0:
                k, rows, wi = conv_units[uj - 1]
                @B.scalar
                def _(e, k=k, rows=rows, wi=wi, uj=uj):
                    e.wait_ge(sph, env.w)
                    e.activation(out=hbc[k][0:rows, HALO:TP], in_=cacc[(uj - 1) % 2][0:rows, HALO:TP],
                                 func=AF.Silu, bias=cbS[0:rows, wi:wi + 1])
                    env.fin(e)
            env.end()

        env.begin()
        @B.sync
        def _(e):
            e.wait_ge(sph, env.w)
            for k in range(16):
                e.dma_start(out=convT[128 * k:128 * (k + 1), :], in_=hbc[k][:]).then_inc(sph, 16)
            e.dma_start(out=convT[2048:2080, :], in_=hbc[16][0:32, :]).then_inc(sph, 16)
            e.dma_start(out=convT[2080:2112, :], in_=hbc[16][32:64, :]).then_inc(sph, 16)
            env.inc(16 * 18)
        @B.vector
        def _(e):
            e.wait_ge(sph, env.w)
            e.tensor_scalar(dtt[:], dtt[:], sdtb[:, 0:1], None, OP.add)
            e.drain()
            e.tensor_scalar_max(sctB[:], dtt[:], 0.0)
            env.fin(e)
        env.end()
        env.begin()
        @B.scalar
        def _(e):
            e.wait_ge(sph, env.w)
            e.activation(out=sctA[:], in_=dtt[:], func=AF.Abs)
            e.drain()
            e.activation(out=sctA[:], in_=sctA[:], func=AF.Exp, scale=-1.0)
            e.drain()
            e.activation(out=sctA[:], in_=sctA[:], func=AF.Ln, bias=1.0)
            env.fin(e)
        env.end()
        env.begin()
        @B.vector
        def _(e):
            e.wait_ge(sph, env.w)
            e.tensor_add(dtt[:], sctA[:], sctB[:])
            e.drain()
            e.tensor_scalar(dl[:], dtt[:], sA[:, 0:1], None, OP.mult)
            env.fin(e)
        env.end()
        env.begin()
        @B.sync
        def _(e):
            e.wait_ge(sph, env.w)
            e.dma_start(out=dtT[:], in_=dtt[:]).then_inc(sph, 16)
            env.inc(16)
        @B.vector
        def _(e):
            e.wait_ge(sph, env.w)
            for c in range(NQ):
                e.tensor_tensor_scan(lbuf[:, Q * c:Q * (c + 1)],
                                     dl[:, HALO + Q * c:HALO + Q * (c + 1)],
                                     dl[:, HALO + Q * c:HALO + Q * (c + 1)],
                                     0.0, OP.add, OP.bypass)
            env.fin(e)
        env.end()

        env.begin()
        @B.sync
        def _(e):
            e.wait_ge(sph, env.w)
            src = strided_ap(lbuf[:], Q - 1, Q, NQ)
            e.dma_start(out=ltotc[:], in_=src, allow_slow_non_contiguous=True).then_inc(sph, 16)
            env.inc(16)
        env.end()

        for c in range(NQ):
            cb0 = HALO + Q * c
            env.begin()
            @B.tensor
            def _(e, c=c, cb0=cb0):
                e.wait_ge(sph, env.w)
                e.transpose(psF[:, 0:32], lbuf[0:32, Q * c:Q * (c + 1)], idf_s[0:32, 0:32])
                e.transpose(psF[:, 32:64], dtt[0:32, cb0:cb0 + Q], idf_s[0:32, 0:32])
                env.fin(e)
            @B.sync
            def _(e, c=c):
                e.wait_ge(sph, env.w)
                e.dma_start(out=ltrow[0:1, 0:H_A], in_=lbuf[:, Q * c + Q - 1:Q * c + Q],
                            allow_slow_non_contiguous=True).then_inc(sph, 16)
                env.inc(16)
            env.end()
            env.begin()
            @B.vector
            def _(e, c=c):
                e.wait_ge(sph, env.w)
                e.tensor_copy(lcol[:, :], psF[:, 0:32])
                e.drain()
                e.tensor_copy(dtcol[:, :], psF[:, 32:64])
                e.drain()
                e.tensor_scalar_mul(lcol[:, :], lcol[:, :], -1.0)
                env.fin(e)
            @B.scalar
            def _(e, c=c):
                e.wait_ge(sph, env.w)
                e.activation(out=erow[:], in_=ltrow[:], func=AF.Exp)
                env.fin(e)
            env.end()
            env.begin()
            @B.tensor
            def _(e, c=c, cb0=cb0):
                e.wait_ge(sph, env.w)
                for k in range(16):
                    e.transpose(psT[:, 128 * k:128 * (k + 1)], hbc[k][:, cb0:cb0 + Q], idb_s[:])
                env.fin(e)
            env.end()
            env.begin()
            @B.scalar
            def _(e, c=c):
                e.wait_ge(sph, env.w)
                e.copy(xtk[:], psT[:])
                env.fin(e)
            @B.tensor
            def _(e, c=c, cb0=cb0):
                e.wait_ge(sph, env.w)
                e.transpose(psTb[:, 0:32], hbc[16][0:32, cb0:cb0 + Q], idb_s[0:32, 0:32])
                env.fin(e)
            env.end()
            env.begin()
            @B.tensor
            def _(e, c=c):
                e.wait_ge(sph, env.w)
                e.matmul(psF[:, 64:96], onesr[:, :], ltrow[0:1, :], start=True, stop=True)
                env.fin(e)
            @B.scalar
            def _(e, c=c):
                e.wait_ge(sph, env.w)
                e.copy(btokb[:], psTb[:, 0:32])
                env.fin(e)
            env.end()
            env.begin()
            @B.vector
            def _(e, c=c):
                e.wait_ge(sph, env.w)
                e.tensor_tensor(w1c[:], psF[:, 64:96], lcol[:], OP.add)
                env.fin(e)
            env.end()
            env.begin()
            @B.scalar
            def _(e, c=c):
                e.wait_ge(sph, env.w)
                e.activation(out=w1c[:], in_=w1c[:], func=AF.Exp)
                env.fin(e)
            env.end()
            env.begin()
            @B.vector
            def _(e, c=c):
                e.wait_ge(sph, env.w)
                e.tensor_mul(w1c[:], w1c[:], dtcol[:])
                e.drain()
                e.tensor_tensor(xw1[:].rearrange("p (h u) -> p h u", u=P_A),
                                xtk[:].rearrange("p (h u) -> p h u", u=P_A),
                                rep_cols(w1c[:, 0:H_A], H_A, P_A), OP.mult)
                env.fin(e)
            env.end()
            env.begin()
            @B.tensor
            def _(e, c=c):
                e.wait_ge(sph, env.w)
                for h in range(H_A):
                    ps_ = psum4[h // 8]
                    e.matmul(ps_[0:N_A, (h % 8) * 64:(h % 8) * 64 + 64],
                             btokb[:], xw1[:, P_A * h:P_A * (h + 1)],
                             start=True, stop=True)
                env.fin(e)
            env.end()
            env.begin()
            @B.vector
            def _(e, c=c):
                e.wait_ge(sph, env.w)
                for qq in range(4):
                    e.tensor_copy(stmp[:, 512 * qq:512 * (qq + 1)], psum4[qq][0:N_A, :])
                env.fin(e)
            env.end()
            env.begin()
            @B.sync
            def _(e, c=c):
                e.wait_ge(sph, env.w)
                e.dma_start(out=schk[c, :, :], in_=stmp[:]).then_inc(sph, 16)
                env.inc(16)
            env.end()
            for qq in range(4):
                env.begin()
                @B.tensor
                def _(e, c=c, qq=qq):
                    e.wait_ge(sph, env.w)
                    e.matmul(psF[0:N_A, 0:512], onesr[0:1, 0:N_A],
                             rep_cols(erow[0:1, 8 * qq:8 * qq + 8], 8, 64),
                             start=True, stop=True)
                    env.fin(e)
                env.end()
                env.begin()
                @B.vector
                def _(e, c=c, qq=qq):
                    e.wait_ge(sph, env.w)
                    e.tensor_tensor(sloc[:, 512 * qq:512 * (qq + 1)],
                                    sloc[:, 512 * qq:512 * (qq + 1)], psF[0:N_A, 0:512], OP.mult)
                    e.drain()
                    e.tensor_add(sloc[:, 512 * qq:512 * (qq + 1)],
                                 sloc[:, 512 * qq:512 * (qq + 1)],
                                 stmp[:, 512 * qq:512 * (qq + 1)])
                    env.fin(e)
                env.end()

        env.begin()
        @B.sync
        def _(e):
            e.wait_ge(sph, env.w)
            e.dma_start(out=score[:], in_=sloc[:]).then_inc(sph, 16)
            env.inc(16)
        env.end()
        env.begin()
        @B.sync
        def _(e):
            e.wait_ge(sph, env.w)
        env.end()
        esC.close()

    return nc



def reorder_Wip(W_in):
    out = np.zeros((W_in.shape[0], PROJ_P), W_in.dtype)
    out[:, 0:2048] = W_in[:, 2048:4096]
    out[:, 2048:2080] = W_in[:, 4096:4128]
    out[:, 2080:2112] = W_in[:, 4128:4160]
    out[:, 2176:2208] = W_in[:, 4160:4192]
    out[:, 2304:4352] = W_in[:, 0:2048]
    return out


def reorder_bip(b_in):
    out = np.zeros((PROJ_P,), b_in.dtype)
    out[0:2048] = b_in[2048:4096]
    out[2048:2080] = b_in[4096:4128]
    out[2080:2112] = b_in[4128:4160]
    out[2176:2208] = b_in[4160:4192]
    out[2304:4352] = b_in[0:2048]
    return out


def l1_inputs(inputs, core):
    x = inputs["L0_attn_a_x"]
    b, t0 = core // 4, (core % 4) * T
    xhv = np.zeros((TP, D), np.float32)
    if t0 > 0:
        xhv[0:HALO] = x[b, t0 - HALO:t0]
    xhv[HALO:HALO + T] = x[b, t0:t0 + T]
    hm = np.ones((128, HALO), np.float32) if t0 > 0 else np.zeros((128, HALO), np.float32)
    return {
        "xh": xhv,
        "Wip": reorder_Wip(np.asarray(inputs["L0_attn_a_W_in"])),
        "bip": reorder_bip(np.asarray(inputs["L0_attn_a_b_in"])),
        "cw": np.asarray(inputs["L0_attn_a_conv_w"]),
        "cb": np.asarray(inputs["L0_attn_a_conv_b"]),
        "alog": np.asarray(inputs["L0_attn_a_A_log"]),
        "dtb": np.asarray(inputs["L0_attn_a_dt_bias"]),
        "hmask": hm,
        "idb": np.eye(128, dtype=bfnp),
        "idf": np.eye(128, dtype=np.float32),
    }


def emit_out_ln_scores(env, nc, Wo, rhs_tile_fn, xT, R1T, xlnT_o, xlntok_o,
                       scores_o, vals_o, hist_o, wg, idb_s, onesr, kch=16,
                       free_rhs=None):
    B, sph = env.B, env.sph
    esW = contextlib.ExitStack()
    sbW = lambda n, s, d: esW.enter_context(nc.sbuf_tensor(n, s, d))
    psW = contextlib.ExitStack()
    psum4 = [psW.enter_context(nc.psum_tensor(f"psw{i}", [128, 512], F32)) for i in range(4)]
    wbufW = [sbW(f"wbufW{i}", [128, kch * 256], BF16) for i in range(2)]
    xTs = [sbW(f"xTs{i}", [128, 1024], F32) for i in range(2)]
    r1s = [sbW(f"r1s{i}", [128, 1024], F32) for i in range(2)]
    SL4 = [(0, 512), (512, 512), (1024, 512), (1536, 512)]

    def w_rhs(k, c0, cwid):
        return rhs_tile_fn(k, c0, cwid)

    def w_pre(e, it):
        g, si = it // 4, it % 4
        c0, cwid = SL4[si]
        for half in range(2):
            ot = 2 * g + half
            e.dma_start(out=xTs[it % 2][:, 512 * half:512 * half + cwid],
                        in_=xT[128 * ot:128 * (ot + 1), c0:c0 + cwid]).then_inc(sph, 16)
            env.inc(16)

    def w_evac(e, ot, c0, cwid, ps, it):
        half = ot % 2
        e.tensor_tensor(r1s[it % 2][:, 512 * half:512 * half + cwid], ps[:, 0:cwid],
                        xTs[it % 2][:, 512 * half:512 * half + cwid], OP.add)

    def w_dma(e, ot, c0, cwid, it):
        half = ot % 2
        e.dma_start(out=R1T[128 * ot:128 * (ot + 1), c0:c0 + cwid],
                    in_=r1s[it % 2][:, 512 * half:512 * half + cwid]).then_inc(sph, 16)
        env.inc(16)

    emit_proj(env, Wo, kch * 128, 16, None, SL4, psum4, wbufW, w_evac,
              dma_fn=w_dma, rhs_fn=w_rhs, pre_fn=w_pre)

    psW.close()
    esW.close()
    if free_rhs is not None:
        free_rhs()

    esL = contextlib.ExitStack()
    sbL = lambda n, s, d: esL.enter_context(nc.sbuf_tensor(n, s, d))
    psL = contextlib.ExitStack()
    psS2a = psL.enter_context(nc.psum_tensor("psS2a", [1, 512], F32))
    psS2b = psL.enter_context(nc.psum_tensor("psS2b", [1, 512], F32))
    r1blk = [sbL(f"r1blk{i}", [128, 512], F32) for i in range(3)]
    sqblk = [sbL(f"sqblk{i}", [128, 512], F32) for i in range(2)]
    onescF = sbL("onescF", [128, 1], F32)
    mrow = sbL("mrow", [1, T], F32)
    srow = sbL("srow", [1, T], F32)
    rrow = sbL("rrow", [1, T], F32)

    env.begin()
    @B.vector
    def _(e):
        e.wait_ge(sph, env.w)
        e.memset(onescF[:], 1.0)
        env.fin(e)
    env.end()

    blocks = [(s, ot) for s in range(4) for ot in range(16)]
    for i in range(len(blocks) + 2):
        env.begin()
        if i < len(blocks):
            s, ot = blocks[i]
            @B.sync
            def _(e, s=s, ot=ot, i=i):
                e.wait_ge(sph, env.w)
                e.dma_start(out=r1blk[i % 3][:], in_=R1T[128 * ot:128 * (ot + 1),
                                                       512 * s:512 * (s + 1)]).then_inc(sph, 16)
                env.inc(16)
        if 1 <= i <= len(blocks):
            s, ot = blocks[i - 1]
            @B.scalar
            def _(e, s=s, ot=ot, i=i):
                e.wait_ge(sph, env.w)
                e.activation(out=sqblk[(i - 1) % 2][:], in_=r1blk[(i - 1) % 3][:], func=AF.Square)
                env.fin(e)
        if 2 <= i <= len(blocks) + 1:
            s, ot = blocks[i - 2]
            @B.tensor
            def _(e, s=s, ot=ot, i=i):
                e.wait_ge(sph, env.w)
                e.matmul(psS2a[0:1, :], onescF[:, 0:1], r1blk[(i - 2) % 3][:],
                         start=(ot == 0), stop=(ot == 15))
                e.matmul(psS2b[0:1, :], onescF[:, 0:1], sqblk[(i - 2) % 2][:],
                         start=(ot == 0), stop=(ot == 15))
                env.fin(e)
            if ot == 15:
                env.end()
                env.begin()
                @B.vector
                def _(e, s=s):
                    e.wait_ge(sph, env.w)
                    e.tensor_scalar_mul(mrow[0:1, 512 * s:512 * (s + 1)], psS2a[0:1, :], 1.0 / D)
                    e.drain()
                    e.tensor_scalar_mul(srow[0:1, 512 * s:512 * (s + 1)], psS2b[0:1, :], 1.0 / D)
                    env.fin(e)
        env.end()

    env.begin()
    @B.vector
    def _(e):
        e.wait_ge(sph, env.w)
        e.tensor_mul(rrow[:], mrow[:], mrow[:])
        e.drain()
        e.tensor_sub(rrow[:], srow[:], rrow[:])
        env.fin(e)
    env.end()
    env.begin()
    @B.scalar
    def _(e):
        e.wait_ge(sph, env.w)
        e.activation(out=rrow[:], in_=rrow[:], func=AF.Ln, bias=LN_EPS)
        e.drain()
        e.activation(out=rrow[:], in_=rrow[:], func=AF.Exp, scale=-0.5)
        env.fin(e)
    env.end()

    esX = contextlib.ExitStack()
    sbX = lambda n, s, d: esX.enter_context(nc.sbuf_tensor(n, s, d))
    psX = contextlib.ExitStack()
    xln2T = [sbX(f"xln2T{k}", [128, T], BF16) for k in range(16)]
    mbc = sbX("mbc", [128, T], F32)
    rbc = sbX("rbc", [128, T], F32)
    wgS = sbX("wgS", [128, 16 * 64], BF16)
    scoreS = sbX("scoreS", [128, 16 * 64], F32)
    valsS = sbX("valsS", [128, 16 * 8], F32)
    mbuf = sbX("mbuf", [128, 64], F32)
    histS = sbX("histS", [64, 1], F32)
    xtst = sbX("xtst", [128, 512], BF16)
    psB1 = psX.enter_context(nc.psum_tensor("psB1", [128, 512], F32))
    psT2 = psX.enter_context(nc.psum_tensor("psT2", [128, 2048], BF16))
    psSC = psX.enter_context(nc.psum_tensor("psSC", [128, 64], F32))
    psH = psX.enter_context(nc.psum_tensor("psH", [64, 64], F32))

    for s in range(4):
        env.begin()
        @B.tensor
        def _(e, s=s):
            e.wait_ge(sph, env.w)
            e.matmul(psB1[:, :], onesr[:, :], mrow[0:1, 512 * s:512 * (s + 1)],
                     start=True, stop=True)
            env.fin(e)
        env.end()
        env.begin()
        @B.vector
        def _(e, s=s):
            e.wait_ge(sph, env.w)
            e.tensor_copy(mbc[:, 512 * s:512 * (s + 1)], psB1[:, :])
            env.fin(e)
        env.end()
        env.begin()
        @B.tensor
        def _(e, s=s):
            e.wait_ge(sph, env.w)
            e.matmul(psB1[:, :], onesr[:, :], rrow[0:1, 512 * s:512 * (s + 1)],
                     start=True, stop=True)
            env.fin(e)
        env.end()
        env.begin()
        @B.vector
        def _(e, s=s):
            e.wait_ge(sph, env.w)
            e.tensor_copy(rbc[:, 512 * s:512 * (s + 1)], psB1[:, :])
            env.fin(e)
        env.end()

    env.begin()
    @B.gpsimd
    def _(e):
        e.wait_ge(sph, env.w)
        src2 = bass.AP(tensor=wg.tensor, offset=wg.offset,
                       ap=[[64, 128], [64 * 128, 16], [1, 64]])
        e.dma_start(out=wgS[:], in_=src2).then_inc(sph, 16)
        env.inc(16)
    env.end()
    for i in range(len(blocks) + 1):
        env.begin()
        if i < len(blocks):
            s, ot = blocks[i]
            @B.sync
            def _(e, s=s, ot=ot, i=i):
                e.wait_ge(sph, env.w)
                e.dma_start(out=r1blk[i % 2][:], in_=R1T[128 * ot:128 * (ot + 1),
                                                       512 * s:512 * (s + 1)]).then_inc(sph, 16)
                env.inc(16)
        if i >= 1:
            s, ot = blocks[i - 1]
            @B.vector
            def _(e, s=s, ot=ot, i=i):
                e.wait_ge(sph, env.w)
                e.tensor_sub(sqblk[(i - 1) % 2][:], r1blk[(i - 1) % 2][:],
                             mbc[:, 512 * s:512 * (s + 1)])
                e.drain()
                e.tensor_mul(xln2T[ot][:, 512 * s:512 * (s + 1)], sqblk[(i - 1) % 2][:],
                             rbc[:, 512 * s:512 * (s + 1)])
                env.fin(e)
        env.end()

    env.begin()
    @B.sync
    def _(e):
        e.wait_ge(sph, env.w)
        for k in range(16):
            e.dma_start(out=xlnT_o[128 * k:128 * (k + 1), :], in_=xln2T[k][:]).then_inc(sph, 16)
        env.inc(16 * 16)
    env.end()

    for tt in range(16):
        for half in range(2):
            env.begin()
            @B.tensor
            def _(e, tt=tt, half=half):
                e.wait_ge(sph, env.w)
                for u in range(8):
                    k = half * 8 + u
                    e.transpose(psT2[:, 1024 * half + 128 * u:1024 * half + 128 * (u + 1)],
                                xln2T[k][:, 128 * tt:128 * (tt + 1)], idb_s[:])
                env.fin(e)
            env.end()
            env.begin()
            @B.scalar
            def _(e, tt=tt, half=half):
                e.wait_ge(sph, env.w)
                e.copy(xtst[:], psT2[:, 1024 * half:1024 * half + 512])
                e.drain()
                e.copy(xtst[:] if False else xtst[:], psT2[:, 1024 * half:1024 * half + 512]) if False else None
                env.fin(e)
            @B.sync
            def _(e, tt=tt, half=half):
                pass
            env.end()
            env.begin()
            @B.sync
            def _(e, tt=tt, half=half):
                e.wait_ge(sph, env.w)
                e.dma_start(out=xlntok_o[128 * tt:128 * (tt + 1), 1024 * half:1024 * half + 512],
                            in_=xtst[:]).then_inc(sph, 16)
                env.inc(16)
            env.end()
            env.begin()
            @B.scalar
            def _(e, tt=tt, half=half):
                e.wait_ge(sph, env.w)
                e.copy(xtst[:], psT2[:, 1024 * half + 512:1024 * half + 1024])
                env.fin(e)
            env.end()
            env.begin()
            @B.sync
            def _(e, tt=tt, half=half):
                e.wait_ge(sph, env.w)
                e.dma_start(out=xlntok_o[128 * tt:128 * (tt + 1), 1024 * half + 512:1024 * (half + 1)],
                            in_=xtst[:]).then_inc(sph, 16)
                env.inc(16)
            env.end()
        env.begin()
        @B.tensor
        def _(e, tt=tt):
            e.wait_ge(sph, env.w)
            for k in range(16):
                e.matmul(psSC[:, :], xln2T[k][:, 128 * tt:128 * (tt + 1)],
                         wgS[:, 64 * k:64 * (k + 1)], start=(k == 0), stop=(k == 15))
            env.fin(e)
        env.end()
        env.begin()
        @B.scalar
        def _(e, tt=tt):
            e.wait_ge(sph, env.w)
            e.activation(out=scoreS[:, 64 * tt:64 * (tt + 1)], in_=psSC[:, :], func=AF.Sigmoid)
            env.fin(e)
        env.end()
        env.begin()
        @B.vector
        def _(e, tt=tt):
            e.wait_ge(sph, env.w)
            e.max(out=valsS[:, 8 * tt:8 * (tt + 1)], in_=scoreS[:, 64 * tt:64 * (tt + 1)])
            e.drain()
            e.tensor_scalar(mbuf[:], scoreS[:, 64 * tt:64 * (tt + 1)],
                            valsS[:, 8 * tt + 5:8 * tt + 6], None, OP.is_ge)
            env.fin(e)
        env.end()
        env.begin()
        @B.tensor
        def _(e, tt=tt):
            e.wait_ge(sph, env.w)
            e.matmul(psH[:, 0:1], mbuf[:], onescF[:, 0:1], start=(tt == 0), stop=(tt == 15))
            env.fin(e)
        @B.sync
        def _(e, tt=tt):
            e.wait_ge(sph, env.w)
            e.dma_start(out=scores_o[128 * tt:128 * (tt + 1), :],
                        in_=scoreS[:, 64 * tt:64 * (tt + 1)]).then_inc(sph, 16)
            e.dma_start(out=vals_o[128 * tt:128 * (tt + 1), :],
                        in_=valsS[:, 8 * tt:8 * (tt + 1)]).then_inc(sph, 16)
            env.inc(32)
        env.end()

    env.begin()
    @B.vector
    def _(e):
        e.wait_ge(sph, env.w)
        e.tensor_copy(histS[:], psH[0:64, 0:1])
        env.fin(e)
    env.end()
    env.begin()
    @B.sync
    def _(e):
        e.wait_ge(sph, env.w)
        e.dma_start(out=hist_o[:], in_=histS[:]).then_inc(sph, 16)
        env.inc(16)
    env.end()
    env.begin()
    @B.sync
    def _(e):
        e.wait_ge(sph, env.w)
    env.end()
    psX.close()
    esX.close()
    psL.close()
    esL.close()



def build_L2():
    nc = bass.Bass("TRN2", target_bir_lowering=False)
    xT = nc.dram_tensor("xT", [D, T], F32, kind="ExternalInput").ap()
    gateT = nc.dram_tensor("gateT", [D, TP], BF16, kind="ExternalInput").ap()
    convT = nc.dram_tensor("convT", [2112, TP], BF16, kind="ExternalInput").ap()
    dtT = nc.dram_tensor("dtT", [H_A, TP], F32, kind="ExternalInput").ap()
    schk = nc.dram_tensor("schk", [NQ, N_A, D], F32, kind="ExternalInput").ap()
    score_all = nc.dram_tensor("score_all", [NCORES, N_A, D], F32, kind="ExternalInput").ap()
    ltotc_all = nc.dram_tensor("ltotc_all", [NCORES, H_A, NQ], F32, kind="ExternalInput").ap()
    pmrow = nc.dram_tensor("pmrow", [1, NCORES], F32, kind="ExternalInput").ap()
    alog = nc.dram_tensor("alog", [H_A], F32, kind="ExternalInput").ap()
    Wo = nc.dram_tensor("Wo", [D, D], F32, kind="ExternalInput").ap()
    nw = nc.dram_tensor("nw", [D], F32, kind="ExternalInput").ap()
    dex = nc.dram_tensor("dex", [D], F32, kind="ExternalInput").ap()
    wg = nc.dram_tensor("wg", [D, 64], F32, kind="ExternalInput").ap()
    idb = nc.dram_tensor("idb", [128, 128], BF16, kind="ExternalInput").ap()
    idf = nc.dram_tensor("idf", [128, 128], F32, kind="ExternalInput").ap()
    utri = nc.dram_tensor("utri", [128, 128], F32, kind="ExternalInput").ap()

    R1T = nc.dram_tensor("R1T", [D, T], F32, kind="ExternalOutput").ap()
    xlnT_o = nc.dram_tensor("xlnT_o", [D, T], BF16, kind="ExternalOutput").ap()
    xlntok_o = nc.dram_tensor("xlntok_o", [T, D], BF16, kind="ExternalOutput").ap()
    scores_o = nc.dram_tensor("scores_o", [T, 64], F32, kind="ExternalOutput").ap()
    vals_o = nc.dram_tensor("vals_o", [T, 8], F32, kind="ExternalOutput").ap()
    hist_o = nc.dram_tensor("hist_o", [64, 1], F32, kind="ExternalOutput").ap()

    es = contextlib.ExitStack()
    with es:
        env = Env(nc, es)
        sbt, pst = env.sbt, env.pst

        dtt = sbt("dtt", [H_A, TP], F32)
        lbuf = sbt("lbuf", [H_A, Q * NQ], F32)
        hc = sbt("hc", [N_A, D], F32)
        hcb = sbt("hcb", [N_A, D], BF16)
        stmp = sbt("stmp", [N_A, D], F32)
        idb_s = sbt("idb_s", [128, 128], BF16)
        idf_s = sbt("idf_s", [128, 128], F32)
        utri_s = sbt("utri_s", [128, 128], F32)
        onesr = sbt("onesr", [1, 128], F32)
        onesc = sbt("onesc", [128, 1], BF16)
        sA = sbt("sA", [H_A, 1], F32)
        nwS = sbt("nwS", [128, 16], F32)
        dexS = sbt("dexS", [128, 16], F32)
        pmr = sbt("pmr", [1, NCORES], F32)
        lcol = sbt("lcol", [128, H_A], F32)
        dtcol = sbt("dtcol", [128, H_A], F32)
        ltrow = sbt("ltrow", [1, H_A], F32)
        erow = sbt("erow", [1, H_A], F32)
        lflat = sbt("lflat", [1, H_A * Q], F32)
        elflat = sbt("elflat", [1, H_A * Q], F32)
        env.setup_consts([LN_EPS, RMS_EPS, 1.0, -1.0, -0.5, 0.5, 2.0, 0.0,
                          1.0 / D, -32.0, 32.0])

        B = es.enter_context(nc.Block())
        env.B = B
        sph = env.sph

        psG1 = contextlib.ExitStack()
        pstG = lambda n, s, d=F32: psG1.enter_context(nc.psum_tensor(n, s, d))
        psT = pstG("psT", [128, 2048], BF16)
        psLB = pstG("psLB", [128, 1024])
        psCB = pstG("psCB", [128, 128])
        psY = [pstG(f"psY{i}", [128, 512]) for i in range(2)]
        psF = pstG("psF", [128, 512])

        esP = contextlib.ExitStack()
        sbP = lambda n, s, d: esP.enter_context(nc.sbuf_tensor(n, s, d))
        sco = [sbP(f"sco{j}", [N_A, D], F32) for j in range(NCORES)]
        lt8 = sbP("lt8", [H_A, NQ * NCORES], F32)
        lc8 = sbP("lc8", [H_A, NCORES], F32)
        mlt = sbP("mlt", [H_A, NCORES], F32)
        wdec = sbP("wdec", [H_A, NCORES], F32)
        totc = sbP("totc", [H_A, 1], F32)
        wdT = sbP("wdT", [1, NCORES * H_A], F32)

        env.begin()
        @B.sync
        def _(e):
            e.dma_start(out=idb_s[:], in_=idb[:]).then_inc(sph, 16)
            e.dma_start(out=idf_s[:], in_=idf[:]).then_inc(sph, 16)
            e.dma_start(out=utri_s[:], in_=utri[:]).then_inc(sph, 16)
            e.dma_start(out=sA[:, 0], in_=alog[:]).then_inc(sph, 16)
            e.dma_start(out=dtt[:], in_=dtT[:]).then_inc(sph, 16)
            e.dma_start(out=pmr[:], in_=pmrow[:]).then_inc(sph, 16)
            for k in range(16):
                e.dma_start(out=nwS[:, k:k + 1], in_=nw[128 * k:128 * (k + 1)]).then_inc(sph, 16)
                e.dma_start(out=dexS[:, k:k + 1], in_=dex[128 * k:128 * (k + 1)]).then_inc(sph, 16)
            for j in range(NCORES):
                e.dma_start(out=sco[j][:], in_=score_all[j, :, :]).then_inc(sph, 16)
                e.dma_start(out=lt8[:, NQ * j:NQ * (j + 1)], in_=ltotc_all[j, :, :]).then_inc(sph, 16)
            env.inc(16 * (6 + 32 + 16))
        @B.vector
        def _(e):
            e.memset(onesr[:], 1.0)
            e.memset(onesc[:], 1.0)
            e.memset(hc[:], 0.0)
            e.drain()
            env.emit_const_memsets(e)
            env.fin(e)
        env.end()

        env.begin()
        @B.scalar
        def _(e):
            e.wait_ge(sph, env.w)
            e.activation(out=sA[:], in_=sA[:], func=AF.Exp)
            e.drain()
            e.mul(sA[:], sA[:], -1.0)
            env.fin(e)
        env.end()
        env.begin()
        @B.vector
        def _(e):
            e.wait_ge(sph, env.w)
            e.tensor_scalar(stmp[0:H_A, 0:2048], dtt[:, HALO:HALO + 2048], sA[:, 0:1], None, OP.mult)
            env.fin(e)
        env.end()
        env.begin()
        @B.vector
        def _(e):
            e.wait_ge(sph, env.w)
            for c in range(NQ):
                e.tensor_tensor_scan(lbuf[:, Q * c:Q * (c + 1)],
                                     stmp[0:H_A, Q * c:Q * (c + 1)],
                                     stmp[0:H_A, Q * c:Q * (c + 1)],
                                     0.0, OP.add, OP.bypass)
            env.fin(e)
        env.end()

        env.begin()
        @B.vector
        def _(e):
            e.wait_ge(sph, env.w)
            for j in range(NCORES):
                e.tensor_reduce(lc8[:, j:j + 1], lt8[:, NQ * j:NQ * (j + 1)], AX, OP.add)
            env.fin(e)
        @B.tensor
        def _(e):
            e.wait_ge(sph, env.w)
            e.matmul(psF[:, 0:8], onesr[:, :], pmr[0:1, 0:8], start=True, stop=True)
            env.fin(e)
        env.end()
        env.begin()
        @B.vector
        def _(e):
            e.wait_ge(sph, env.w)
            e.tensor_tensor(mlt[:], lc8[:], psF[0:H_A, 0:8], OP.mult)
            e.drain()
            e.tensor_reduce(totc[:, 0:1], mlt[:], AX, OP.add)
            e.drain()
            e.tensor_tensor_scan(wdec[:], mlt[:], mlt[:], 0.0, OP.add, OP.bypass)
            e.drain()
            e.tensor_scalar(wdec[:], wdec[:], totc[:, 0:1], -1.0, OP.subtract, OP.mult)
            env.fin(e)
        env.end()
        env.begin()
        @B.scalar
        def _(e):
            e.wait_ge(sph, env.w)
            e.activation(out=wdec[:], in_=wdec[:], func=AF.Exp)
            env.fin(e)
        env.end()
        env.begin()
        @B.vector
        def _(e):
            e.wait_ge(sph, env.w)
            e.tensor_tensor(wdec[:], wdec[:], psF[0:H_A, 0:8], OP.mult)
            env.fin(e)
        env.end()
        env.begin()
        @B.sync
        def _(e):
            e.wait_ge(sph, env.w)
            for j in range(NCORES):
                e.dma_start(out=wdT[0:1, H_A * j:H_A * (j + 1)], in_=wdec[:, j:j + 1],
                            allow_slow_non_contiguous=True).then_inc(sph, 16)
            env.inc(16 * NCORES)
        env.end()
        for j in range(NCORES):
            for qq in range(4):
                env.begin()
                @B.tensor
                def _(e, j=j, qq=qq):
                    e.wait_ge(sph, env.w)
                    e.matmul(psF[0:N_A, 0:512], onesr[0:1, 0:N_A],
                             rep_cols(wdT[0:1, H_A * j + 8 * qq:H_A * j + 8 * qq + 8], 8, 64),
                             start=True, stop=True)
                    env.fin(e)
                env.end()
                env.begin()
                @B.vector
                def _(e, j=j, qq=qq):
                    e.wait_ge(sph, env.w)
                    e.scalar_tensor_tensor(stmp[0:N_A, 512 * qq:512 * (qq + 1)],
                                           sco[j][:, 512 * qq:512 * (qq + 1)], 1.0,
                                           psF[0:N_A, 0:512], OP.mult, OP.mult)
                    e.drain()
                    e.tensor_add(hc[:, 512 * qq:512 * (qq + 1)], hc[:, 512 * qq:512 * (qq + 1)],
                                 stmp[0:N_A, 512 * qq:512 * (qq + 1)])
                    env.fin(e)
                env.end()
        esP.close()

        esC = contextlib.ExitStack()
        ygated = esC.enter_context(nc.sbuf_tensor("ygated", [128, 16 * T], BF16))
        esC2 = contextlib.ExitStack()
        sbC = lambda n, s, d: esC2.enter_context(nc.sbuf_tensor(n, s, d))
        hbcC = [sbC(f"hbcC{i}", [128, 16 * Q], BF16) for i in range(2)]
        hbB = [sbC(f"hbB{i}", [32, Q], BF16) for i in range(2)]
        hbC2 = [sbC(f"hbC2{i}", [32, Q], BF16) for i in range(2)]
        gTc = [sbC(f"gTc{i}", [128, 16 * Q], BF16) for i in range(2)]
        xtk = sbC("xtk", [128, D], BF16)
        xw2 = sbC("xw2", [128, D], BF16)
        gbuf = sbC("gbuf", [128, H_A * Q], BF16)
        cbS = sbC("cbS", [128, 128], F32)
        dbuf = sbC("dbuf", [128, 1024], F32)
        cwb = sbC("cwb", [N_A, H_A * Q], BF16)
        ydc = sbC("ydc", [128, 2048], F32)
        sgc = sbC("sgc", [128, 2048], F32)
        ysq = sbC("ysq", [128, 2048], BF16)
        rmsb = sbC("rmsb", [1, Q], F32)

        def conv_cols_ap(c):
            cb0 = HALO + Q * c
            return bass.AP(tensor=convT.tensor, offset=convT.offset + cb0,
                           ap=[[TP, 128], [TP * 128, 16], [1, Q]])

        def bc_ap(c, row0):
            cb0 = HALO + Q * c
            return bass.AP(tensor=convT.tensor, offset=convT.offset + row0 * TP + cb0,
                           ap=[[TP, 32], [1, Q]])

        def gate_cols_ap(c):
            cb0 = HALO + Q * c
            return bass.AP(tensor=gateT.tensor, offset=gateT.offset + cb0,
                           ap=[[TP, 128], [TP * 128, 16], [1, Q]])

        env.begin()
        @B.sync
        def _(e):
            e.wait_ge(sph, env.w)
            e.dma_start(out=hbcC[0][:, 0:16 * Q], in_=conv_cols_ap(0)).then_inc(sph, 16)
            e.dma_start(out=gTc[0][:, 0:16 * Q], in_=gate_cols_ap(0)).then_inc(sph, 16)
            e.dma_start(out=hbB[0][:], in_=bc_ap(0, 2048)).then_inc(sph, 16)
            e.dma_start(out=hbC2[0][:], in_=bc_ap(0, 2080)).then_inc(sph, 16)
            env.inc(64)
        env.end()

        for c in range(NQ):
            cb0 = HALO + Q * c
            hb = hbcC[c % 2]
            gb = gTc[c % 2]
            env.begin()
            if c + 1 < NQ:
                @B.sync
                def _(e, c=c):
                    e.wait_ge(sph, env.w)
                    e.dma_start(out=hbcC[(c + 1) % 2][:, 0:16 * Q], in_=conv_cols_ap(c + 1)).then_inc(sph, 16)
                    e.dma_start(out=gTc[(c + 1) % 2][:, 0:16 * Q], in_=gate_cols_ap(c + 1)).then_inc(sph, 16)
                    e.dma_start(out=hbB[(c + 1) % 2][:], in_=bc_ap(c + 1, 2048)).then_inc(sph, 16)
                    e.dma_start(out=hbC2[(c + 1) % 2][:], in_=bc_ap(c + 1, 2080)).then_inc(sph, 16)
                    env.inc(64)
            @B.scalar
            def _(e, c=c):
                e.wait_ge(sph, env.w)
                e.dma_start(out=stmp[:], in_=schk[c, :, :]).then_inc(sph, 16)
                e.dma_start(out=lflat[0:1, :], in_=lbuf[:, Q * c:Q * (c + 1)]).then_inc(sph, 16)
                e.dma_start(out=ltrow[0:1, 0:H_A], in_=lbuf[:, Q * c + Q - 1:Q * c + Q],
                            allow_slow_non_contiguous=True).then_inc(sph, 16)
                env.inc(48)
            @B.tensor
            def _(e, c=c):
                e.wait_ge(sph, env.w)
                e.transpose(psF[:, 0:32], lbuf[0:32, Q * c:Q * (c + 1)], idf_s[0:32, 0:32])
                e.transpose(psF[:, 32:64], dtt[0:32, cb0:cb0 + Q], idf_s[0:32, 0:32])
                env.fin(e)
            @B.vector
            def _(e, c=c):
                e.wait_ge(sph, env.w)
                e.tensor_copy(hcb[:], hc[:])
                env.fin(e)
            env.end()
            env.begin()
            @B.vector
            def _(e, c=c):
                e.wait_ge(sph, env.w)
                e.tensor_scalar(lcol[:, :], psF[:, 0:32], -1.0, None, OP.mult)
                e.drain()
                e.tensor_copy(dtcol[:, :], psF[:, 32:64])
                env.fin(e)
            @B.scalar
            def _(e, c=c):
                e.wait_ge(sph, env.w)
                e.activation(out=elflat[:], in_=lflat[:], func=AF.Exp)
                e.drain()
                e.activation(out=erow[:], in_=ltrow[:], func=AF.Exp)
                env.fin(e)
            @B.tensor
            def _(e, c=c):
                e.wait_ge(sph, env.w)
                e.matmul(psCB[:, :], hbB[c % 2][:], hbC2[c % 2][:], start=True, stop=True)
                for k in range(16):
                    e.transpose(psT[:, 128 * k:128 * (k + 1)], hb[:, Q * k:Q * (k + 1)], idb_s[:])
                env.fin(e)
            env.end()
            env.begin()
            @B.scalar
            def _(e, c=c):
                e.wait_ge(sph, env.w)
                e.copy(xtk[:], psT[:])
                env.fin(e)
            @B.vector
            def _(e, c=c):
                e.wait_ge(sph, env.w)
                e.tensor_copy(cbS[:], psCB[:, :])
                env.fin(e)
            env.end()
            env.begin()
            @B.vector
            def _(e, c=c):
                e.wait_ge(sph, env.w)
                e.tensor_tensor(xw2[:].rearrange("p (h u) -> p h u", u=P_A),
                                xtk[:].rearrange("p (h u) -> p h u", u=P_A),
                                rep_cols(dtcol[:, 0:H_A], H_A, P_A), OP.mult)
                env.fin(e)
            env.end()
            for hg in range(4):
                env.begin()
                @B.tensor
                def _(e, hg=hg):
                    e.wait_ge(sph, env.w)
                    e.matmul(psLB[:, 0:512], onesr[:, :], lflat[0:1, 1024 * hg:1024 * hg + 512],
                             start=True, stop=True)
                    e.matmul(psLB[:, 512:1024], onesr[:, :], lflat[0:1, 1024 * hg + 512:1024 * (hg + 1)],
                             start=True, stop=True)
                    env.fin(e)
                env.end()
                env.begin()
                @B.vector
                def _(e, hg=hg):
                    e.wait_ge(sph, env.w)
                    e.tensor_tensor(dbuf[:].rearrange("p (h t) -> p h t", t=Q),
                                    psLB[:, 0:1024].rearrange("p (h t) -> p h t", t=Q),
                                    rep_cols(lcol[:, 8 * hg:8 * hg + 8], 8, Q), OP.add)
                    e.drain()
                    e.tensor_scalar_min(dbuf[:], dbuf[:], 0.0)
                    env.fin(e)
                env.end()
                env.begin()
                @B.scalar
                def _(e, hg=hg):
                    e.wait_ge(sph, env.w)
                    e.activation(out=dbuf[:], in_=dbuf[:], func=AF.Exp)
                    env.fin(e)
                env.end()
                env.begin()
                @B.vector
                def _(e, hg=hg):
                    e.wait_ge(sph, env.w)
                    e.tensor_tensor(dbuf[:].rearrange("p (h t) -> p h t", t=Q),
                                    dbuf[:].rearrange("p (h t) -> p h t", t=Q),
                                    rep_tile(cbS[:, 0:Q], 8, Q), OP.mult)
                    env.fin(e)
                env.end()
                env.begin()
                @B.vector
                def _(e, hg=hg):
                    e.wait_ge(sph, env.w)
                    e.tensor_tensor(gbuf[:, 1024 * hg:1024 * (hg + 1)].rearrange("p (h t) -> p h t", t=Q),
                                    dbuf[:].rearrange("p (h t) -> p h t", t=Q),
                                    rep_tile(utri_s[:, 0:Q], 8, Q), OP.mult)
                    env.fin(e)
                env.end()
                env.begin()
                @B.tensor
                def _(e, hg=hg):
                    e.wait_ge(sph, env.w)
                    e.matmul(psLB[0:N_A, 0:512], onesr[0:1, 0:N_A],
                             elflat[0:1, 1024 * hg:1024 * hg + 512], start=True, stop=True)
                    e.matmul(psLB[0:N_A, 512:1024], onesr[0:1, 0:N_A],
                             elflat[0:1, 1024 * hg + 512:1024 * (hg + 1)], start=True, stop=True)
                    env.fin(e)
                env.end()
                env.begin()
                @B.vector
                def _(e, hg=hg):
                    e.wait_ge(sph, env.w)
                    e.tensor_tensor(cwb[:, 1024 * hg:1024 * (hg + 1)].rearrange("n (h t) -> n h t", t=Q),
                                    psLB[0:N_A, 0:1024].rearrange("n (h t) -> n h t", t=Q),
                                    rep_tile(hbC2[c % 2][0:N_A, 0:Q], 8, Q), OP.mult)
                    env.fin(e)
                env.end()
            for grp in range(5):
                env.begin()
                if grp < 4:
                    @B.tensor
                    def _(e, grp=grp, c=c):
                        e.wait_ge(sph, env.w)
                        for u in range(4):
                            hp = grp * 4 + u
                            for hh in range(2):
                                h = 2 * hp + hh
                                out = psY[grp % 2][64 * hh:64 * (hh + 1), 128 * u:128 * (u + 1)]
                                e.matmul(out, xw2[:, P_A * h:P_A * (h + 1)],
                                         gbuf[:, Q * h:Q * (h + 1)], start=True, stop=False)
                                e.matmul(out, hcb[:, P_A * h:P_A * (h + 1)],
                                         cwb[:, Q * h:Q * (h + 1)], start=False, stop=True)
                        env.fin(e)
                if grp > 0:
                    @B.vector
                    def _(e, grp=grp, c=c):
                        e.wait_ge(sph, env.w)
                        for u in range(4):
                            hp = (grp - 1) * 4 + u
                            e.scalar_tensor_tensor(ydc[:, 128 * hp:128 * (hp + 1)],
                                                   hb[:, Q * hp:Q * (hp + 1)],
                                                   dexS[:, hp:hp + 1],
                                                   psY[(grp - 1) % 2][:, 128 * u:128 * (u + 1)],
                                                   OP.mult, OP.add)
                        env.fin(e)
                env.end()
            env.begin()
            @B.scalar
            def _(e, c=c):
                e.wait_ge(sph, env.w)
                e.activation(out=sgc[:], in_=gb[:, 0:16 * Q], func=AF.Silu)
                env.fin(e)
            env.end()
            env.begin()
            @B.vector
            def _(e, c=c):
                e.wait_ge(sph, env.w)
                e.tensor_mul(ydc[:], ydc[:], sgc[:])
                env.fin(e)
            env.end()
            env.begin()
            @B.vector
            def _(e, c=c):
                e.wait_ge(sph, env.w)
                e.tensor_copy(bass.AP(tensor=ygated[:].tensor, offset=ygated[:].offset + Q * c,
                                      ap=[[ygated[:].ap[0][0], 128], [T, 16], [1, Q]]),
                              ydc[:].rearrange("p (k t) -> p k t", t=Q))
                env.fin(e)
            @B.scalar
            def _(e, c=c):
                e.wait_ge(sph, env.w)
                e.activation(out=ysq[:], in_=ydc[:], func=AF.Square)
                env.fin(e)
            env.end()
            env.begin()
            @B.tensor
            def _(e, c=c):
                e.wait_ge(sph, env.w)
                for k in range(16):
                    e.matmul(psF[0:1, 0:Q], onesc[:, 0:1], ysq[:, Q * k:Q * (k + 1)],
                             start=(k == 0), stop=(k == 15))
                env.fin(e)
            env.end()
            env.begin()
            @B.vector
            def _(e, c=c):
                e.wait_ge(sph, env.w)
                e.tensor_scalar_mul(rmsb[:], psF[0:1, 0:Q], 1.0 / D)
                env.fin(e)
            env.end()
            env.begin()
            @B.scalar
            def _(e, c=c):
                e.wait_ge(sph, env.w)
                e.activation(out=rmsb[:], in_=rmsb[:], func=AF.Ln, bias=RMS_EPS)
                e.drain()
                e.activation(out=rmsb[:], in_=rmsb[:], func=AF.Exp, scale=-0.5)
                env.fin(e)
            env.end()
            env.begin()
            @B.tensor
            def _(e, c=c):
                e.wait_ge(sph, env.w)
                e.matmul(psF[:, 0:Q], onesr[:, :], rmsb[0:1, 0:Q], start=True, stop=True)
                env.fin(e)
            env.end()
            env.begin()
            @B.vector
            def _(e, c=c):
                e.wait_ge(sph, env.w)
                for k in range(16):
                    e.scalar_tensor_tensor(ygated[:, T * k + Q * c:T * k + Q * (c + 1)],
                                           ygated[:, T * k + Q * c:T * k + Q * (c + 1)],
                                           nwS[:, k:k + 1], psF[:, 0:Q], OP.mult, OP.mult)
                env.fin(e)
            env.end()
            for qq in range(4):
                env.begin()
                @B.tensor
                def _(e, c=c, qq=qq):
                    e.wait_ge(sph, env.w)
                    e.matmul(psF[0:N_A, 0:512], onesr[0:1, 0:N_A],
                             rep_cols(erow[0:1, 8 * qq:8 * qq + 8], 8, 64),
                             start=True, stop=True)
                    env.fin(e)
                env.end()
                env.begin()
                @B.vector
                def _(e, c=c, qq=qq):
                    e.wait_ge(sph, env.w)
                    e.tensor_tensor(hc[:, 512 * qq:512 * (qq + 1)],
                                    hc[:, 512 * qq:512 * (qq + 1)], psF[0:N_A, 0:512], OP.mult)
                    e.drain()
                    e.tensor_add(hc[:, 512 * qq:512 * (qq + 1)],
                                 hc[:, 512 * qq:512 * (qq + 1)],
                                 stmp[0:N_A, 512 * qq:512 * (qq + 1)])
                    env.fin(e)
                env.end()


        psG1.close()
        esC2.close()

        emit_out_ln_scores(env, nc, Wo, lambda k, c0, cwid: ygated[:, T * k + c0:T * k + c0 + cwid],
                           xT, R1T, xlnT_o, xlntok_o, scores_o, vals_o, hist_o,
                           wg, idb_s, onesr, free_rhs=esC.close)
    return nc



def l2_inputs(inputs, core, l1outs, l1all):
    b = core // 4
    pm = np.zeros((1, NCORES), np.float32)
    for j in range(NCORES):
        if j < core and j // 4 == core // 4:
            pm[0, j] = 1.0
    return {
        "xT": l1outs["xTo"],
        "gateT": l1outs["gateT"],
        "convT": l1outs["convT"],
        "dtT": l1outs["dtT"],
        "schk": l1outs["schk"],
        "score_all": np.stack([l1all[j]["score"] for j in range(NCORES)]),
        "ltotc_all": np.stack([l1all[j]["ltotc"] for j in range(NCORES)]),
        "pmrow": pm,
        "alog": np.asarray(inputs["L0_attn_a_A_log"]),
        "Wo": np.asarray(inputs["L0_attn_a_W_out"]),
        "nw": np.asarray(inputs["L0_attn_a_norm_w"]),
        "dex": np.repeat(np.asarray(inputs["L0_attn_a_D_param"]), P_A),
        "wg": np.asarray(inputs["L0_ffn_w_gate"]),
        "idb": np.eye(128, dtype=bfnp),
        "idf": np.eye(128, dtype=np.float32),
        "utri": np.triu(np.ones((128, 128), np.float32)),
    }


CAPP = 256
NE = 64
BIGIDX = 2.0e6

def build_L3():
    nc = bass.Bass("TRN2", target_bir_lowering=False)
    R1T = nc.dram_tensor("R1T", [D, T], F32, kind="ExternalInput").ap()
    xlnTf = nc.dram_tensor("xlnTf", [D, T], BF16, kind="ExternalInput").ap()
    xlntok = nc.dram_tensor("xlntok", [T, D], BF16, kind="ExternalInput").ap()
    scores = nc.dram_tensor("scores", [T, NE], F32, kind="ExternalInput").ap()
    vals = nc.dram_tensor("vals", [T, 8], F32, kind="ExternalInput").ap()
    hist_all = nc.dram_tensor("hist_all", [NCORES, NE], F32, kind="ExternalInput").ap()
    pmrow = nc.dram_tensor("pmrow", [1, NCORES], F32, kind="ExternalInput").ap()
    wsu = nc.dram_tensor("wsu", [D, 1024], F32, kind="ExternalInput").ap()
    wsd = nc.dram_tensor("wsd", [512, D], F32, kind="ExternalInput").ap()
    wup = nc.dram_tensor("wup", [NE, D, 512], F32, kind="ExternalInput").ap()
    wdn = nc.dram_tensor("wdn", [NE, 256, D], F32, kind="ExternalInput").ap()
    idb = nc.dram_tensor("idb", [128, 128], BF16, kind="ExternalInput").ap()
    idf = nc.dram_tensor("idf", [128, 128], F32, kind="ExternalInput").ap()
    eoff = nc.dram_tensor("eoff", [NE, 1], F32, kind="ExternalInput").ap()

    R2T = nc.dram_tensor("R2T", [D, T], F32, kind="ExternalOutput").ap()

    xsel = nc.dram_tensor("xsel", [NE * CAPP, D], BF16, kind="Internal").ap()
    ysel = nc.dram_tensor("ysel", [NE * CAPP, D], BF16, kind="Internal").ap()

    es = contextlib.ExitStack()
    with es:
        env = Env(nc, es)
        sbt, pst = env.sbt, env.pst

        idb_s = sbt("idb_s", [128, 128], BF16)
        idf_s = sbt("idf_s", [128, 128], F32)
        onesr = sbt("onesr", [1, 128], F32)
        onescF = sbt("onescF", [128, 1], F32)
        basec = sbt("basec", [NE, 1], F32)
        eoffS = sbt("eoffS", [NE, 1], F32)
        idxI = sbt("idxI", [128, 16 * 6], I32)
        wn = sbt("wn", [128, 16 * 6], F32)
        rsum = sbt("rsum", [128, 1], F32)
        env.setup_consts([LN_EPS, RMS_EPS, 1.0, -1.0, -0.5, 0.5, 2.0, 0.0,
                          1.0 / D, 1024.0, float(CAPP), BIGIDX, -BIGIDX])
        esR = contextlib.ExitStack()
        sbR = lambda n, s, d: esR.enter_context(nc.sbuf_tensor(n, s, d))
        scoreS = sbR("scoreS", [128, 16 * NE], F32)
        valsS = sbR("valsS", [128, 16 * 8], F32)
        MT = sbR("MT", [NE, T], F32)
        CT = sbR("CT", [NE, T], F32)
        gposT = sbR("gposT", [NE, T], F32)
        hist8 = sbR("hist8", [NCORES, NE], F32)
        pmc = sbR("pmc", [NCORES, 1], F32)
        idxF = sbR("idxF", [128, 16 * 6], F32)
        mbuf = sbR("mbuf", [128, NE], F32)
        mbuf2 = sbR("mbuf2", [128, NE], F32)

        B = es.enter_context(nc.Block())
        env.B = B
        sph = env.sph
        bndreg = nc.gpsimd.alloc_register("bndreg")

        psR = contextlib.ExitStack()
        psM = psR.enter_context(nc.psum_tensor("psM", [128, 128], F32))

        env.begin()
        @B.sync
        def _(e):
            e.dma_start(out=idb_s[:], in_=idb[:]).then_inc(sph, 16)
            e.dma_start(out=idf_s[:], in_=idf[:]).then_inc(sph, 16)
            e.dma_start(out=eoffS[:], in_=eoff[:]).then_inc(sph, 16)
            e.dma_start(out=hist8[:], in_=hist_all[:]).then_inc(sph, 16)
            e.dma_start(out=pmc[:, 0:1], in_=pmrow[0:1, :], allow_slow_non_contiguous=True).then_inc(sph, 16)
            for tt in range(16):
                e.dma_start(out=scoreS[:, NE * tt:NE * (tt + 1)],
                            in_=scores[128 * tt:128 * (tt + 1), :]).then_inc(sph, 16)
                e.dma_start(out=valsS[:, 8 * tt:8 * (tt + 1)],
                            in_=vals[128 * tt:128 * (tt + 1), :]).then_inc(sph, 16)
            env.inc(16 * 37)
        @B.vector
        def _(e):
            e.memset(onesr[:], 1.0)
            e.memset(onescF[:], 1.0)
            e.drain()
            env.emit_const_memsets(e)
            env.fin(e)
        @B.gpsimd
        def _(e):
            e.reg_mov(bndreg, NE * CAPP - 1)
            env.fin(e)
        env.end()

        env.begin()
        @B.tensor
        def _(e):
            e.wait_ge(sph, env.w)
            e.matmul(psM[0:NE, 0:1], hist8[:], pmc[:], start=True, stop=True)
            env.fin(e)
        env.end()
        env.begin()
        @B.vector
        def _(e):
            e.wait_ge(sph, env.w)
            e.tensor_copy(basec[:], psM[0:NE, 0:1])
            env.fin(e)
        env.end()

        for tt in range(16):
            env.begin()
            @B.vector
            def _(e, tt=tt):
                e.wait_ge(sph, env.w)
                e.tensor_scalar(mbuf[:], scoreS[:, NE * tt:NE * (tt + 1)],
                                valsS[:, 8 * tt + 5:8 * tt + 6], None, OP.is_ge)
                env.fin(e)
            env.end()
            env.begin()
            @B.tensor
            def _(e, tt=tt):
                e.wait_ge(sph, env.w)
                e.transpose(psM[:, 0:128], mbuf[:, 0:64], idf_s[:]) if False else None
                e.transpose(psM[0:64, 0:128], mbuf[:, 0:64], idf_s[:])
                env.fin(e)
            env.end()
            env.begin()
            @B.vector
            def _(e, tt=tt):
                e.wait_ge(sph, env.w)
                e.tensor_copy(MT[:, 128 * tt:128 * (tt + 1)], psM[0:64, 0:128])
                env.fin(e)
            env.end()

        env.begin()
        @B.vector
        def _(e):
            e.wait_ge(sph, env.w)
            e.tensor_tensor_scan(CT[:], MT[:], MT[:], 0.0, OP.add, OP.bypass)
            e.drain()
            e.tensor_sub(CT[:], CT[:], MT[:])
            e.drain()
            e.tensor_scalar(gposT[:], CT[:], basec[:, 0:1], None, OP.add)
            env.fin(e)
        env.end()
        env.begin()
        @B.vector
        def _(e):
            e.wait_ge(sph, env.w)
            e.tensor_scalar(gposT[:], gposT[:], 1024.0, None, OP.is_lt)
            e.drain()
            e.tensor_mul(gposT[:], gposT[:], MT[:])
            e.drain()
            e.tensor_scalar(MT[:], CT[:], float(CAPP), None, OP.is_lt)
            e.drain()
            e.tensor_mul(gposT[:], gposT[:], MT[:])
            env.fin(e)
        env.end()
        env.begin()
        @B.vector
        def _(e):
            e.wait_ge(sph, env.w)
            e.tensor_scalar(CT[:], CT[:], eoffS[:, 0:1], -BIGIDX, OP.add, OP.add) if False else None
            e.tensor_scalar(CT[:], CT[:], eoffS[:, 0:1], None, OP.add)
            e.drain()
            e.tensor_scalar_add(CT[:], CT[:], -BIGIDX)
            e.drain()
            e.tensor_mul(CT[:], CT[:], gposT[:])
            e.drain()
            e.tensor_scalar_add(CT[:], CT[:], BIGIDX)
            env.fin(e)
        env.end()

        for tt in range(16):
            env.begin()
            @B.tensor
            def _(e, tt=tt):
                e.wait_ge(sph, env.w)
                e.transpose(psM[0:128, 0:64], CT[0:64, 128 * tt:128 * (tt + 1)], idf_s[0:64, 0:64])
                env.fin(e)
            env.end()
            env.begin()
            @B.vector
            def _(e, tt=tt):
                e.wait_ge(sph, env.w)
                e.reduce_sum(rsum[:], valsS[:, 8 * tt:8 * tt + 6], AX)
                env.fin(e)
            env.end()
            env.begin()
            @B.scalar
            def _(e, tt=tt):
                e.wait_ge(sph, env.w)
                e.activation(out=rsum[:], in_=rsum[:], func=AF.Ln)
                e.drain()
                e.activation(out=rsum[:], in_=rsum[:], func=AF.Exp, scale=-1.0)
                env.fin(e)
            env.end()
            env.begin()
            @B.vector
            def _(e, tt=tt):
                e.wait_ge(sph, env.w)
                e.tensor_scalar(wn[:, 6 * tt:6 * tt + 6], valsS[:, 8 * tt:8 * tt + 6],
                                rsum[:, 0:1], None, OP.mult)
                env.fin(e)
            env.end()
            for j in range(6):
                env.begin()
                @B.vector
                def _(e, tt=tt, j=j):
                    e.wait_ge(sph, env.w)
                    e.tensor_scalar(mbuf2[:], scoreS[:, NE * tt:NE * (tt + 1)],
                                    valsS[:, 8 * tt + j:8 * tt + j + 1], None, OP.is_equal)
                    e.drain()
                    e.tensor_tensor(mbuf2[:], mbuf2[:], psM[0:128, 0:64], OP.mult)
                    e.drain()
                    e.reduce_sum(idxF[:, 6 * tt + j:6 * tt + j + 1], mbuf2[:], AX)
                    env.fin(e)
                env.end()
        env.begin()
        @B.vector
        def _(e):
            e.wait_ge(sph, env.w)
            e.tensor_copy(idxI[:], idxF[:])
            env.fin(e)
        env.end()

        esR.close()
        esD = contextlib.ExitStack()
        xtokS = [esD.enter_context(nc.sbuf_tensor(f"xtokS{i}", [128, D], BF16)) for i in range(2)]
        env.begin()
        @B.sync
        def _(e):
            e.wait_ge(sph, env.w)
            e.dma_start(out=xtokS[0][:], in_=xlntok[0:128, :]).then_inc(sph, 16)
            env.inc(16)
        env.end()
        for tt in range(16):
            env.begin()
            if tt + 1 < 16:
                @B.sync
                def _(e, tt=tt):
                    e.wait_ge(sph, env.w)
                    e.dma_start(out=xtokS[(tt + 1) % 2][:],
                                in_=xlntok[128 * (tt + 1):128 * (tt + 2), :]).then_inc(sph, 16)
                    env.inc(16)
            @B.gpsimd
            def _(e, tt=tt):
                e.wait_ge(sph, env.w)
                for j in range(6):
                    e.indirect_dma_start(
                        out=xsel[:], out_offset=bass.IndirectOffsetOnAxis(idxI[:, 6 * tt + j:6 * tt + j + 1], 0),
                        in_=xtokS[tt % 2][:], in_offset=None,
                        bounds_check=bndreg, oob_is_err=False,
                    ).then_inc(sph, 16)
                env.inc(16 * 6)
            env.end()
        esD.close()

        esS = contextlib.ExitStack()
        sbS = lambda n, s, d: esS.enter_context(nc.sbuf_tensor(n, s, d))
        psS = contextlib.ExitStack()
        psum4 = [psS.enter_context(nc.psum_tensor(f"pss{i}", [128, 512], F32)) for i in range(4)]
        hshT = [sbS(f"hshT{k}", [128, T], BF16) for k in range(4)]
        esS2 = contextlib.ExitStack()
        sbS2 = lambda n, s, d: esS2.enter_context(nc.sbuf_tensor(n, s, d))
        suT = [sbS2(f"suT{k}", [128, T], F32) for k in range(4)]
        esS3 = contextlib.ExitStack()
        sbS3 = lambda n, s, d: esS3.enter_context(nc.sbuf_tensor(n, s, d))
        xlnF = [sbS3(f"xlnF{k}", [128, T], BF16) for k in range(16)]
        wbufS = [sbS3(f"wbufS{i}", [128, 16 * 256], BF16) for i in range(2)]

        env.begin()
        @B.sync
        def _(e):
            e.wait_ge(sph, env.w)
            for k in range(16):
                e.dma_start(out=xlnF[k][:], in_=xlnTf[128 * k:128 * (k + 1), :]).then_inc(sph, 16)
            env.inc(16 * 16)
        env.end()

        SL4 = [(0, 512), (512, 512), (1024, 512), (1536, 512)]

        def evac_su(e, ot, c0, cwid, ps, it):
            if ot < 4:
                e.tensor_copy(suT[ot][:, c0:c0 + cwid], ps[:, 0:cwid])
            else:
                e.tensor_copy(hshT[ot - 4][:, c0:c0 + cwid], ps[:, 0:cwid])

        def su_rhs(k, c0, cwid):
            return xlnF[k][:, c0:c0 + cwid]

        emit_proj(env, wsu, D, 8, None, SL4, psum4, wbufS, evac_su, rhs_fn=su_rhs)
        esS3.close()

        for k in range(4):
            env.begin()
            @B.scalar
            def _(e, k=k):
                e.wait_ge(sph, env.w)
                env.fin(e)
            env.end()
        for k in range(4):
            for c0, cwid in SL4:
                env.begin()
                @B.scalar
                def _(e, k=k, c0=c0, cwid=cwid):
                    e.wait_ge(sph, env.w)
                    e.activation(out=psum4[0][:, 0:cwid], in_=hshT[k][:, c0:c0 + cwid], func=AF.Silu)
                    env.fin(e)
                env.end()
                env.begin()
                @B.vector
                def _(e, k=k, c0=c0, cwid=cwid):
                    e.wait_ge(sph, env.w)
                    e.tensor_tensor(hshT[k][:, c0:c0 + cwid], psum4[0][:, 0:cwid],
                                    suT[k][:, c0:c0 + cwid], OP.mult)
                    env.fin(e)
                env.end()

        esS2.close()
        sharedT = [sbS(f"sharedT{k}", [128, T], BF16) for k in range(16)]
        wbufS2 = [sbS(f"wbufS2_{i}", [128, 4 * 256], BF16) for i in range(2)]

        def evac_sd(e, ot, c0, cwid, ps, it):
            e.tensor_copy(sharedT[ot][:, c0:c0 + cwid], ps[:, 0:cwid])

        def sd_rhs(k, c0, cwid):
            return hshT[k][:, c0:c0 + cwid]

        emit_proj(env, wsd, 512, 16, None, SL4, psum4, wbufS2, evac_sd, rhs_fn=sd_rhs)
        psS.close()

        esE = contextlib.ExitStack()
        sbE = lambda n, s, d: esE.enter_context(nc.sbuf_tensor(n, s, d))
        psE = contextlib.ExitStack()
        psU = [psE.enter_context(nc.psum_tensor(f"psU{i}", [128, 512], F32)) for i in range(2)]
        psDn = [psE.enter_context(nc.psum_tensor(f"psDn{i}", [128, 512], F32)) for i in range(4)]
        psHh = psE.enter_context(nc.psum_tensor("psHh", [128, 512], BF16))
        xsT = [sbE(f"xsT{i}", [128, 16 * 256], BF16) for i in range(2)]
        wupS = [sbE(f"wupS{i}", [128, 16 * 512], BF16) for i in range(2)]
        wdnS = [sbE(f"wdnS{i}", [128, 2 * 2048], BF16) for i in range(2)]
        sgl = sbE("sgl", [128, 512], F32)
        hbuf = sbE("hbuf", [128, 512], BF16)
        hT = sbE("hT", [128, 512], BF16)
        ydst = [sbE(f"ydst{i}", [128, 512], BF16) for i in range(4)]

        def wup_ap(ei):
            return bass.AP(tensor=wup.tensor, offset=wup.offset + ei * D * 512,
                           ap=[[512, 128], [512 * 128, 16], [1, 512]])

        def wdn_ap(ei):
            return bass.AP(tensor=wdn.tensor, offset=wdn.offset + ei * 256 * D,
                           ap=[[D, 128], [D * 128, 2], [1, 2048]])

        env.begin()
        @B.gpsimd
        def _(e):
            e.wait_ge(sph, env.w)
            e.dma_start(out=wupS[0][:], in_=wup_ap(0)).then_inc(sph, 16)
            e.dma_start(out=wdnS[0][:], in_=wdn_ap(0)).then_inc(sph, 16)
            env.inc(32)
        @B.sync
        def _(e):
            e.wait_ge(sph, env.w)
            for kk in range(16):
                e.dma_start(out=xsT[0][:, 256 * kk:256 * (kk + 1)],
                            in_=xsel[0:CAPP, 128 * kk:128 * (kk + 1)], transpose=True).then_inc(sph, 16)
            env.inc(16 * 16)
        env.end()

        for ei in range(NE):
            env.begin()
            if ei + 1 < NE:
                @B.gpsimd
                def _(e, ei=ei):
                    e.wait_ge(sph, env.w)
                    e.dma_start(out=wupS[(ei + 1) % 2][:], in_=wup_ap(ei + 1)).then_inc(sph, 16)
                    e.dma_start(out=wdnS[(ei + 1) % 2][:], in_=wdn_ap(ei + 1)).then_inc(sph, 16)
                    env.inc(32)
                @B.sync
                def _(e, ei=ei):
                    e.wait_ge(sph, env.w)
                    for kk in range(16):
                        e.dma_start(out=xsT[(ei + 1) % 2][:, 256 * kk:256 * (kk + 1)],
                                    in_=xsel[CAPP * (ei + 1):CAPP * (ei + 2), 128 * kk:128 * (kk + 1)],
                                    transpose=True).then_inc(sph, 16)
                    env.inc(16 * 16)
            @B.tensor
            def _(e, ei=ei):
                e.wait_ge(sph, env.w)
                for sh in range(2):
                    for kk in range(16):
                        e.matmul(psU[sh][:, :],
                                 xsT[ei % 2][:, 256 * kk + 128 * sh:256 * kk + 128 * (sh + 1)],
                                 wupS[ei % 2][:, 512 * kk:512 * (kk + 1)],
                                 start=(kk == 0), stop=(kk == 15))
                env.fin(e)
            env.end()
            env.begin()
            @B.scalar
            def _(e, ei=ei):
                e.wait_ge(sph, env.w)
                e.activation(out=sgl[:, 0:256], in_=psU[0][:, 0:256], func=AF.Silu)
                e.drain()
                e.activation(out=sgl[:, 256:512], in_=psU[1][:, 0:256], func=AF.Silu)
                env.fin(e)
            env.end()
            env.begin()
            @B.vector
            def _(e, ei=ei):
                e.wait_ge(sph, env.w)
                e.tensor_tensor(hbuf[:, 0:256], sgl[:, 0:256], psU[0][:, 256:512], OP.mult)
                e.drain()
                e.tensor_tensor(hbuf[:, 256:512], sgl[:, 256:512], psU[1][:, 256:512], OP.mult)
                env.fin(e)
            env.end()
            env.begin()
            @B.tensor
            def _(e, ei=ei):
                e.wait_ge(sph, env.w)
                for sh in range(2):
                    for kk in range(2):
                        e.transpose(psHh[:, 128 * (2 * sh + kk):128 * (2 * sh + kk + 1)],
                                    hbuf[:, 256 * sh + 128 * kk:256 * sh + 128 * (kk + 1)], idb_s[:])
                env.fin(e)
            env.end()
            env.begin()
            @B.scalar
            def _(e, ei=ei):
                e.wait_ge(sph, env.w)
                for sh in range(2):
                    for kk in range(2):
                        e.copy(hT[:, 256 * kk + 128 * sh:256 * kk + 128 * (sh + 1)],
                               psHh[:, 128 * (2 * sh + kk):128 * (2 * sh + kk + 1)])
                env.fin(e)
            env.end()
            env.begin()
            @B.tensor
            def _(e, ei=ei):
                e.wait_ge(sph, env.w)
                for og in range(2):
                    for sh in range(2):
                        for kk in range(2):
                            e.matmul(psDn[2 * og + sh][:, :],
                                     hT[:, 256 * kk + 128 * sh:256 * kk + 128 * (sh + 1)],
                                     wdnS[ei % 2][:, 2048 * kk + 512 * og:2048 * kk + 512 * (og + 1)],
                                     start=(kk == 0), stop=(kk == 1))
                env.fin(e)
            env.end()
            env.begin()
            @B.vector
            def _(e, ei=ei):
                e.wait_ge(sph, env.w)
                for u in range(4):
                    e.tensor_copy(ydst[u][:], psDn[u][:, :])
                env.fin(e)
            env.end()
            env.begin()
            @B.tensor
            def _(e, ei=ei):
                e.wait_ge(sph, env.w)
                for og in range(2):
                    for sh in range(2):
                        for kk in range(2):
                            e.matmul(psDn[2 * og + sh][:, :],
                                     hT[:, 256 * kk + 128 * sh:256 * kk + 128 * (sh + 1)],
                                     wdnS[ei % 2][:, 2048 * kk + 512 * (og + 2):2048 * kk + 512 * (og + 3)],
                                     start=(kk == 0), stop=(kk == 1))
                env.fin(e)
            @B.sync
            def _(e, ei=ei):
                e.wait_ge(sph, env.w)
                for og in range(2):
                    for sh in range(2):
                        e.dma_start(out=ysel[CAPP * ei + 128 * sh:CAPP * ei + 128 * (sh + 1),
                                             512 * og:512 * (og + 1)],
                                    in_=ydst[2 * og + sh][:]).then_inc(sph, 16)
                env.inc(64)
            env.end()
            env.begin()
            @B.vector
            def _(e, ei=ei):
                e.wait_ge(sph, env.w)
                for u in range(4):
                    e.tensor_copy(ydst[u][:], psDn[u][:, :])
                env.fin(e)
            env.end()
            env.begin()
            @B.sync
            def _(e, ei=ei):
                e.wait_ge(sph, env.w)
                for og in range(2):
                    for sh in range(2):
                        e.dma_start(out=ysel[CAPP * ei + 128 * sh:CAPP * ei + 128 * (sh + 1),
                                             512 * (og + 2):512 * (og + 3)],
                                    in_=ydst[2 * og + sh][:]).then_inc(sph, 16)
                env.inc(64)
            env.end()
        psE.close()
        esE.close()

        esM = contextlib.ExitStack()
        sbM = lambda n, s, d: esM.enter_context(nc.sbuf_tensor(n, s, d))
        psM2 = contextlib.ExitStack()
        psFc = psM2.enter_context(nc.psum_tensor("psFc", [128, 512], F32))
        gb = [sbM(f"gb{i}", [128, D], BF16) for i in range(2)]
        racc = sbM("racc", [128, D], F32)
        r1c = sbM("r1c", [128, 16 * 128], F32)
        r2stg = sbM("r2stg", [128, 512], F32)

        for tt in range(16):
            env.begin()
            @B.vector
            def _(e, tt=tt):
                e.wait_ge(sph, env.w)
                e.memset(gb[0][:], 0.0)
                e.memset(gb[1][:], 0.0)
                e.memset(racc[:], 0.0)
                env.fin(e)
            @B.sync
            def _(e, tt=tt):
                e.wait_ge(sph, env.w)
                src2 = bass.AP(tensor=R1T.tensor, offset=R1T.offset + 128 * tt,
                               ap=[[T, 128], [T * 128, 16], [1, 128]])
                e.dma_start(out=r1c[:], in_=src2).then_inc(sph, 16)
                env.inc(16)
            env.end()
            env.begin()
            @B.gpsimd
            def _(e, tt=tt):
                e.wait_ge(sph, env.w)
                e.indirect_dma_start(
                    out=gb[0][:], out_offset=None,
                    in_=ysel[:], in_offset=bass.IndirectOffsetOnAxis(idxI[:, 6 * tt:6 * tt + 1], 0),
                    bounds_check=bndreg, oob_is_err=False,
                ).then_inc(sph, 16)
                env.inc(16)
            env.end()
            for j in range(6):
                env.begin()
                if j + 1 < 6:
                    @B.gpsimd
                    def _(e, tt=tt, j=j):
                        e.wait_ge(sph, env.w)
                        e.indirect_dma_start(
                            out=gb[(j + 1) % 2][:], out_offset=None,
                            in_=ysel[:], in_offset=bass.IndirectOffsetOnAxis(
                                idxI[:, 6 * tt + j + 1:6 * tt + j + 2], 0),
                            bounds_check=bndreg, oob_is_err=False,
                        ).then_inc(sph, 16)
                        env.inc(16)
                @B.vector
                def _(e, tt=tt, j=j):
                    e.wait_ge(sph, env.w)
                    e.scalar_tensor_tensor(racc[:], gb[j % 2][:], wn[:, 6 * tt + j:6 * tt + j + 1],
                                           racc[:], OP.mult, OP.add)
                    if j + 2 < 6:
                        e.drain()
                        e.memset(gb[j % 2][:], 0.0)
                    env.fin(e)
                env.end()
            for grp in range(4):
                env.begin()
                @B.tensor
                def _(e, tt=tt, grp=grp):
                    e.wait_ge(sph, env.w)
                    for u in range(4):
                        ot = grp * 4 + u
                        e.transpose(psFc[:, 128 * u:128 * (u + 1)],
                                    racc[:, 128 * ot:128 * (ot + 1)], idf_s[:])
                    env.fin(e)
                env.end()
                env.begin()
                @B.vector
                def _(e, tt=tt, grp=grp):
                    e.wait_ge(sph, env.w)
                    for u in range(4):
                        ot = grp * 4 + u
                        e.tensor_tensor(r2stg[:, 128 * u:128 * (u + 1)], psFc[:, 128 * u:128 * (u + 1)],
                                        r1c[:, 128 * ot:128 * (ot + 1)], OP.add)
                        e.drain()
                        e.tensor_tensor(r2stg[:, 128 * u:128 * (u + 1)], r2stg[:, 128 * u:128 * (u + 1)],
                                        sharedT[ot][:, 128 * tt:128 * (tt + 1)], OP.add)
                        if u < 3:
                            e.drain()
                    env.fin(e)
                env.end()
                env.begin()
                @B.sync
                def _(e, tt=tt, grp=grp):
                    e.wait_ge(sph, env.w)
                    for u in range(4):
                        ot = grp * 4 + u
                        e.dma_start(out=R2T[128 * ot:128 * (ot + 1), 128 * tt:128 * (tt + 1)],
                                    in_=r2stg[:, 128 * u:128 * (u + 1)]).then_inc(sph, 16)
                    env.inc(64)
                env.end()

        env.begin()
        @B.sync
        def _(e):
            e.wait_ge(sph, env.w)
        env.end()
        psM2.close()
        esM.close()
        esS.close()
        psR.close()
    return nc


def l3_inputs(inputs, core, l2outs, l2all):
    pm = np.zeros((1, NCORES), np.float32)
    for j in range(NCORES):
        if j < core:
            pm[0, j] = 1.0
    return {
        "R1T": l2outs["R1T"],
        "xlnTf": l2outs["xlnT_o"],
        "xlntok": l2outs["xlntok_o"],
        "scores": l2outs["scores_o"],
        "vals": l2outs["vals_o"],
        "hist_all": np.stack([l2all[j]["hist_o"][:, 0] for j in range(NCORES)]),
        "pmrow": pm,
        "wsu": np.asarray(inputs["L0_ffn_w_shared_up"]),
        "wsd": np.asarray(inputs["L0_ffn_w_shared_down"]),
        "wup": np.asarray(inputs["L0_ffn_w_up"]),
        "wdn": np.asarray(inputs["L0_ffn_w_down"]),
        "idb": np.eye(128, dtype=bfnp),
        "idf": np.eye(128, dtype=np.float32),
        "eoff": (np.arange(NE, dtype=np.float32) * CAPP).reshape(NE, 1),
    }


H_B, DK_B, DV_B, M_B = 16, 64, 128, 64
SCL_B = DK_B ** -0.5

def build_L3b():
    nc = bass.Bass("TRN2", target_bir_lowering=False)
    R2T = nc.dram_tensor("R2T", [D, T], F32, kind="ExternalInput").ap()
    Wqks = nc.dram_tensor("Wqks", [D, 3072], F32, kind="ExternalInput").ap()
    Wkvg = nc.dram_tensor("Wkvg", [D, 5120], F32, kind="ExternalInput").ap()
    idb = nc.dram_tensor("idb", [128, 128], BF16, kind="ExternalInput").ap()
    idf = nc.dram_tensor("idf", [128, 128], F32, kind="ExternalInput").ap()

    qT_o = nc.dram_tensor("qT_o", [1024, T], F32, kind="ExternalOutput").ap()
    kT_o = nc.dram_tensor("kT_o", [1024, T], F32, kind="ExternalOutput").ap()
    sT_o = nc.dram_tensor("sT_o", [1024, T], F32, kind="ExternalOutput").ap()
    ktok_o = nc.dram_tensor("ktok_o", [T, 1024], F32, kind="ExternalOutput").ap()
    vtok_o = nc.dram_tensor("vtok_o", [T, 2048], F32, kind="ExternalOutput").ap()
    gtok_o = nc.dram_tensor("gtok_o", [T, 2048], F32, kind="ExternalOutput").ap()
    hk_o = nc.dram_tensor("hk_o", [128, 8 * DK_B], F32, kind="ExternalOutput").ap()
    hv_o = nc.dram_tensor("hv_o", [128, 8 * DV_B], F32, kind="ExternalOutput").ap()
    zc_o = nc.dram_tensor("zc_o", [128, 8], F32, kind="ExternalOutput").ap()

    es = contextlib.ExitStack()
    with es:
        env = Env(nc, es)
        sbt, pst = env.sbt, env.pst
        idb_s = sbt("idb_s", [128, 128], BF16)
        idf_s = sbt("idf_s", [128, 128], F32)
        onesr = sbt("onesr", [1, 128], F32)
        onescF = sbt("onescF", [128, 1], F32)
        hkS = sbt("hkS", [128, 8 * DK_B], F32)
        hvS = sbt("hvS", [128, 8 * DV_B], F32)
        zacc = sbt("zacc", [128, 8], F32)
        rold = sbt("rold", [128, 8], F32)
        rnew = sbt("rnew", [128, 8], F32)
        resc = sbt("resc", [128, 8], F32)
        nrn = sbt("nrn", [128, 8], F32)
        mrow = sbt("mrow", [1, T], F32)
        srow = sbt("srow", [1, T], F32)
        rrow = sbt("rrow", [1, T], F32)
        env.setup_consts([LN_EPS, 1.0, -1.0, -0.5, 0.5, 0.0, 1.0 / D, -32.0, 32.0])

        B = es.enter_context(nc.Block())
        env.B = B
        sph = env.sph

        env.begin()
        @B.sync
        def _(e):
            e.dma_start(out=idb_s[:], in_=idb[:]).then_inc(sph, 16)
            e.dma_start(out=idf_s[:], in_=idf[:]).then_inc(sph, 16)
            env.inc(32)
        @B.vector
        def _(e):
            e.memset(onesr[:], 1.0)
            e.memset(onescF[:], 1.0)
            e.memset(hkS[:], 0.0)
            e.memset(hvS[:], 0.0)
            e.memset(zacc[:], 0.0)
            e.memset(rold[:], -1.0e30)
            e.drain()
            env.emit_const_memsets(e)
            env.fin(e)
        env.end()

        esL = contextlib.ExitStack()
        sbL = lambda n, s, d: esL.enter_context(nc.sbuf_tensor(n, s, d))
        psL = contextlib.ExitStack()
        psS2a = psL.enter_context(nc.psum_tensor("psS2a", [1, 512], F32))
        psS2b = psL.enter_context(nc.psum_tensor("psS2b", [1, 512], F32))
        r1blk = [sbL(f"r1blk{i}", [128, 512], F32) for i in range(3)]
        sqblk = [sbL(f"sqblk{i}", [128, 512], F32) for i in range(2)]
        blocks = [(s, ot) for s in range(4) for ot in range(16)]
        for i in range(len(blocks) + 2):
            env.begin()
            if i < len(blocks):
                s, ot = blocks[i]
                @B.sync
                def _(e, s=s, ot=ot, i=i):
                    e.wait_ge(sph, env.w)
                    e.dma_start(out=r1blk[i % 3][:], in_=R2T[128 * ot:128 * (ot + 1),
                                                           512 * s:512 * (s + 1)]).then_inc(sph, 16)
                    env.inc(16)
            if 1 <= i <= len(blocks):
                s, ot = blocks[i - 1]
                @B.scalar
                def _(e, s=s, ot=ot, i=i):
                    e.wait_ge(sph, env.w)
                    e.activation(out=sqblk[(i - 1) % 2][:], in_=r1blk[(i - 1) % 3][:], func=AF.Square)
                    env.fin(e)
            if 2 <= i <= len(blocks) + 1:
                s, ot = blocks[i - 2]
                @B.tensor
                def _(e, s=s, ot=ot, i=i):
                    e.wait_ge(sph, env.w)
                    e.matmul(psS2a[0:1, :], onescF[:, 0:1], r1blk[(i - 2) % 3][:],
                             start=(ot == 0), stop=(ot == 15))
                    e.matmul(psS2b[0:1, :], onescF[:, 0:1], sqblk[(i - 2) % 2][:],
                             start=(ot == 0), stop=(ot == 15))
                    env.fin(e)
                if ot == 15:
                    env.end()
                    env.begin()
                    @B.vector
                    def _(e, s=s):
                        e.wait_ge(sph, env.w)
                        e.tensor_scalar_mul(mrow[0:1, 512 * s:512 * (s + 1)], psS2a[0:1, :], 1.0 / D)
                        e.drain()
                        e.tensor_scalar_mul(srow[0:1, 512 * s:512 * (s + 1)], psS2b[0:1, :], 1.0 / D)
                        env.fin(e)
            env.end()
        env.begin()
        @B.vector
        def _(e):
            e.wait_ge(sph, env.w)
            e.tensor_mul(rrow[:], mrow[:], mrow[:])
            e.drain()
            e.tensor_sub(rrow[:], srow[:], rrow[:])
            env.fin(e)
        env.end()
        env.begin()
        @B.scalar
        def _(e):
            e.wait_ge(sph, env.w)
            e.activation(out=rrow[:], in_=rrow[:], func=AF.Ln, bias=LN_EPS)
            e.drain()
            e.activation(out=rrow[:], in_=rrow[:], func=AF.Exp, scale=-0.5)
            env.fin(e)
        env.end()
        psL.close()
        esL.close()

        esX = contextlib.ExitStack()
        sbX = lambda n, s, d: esX.enter_context(nc.sbuf_tensor(n, s, d))
        psX = contextlib.ExitStack()
        psB1 = psX.enter_context(nc.psum_tensor("psB1", [128, 512], F32))
        xln3T = [sbX(f"xln3T{k}", [128, T], BF16) for k in range(16)]
        mbc = sbX("mbc", [128, T], F32)
        rbc = sbX("rbc", [128, T], F32)
        xblk = [sbX(f"xblk{i}", [128, 512], F32) for i in range(2)]
        for s in range(4):
            for which, row, dst in ((0, mrow, mbc), (1, rrow, rbc)):
                env.begin()
                @B.tensor
                def _(e, s=s, row=row):
                    e.wait_ge(sph, env.w)
                    e.matmul(psB1[:, :], onesr[:, :], row[0:1, 512 * s:512 * (s + 1)],
                             start=True, stop=True)
                    env.fin(e)
                env.end()
                env.begin()
                @B.vector
                def _(e, s=s, dst=dst):
                    e.wait_ge(sph, env.w)
                    e.tensor_copy(dst[:, 512 * s:512 * (s + 1)], psB1[:, :])
                    env.fin(e)
                env.end()
        for i in range(len(blocks) + 1):
            env.begin()
            if i < len(blocks):
                s, ot = blocks[i]
                @B.sync
                def _(e, s=s, ot=ot, i=i):
                    e.wait_ge(sph, env.w)
                    e.dma_start(out=xblk[i % 2][:], in_=R2T[128 * ot:128 * (ot + 1),
                                                          512 * s:512 * (s + 1)]).then_inc(sph, 16)
                    env.inc(16)
            if i >= 1:
                s, ot = blocks[i - 1]
                @B.vector
                def _(e, s=s, ot=ot, i=i):
                    e.wait_ge(sph, env.w)
                    e.tensor_sub(xblk[(i - 1) % 2][:], xblk[(i - 1) % 2][:],
                                 mbc[:, 512 * s:512 * (s + 1)])
                    e.drain()
                    e.tensor_mul(xln3T[ot][:, 512 * s:512 * (s + 1)], xblk[(i - 1) % 2][:],
                                 rbc[:, 512 * s:512 * (s + 1)])
                    env.fin(e)
            env.end()

        esP = contextlib.ExitStack()
        sbP = lambda n, s, d: esP.enter_context(nc.sbuf_tensor(n, s, d))
        psP = contextlib.ExitStack()
        psum4 = [psP.enter_context(nc.psum_tensor(f"psp{i}", [128, 512], F32)) for i in range(4)]
        wbufP = [sbP(f"wbufP{i}", [128, 16 * 256], BF16) for i in range(2)]
        pst_ = [[sbP(f"pst{i}_{h}", [128, 512], BF16) for h in range(2)] for i in range(2)]
        pstF = [[sbP(f"pstF{i}_{h}", [128, 512], F32) for h in range(2)] for i in range(2)]
        SL4 = [(0, 512), (512, 512), (1024, 512), (1536, 512)]

        def evac_qks(e, ot, c0, cwid, ps, it):
            if 16 <= ot < 24:
                e.tensor_scalar(pstF[it % 2][ot % 2][:, 0:cwid], ps[:, 0:cwid],
                                32.0, -32.0, OP.min, OP.max)
            else:
                e.tensor_copy(pstF[it % 2][ot % 2][:, 0:cwid], ps[:, 0:cwid])

        def dma_qks(e, ot, c0, cwid, it):
            if ot < 8:
                dst = qT_o[128 * ot:128 * (ot + 1), c0:c0 + cwid]
            elif ot < 16:
                dst = kT_o[128 * (ot - 8):128 * (ot - 7), c0:c0 + cwid]
            else:
                dst = sT_o[128 * (ot - 16):128 * (ot - 15), c0:c0 + cwid]
            e.dma_start(out=dst, in_=pstF[it % 2][ot % 2][:, 0:cwid]).then_inc(sph, 16)
            env.inc(16)

        def qks_rhs(k, c0, cwid):
            return xln3T[k][:, c0:c0 + cwid]

        emit_proj(env, Wqks, D, 24, None, SL4, psum4, wbufP, evac_qks,
                  dma_fn=dma_qks, rhs_fn=qks_rhs)

        wkv = [sbP(f"wkv{i}", [128, 16 * 512], BF16) for i in range(2)]
        ESL = [("k", ktok_o, 0), ("v", vtok_o, 1024), ("g", gtok_o, 3072)]

        def kvg_dst(og):
            base = og * 512
            if base < 1024:
                return ktok_o, base
            if base < 3072:
                return vtok_o, base - 1024
            return gtok_o, base - 3072

        env.begin()
        @B.gpsimd
        def _(e):
            e.wait_ge(sph, env.w)
            src2 = bass.AP(tensor=Wkvg.tensor, offset=Wkvg.offset + 0,
                           ap=[[5120, 128], [5120 * 128, 16], [1, 512]])
            e.dma_start(out=wkv[0][:], in_=src2).then_inc(sph, 16)
            env.inc(16)
        env.end()
        it2 = [(og, tt) for og in range(10) for tt in range(16)]
        for i in range(len(it2) + 2):
            env.begin()
            if i < len(it2):
                og, tt = it2[i]
                if tt == 0 and og + 1 < 10:
                    @B.gpsimd
                    def _(e, og=og):
                        e.wait_ge(sph, env.w)
                        src2 = bass.AP(tensor=Wkvg.tensor, offset=Wkvg.offset + (og + 1) * 512,
                                       ap=[[5120, 128], [5120 * 128, 16], [1, 512]])
                        e.dma_start(out=wkv[(og + 1) % 2][:], in_=src2).then_inc(sph, 16)
                        env.inc(16)
                @B.tensor
                def _(e, og=og, tt=tt, i=i):
                    e.wait_ge(sph, env.w)
                    ps_ = psum4[i % 2]
                    for kc in range(16):
                        e.matmul(ps_[:, :], xln3T[kc][:, 128 * tt:128 * (tt + 1)],
                                 wkv[og % 2][:, 512 * kc:512 * (kc + 1)],
                                 start=(kc == 0), stop=(kc == 15))
                    env.fin(e)
            if 1 <= i <= len(it2):
                og, tt = it2[i - 1]
                @B.vector
                def _(e, og=og, tt=tt, i=i):
                    e.wait_ge(sph, env.w)
                    e.tensor_copy(pstF[(i - 1) % 2][0][:, :], psum4[(i - 1) % 2][:, :])
                    env.fin(e)
            if 2 <= i <= len(it2) + 1:
                og, tt = it2[i - 2]
                @B.sync
                def _(e, og=og, tt=tt, i=i):
                    e.wait_ge(sph, env.w)
                    dstt, cb = kvg_dst(og)
                    e.dma_start(out=dstt[128 * tt:128 * (tt + 1), cb:cb + 512],
                                in_=pstF[(i - 2) % 2][0][:, :]).then_inc(sph, 16)
                    env.inc(16)
            env.end()
        psP.close()
        esP.close()
        esX.close()

        esY = contextlib.ExitStack()
        sbY = lambda n, s, d: esY.enter_context(nc.sbuf_tensor(n, s, d))
        psY = contextlib.ExitStack()
        psTb = psY.enter_context(nc.psum_tensor("psTb", [128, 1024], F32))
        psHK = psY.enter_context(nc.psum_tensor("psHK", [128, 8 * DK_B], F32))
        psHV = psY.enter_context(nc.psum_tensor("psHV", [128, 8 * DV_B], F32))
        sTc = [sbY(f"sTc{i}", [128, 1024], F32) for i in range(2)]
        ktc = [sbY(f"ktc{i}", [128, 1024], F32) for i in range(2)]
        vtc = [sbY(f"vtc{i}", [128, 2048], F32) for i in range(2)]
        ptil = sbY("ptil", [128, 1024], F32)
        ptk = sbY("ptk", [128, 1024], F32)
        mxc = sbY("mxc", [128, 8], F32)
        zsumc = sbY("zsumc", [128, 8], F32)

        def sT_cols(c):
            return bass.AP(tensor=sT_o.tensor, offset=sT_o.offset + Q * c,
                           ap=[[T, 128], [T * 128, 8], [1, Q]])

        env.begin()
        @B.sync
        def _(e):
            e.wait_ge(sph, env.w)
            e.dma_start(out=sTc[0][:], in_=sT_cols(0)).then_inc(sph, 16)
            e.dma_start(out=ktc[0][:], in_=ktok_o[0:128, :]).then_inc(sph, 16)
            e.dma_start(out=vtc[0][:], in_=vtok_o[0:128, :]).then_inc(sph, 16)
            env.inc(48)
        env.end()

        for c in range(NQ):
            env.begin()
            if c + 1 < NQ:
                @B.sync
                def _(e, c=c):
                    e.wait_ge(sph, env.w)
                    e.dma_start(out=sTc[(c + 1) % 2][:], in_=sT_cols(c + 1)).then_inc(sph, 16)
                    e.dma_start(out=ktc[(c + 1) % 2][:], in_=ktok_o[128 * (c + 1):128 * (c + 2), :]).then_inc(sph, 16)
                    e.dma_start(out=vtc[(c + 1) % 2][:], in_=vtok_o[128 * (c + 1):128 * (c + 2), :]).then_inc(sph, 16)
                    env.inc(48)
            @B.vector
            def _(e, c=c):
                e.wait_ge(sph, env.w)
                for it in range(8):
                    e.reduce_max(mxc[:, it:it + 1], sTc[c % 2][:, Q * it:Q * (it + 1)], AX)
                e.drain()
                e.tensor_max(rnew[:], rold[:], mxc[:])
                e.drain()
                e.tensor_sub(resc[:], rold[:], rnew[:])
                e.drain()
                e.tensor_scalar_mul(nrn[:], rnew[:], -1.0)
                env.fin(e)
            env.end()
            env.begin()
            @B.scalar
            def _(e, c=c):
                e.wait_ge(sph, env.w)
                e.activation(out=resc[:], in_=resc[:], func=AF.Exp)
                e.drain()
                for it in range(8):
                    e.activation(out=ptil[:, Q * it:Q * (it + 1)], in_=sTc[c % 2][:, Q * it:Q * (it + 1)],
                                 func=AF.Exp, bias=nrn[:, it:it + 1])
                    e.drain()
                env.fin(e)
            env.end()
            env.begin()
            @B.vector
            def _(e, c=c):
                e.wait_ge(sph, env.w)
                for it in range(8):
                    e.reduce_sum(zsumc[:, it:it + 1], ptil[:, Q * it:Q * (it + 1)], AX)
                e.drain()
                e.tensor_mul(zacc[:], zacc[:], resc[:])
                e.drain()
                e.tensor_add(zacc[:], zacc[:], zsumc[:])
                e.drain()
                e.tensor_copy(rold[:], rnew[:])
                env.fin(e)
            @B.tensor
            def _(e, c=c):
                e.wait_ge(sph, env.w)
                for it in range(8):
                    e.transpose(psTb[:, 128 * it:128 * (it + 1)], ptil[:, Q * it:Q * (it + 1)], idf_s[:])
                env.fin(e)
            env.end()
            env.begin()
            @B.scalar
            def _(e, c=c):
                e.wait_ge(sph, env.w)
                e.copy(ptk[:], psTb[:])
                env.fin(e)
            env.end()
            env.begin()
            @B.tensor
            def _(e, c=c):
                e.wait_ge(sph, env.w)
                for h in range(H_B):
                    bb = (h % 2) * 64
                    lhs = ptk[:, (h // 2) * 128 + bb:(h // 2) * 128 + bb + 64]
                    e.matmul(psHK[bb:bb + 64, (h // 2) * DK_B:(h // 2 + 1) * DK_B],
                             lhs, ktc[c % 2][:, h * DK_B:(h + 1) * DK_B], start=True, stop=True)
                    e.matmul(psHV[bb:bb + 64, (h // 2) * DV_B:(h // 2 + 1) * DV_B],
                             lhs, vtc[c % 2][:, h * DV_B:(h + 1) * DV_B], start=True, stop=True)
                env.fin(e)
            env.end()
            env.begin()
            @B.vector
            def _(e, c=c):
                e.wait_ge(sph, env.w)
                for it in range(8):
                    e.scalar_tensor_tensor(hkS[:, DK_B * it:DK_B * (it + 1)],
                                           hkS[:, DK_B * it:DK_B * (it + 1)],
                                           resc[:, it:it + 1], psHK[:, DK_B * it:DK_B * (it + 1)],
                                           OP.mult, OP.add)
                for it in range(8):
                    e.scalar_tensor_tensor(hvS[:, DV_B * it:DV_B * (it + 1)],
                                           hvS[:, DV_B * it:DV_B * (it + 1)],
                                           resc[:, it:it + 1], psHV[:, DV_B * it:DV_B * (it + 1)],
                                           OP.mult, OP.add)
                env.fin(e)
            env.end()

        env.begin()
        @B.scalar
        def _(e):
            e.wait_ge(sph, env.w)
            e.activation(out=resc[:], in_=zacc[:], func=AF.Ln)
            e.drain()
            e.activation(out=resc[:], in_=resc[:], func=AF.Exp, scale=-1.0)
            env.fin(e)
        env.end()
        env.begin()
        @B.vector
        def _(e):
            e.wait_ge(sph, env.w)
            for it in range(8):
                e.tensor_scalar(hkS[:, DK_B * it:DK_B * (it + 1)],
                                hkS[:, DK_B * it:DK_B * (it + 1)], resc[:, it:it + 1], None, OP.mult)
                e.tensor_scalar(hvS[:, DV_B * it:DV_B * (it + 1)],
                                hvS[:, DV_B * it:DV_B * (it + 1)], resc[:, it:it + 1], None, OP.mult)
            env.fin(e)
        env.end()
        env.begin()
        @B.scalar
        def _(e):
            e.wait_ge(sph, env.w)
            e.activation(out=zacc[:], in_=zacc[:], func=AF.Ln)
            env.fin(e)
        env.end()
        env.begin()
        @B.vector
        def _(e):
            e.wait_ge(sph, env.w)
            e.tensor_add(zacc[:], zacc[:], rold[:])
            env.fin(e)
        env.end()
        env.begin()
        @B.sync
        def _(e):
            e.wait_ge(sph, env.w)
            e.dma_start(out=hk_o[:], in_=hkS[:]).then_inc(sph, 16)
            e.dma_start(out=hv_o[:], in_=hvS[:]).then_inc(sph, 16)
            e.dma_start(out=zc_o[:], in_=zacc[:]).then_inc(sph, 16)
            env.inc(48)
        env.end()
        env.begin()
        @B.sync
        def _(e):
            e.wait_ge(sph, env.w)
        env.end()
        psY.close()
        esY.close()
    return nc


def l3b_inputs(inputs, core, l3outs):
    return {
        "R2T": l3outs["R2T"],
        "Wqks": np.concatenate([np.asarray(inputs["L0_attn_b_Wq"]),
                                np.asarray(inputs["L0_attn_b_Wk"]),
                                np.asarray(inputs["L0_attn_b_Ws"])], axis=1),
        "Wkvg": np.concatenate([np.asarray(inputs["L0_attn_b_Wk"]),
                                np.asarray(inputs["L0_attn_b_Wv"]),
                                np.asarray(inputs["L0_attn_b_Wg"])], axis=1),
        "idb": np.eye(128, dtype=bfnp),
        "idf": np.eye(128, dtype=np.float32),
    }



DEBUG_L4 = False


def build_L4():
    nc = bass.Bass("TRN2", target_bir_lowering=False)
    R2T = nc.dram_tensor("R2T", [D, T], F32, kind="ExternalInput").ap()
    qT = nc.dram_tensor("qT", [1024, T], F32, kind="ExternalInput").ap()
    kT = nc.dram_tensor("kT", [1024, T], F32, kind="ExternalInput").ap()
    sT = nc.dram_tensor("sT", [1024, T], F32, kind="ExternalInput").ap()
    ktok = nc.dram_tensor("ktok", [T, 1024], F32, kind="ExternalInput").ap()
    vtok = nc.dram_tensor("vtok", [T, 2048], F32, kind="ExternalInput").ap()
    gtok = nc.dram_tensor("gtok", [T, 2048], F32, kind="ExternalInput").ap()
    hk_all = nc.dram_tensor("hk_all", [NCORES, 128, 8 * DK_B], F32, kind="ExternalInput").ap()
    hv_all = nc.dram_tensor("hv_all", [NCORES, 128, 8 * DV_B], F32, kind="ExternalInput").ap()
    zc_all = nc.dram_tensor("zc_all", [NCORES, 128, 8], F32, kind="ExternalInput").ap()
    zmask = nc.dram_tensor("zmask", [128, 8 * NCORES], F32, kind="ExternalInput").ap()
    pm01 = nc.dram_tensor("pm01", [128, NCORES], F32, kind="ExternalInput").ap()
    Wo2 = nc.dram_tensor("Wo2", [D, D], F32, kind="ExternalInput").ap()
    gnwt = nc.dram_tensor("gnwt", [D], F32, kind="ExternalInput").ap()
    wg = nc.dram_tensor("wg", [D, 64], F32, kind="ExternalInput").ap()
    idb = nc.dram_tensor("idb", [128, 128], BF16, kind="ExternalInput").ap()
    idf = nc.dram_tensor("idf", [128, 128], F32, kind="ExternalInput").ap()
    utri = nc.dram_tensor("utri", [128, 128], F32, kind="ExternalInput").ap()

    R3T = nc.dram_tensor("R3T", [D, T], F32, kind="ExternalOutput").ap()
    xlnT_o = nc.dram_tensor("xlnT_o", [D, T], BF16, kind="ExternalOutput").ap()
    xlntok_o = nc.dram_tensor("xlntok_o", [T, D], BF16, kind="ExternalOutput").ap()
    scores_o = nc.dram_tensor("scores_o", [T, 64], F32, kind="ExternalOutput").ap()
    vals_o = nc.dram_tensor("vals_o", [T, 8], F32, kind="ExternalOutput").ap()
    hist_o = nc.dram_tensor("hist_o", [64, 1], F32, kind="ExternalOutput").ap()
    dbg_ov = nc.dram_tensor("dbg_ov", [D, T], BF16, kind="ExternalOutput").ap() if DEBUG_L4 else None
    dbg_z = nc.dram_tensor("dbg_z", [NQ, 128, 8], F32, kind="ExternalOutput").ap() if DEBUG_L4 else None
    dbg_hk = nc.dram_tensor("dbg_hk", [NQ, 128, 512], F32, kind="ExternalOutput").ap() if DEBUG_L4 else None
    dbg_hv = nc.dram_tensor("dbg_hv", [NQ, 128, 1024], F32, kind="ExternalOutput").ap() if DEBUG_L4 else None

    es = contextlib.ExitStack()
    with es:
        env = Env(nc, es)
        sbt, pst = env.sbt, env.pst
        idb_s = sbt("idb_s", [128, 128], BF16)
        idf_s = sbt("idf_s", [128, 128], F32)
        utri_s = sbt("utri_s", [128, 128], F32)
        onesr = sbt("onesr", [1, 128], F32)
        hkR = sbt("hkR", [128, 8 * DK_B], F32)
        hvR = sbt("hvR", [128, 8 * DV_B], F32)
        zprev = sbt("zprev", [128, 8], F32)
        gnwbc = sbt("gnwbc", [128, D], F32)
        gnr = sbt("gnr", [1, D], F32)
        env.setup_consts([LN_EPS, 1e-5, 1.0, -1.0, -0.5, 0.5, 0.0, 1.0 / D,
                          1.0 / DV_B, SCL_B])

        B = es.enter_context(nc.Block())
        env.B = B
        sph = env.sph

        env.begin()
        @B.sync
        def _(e):
            e.dma_start(out=idb_s[:], in_=idb[:]).then_inc(sph, 16)
            e.dma_start(out=idf_s[:], in_=idf[:]).then_inc(sph, 16)
            e.dma_start(out=utri_s[:], in_=utri[:]).then_inc(sph, 16)
            e.dma_start(out=gnr[0:1, :], in_=gnwt[:]).then_inc(sph, 16)
            env.inc(64)
        @B.vector
        def _(e):
            e.memset(onesr[:], 1.0)
            e.drain()
            env.emit_const_memsets(e)
            env.fin(e)
        env.end()

        for s4 in range(4):
            env.begin()
            @B.tensor
            def _(e, s4=s4):
                e.wait_ge(sph, env.w)
                e.matmul(psC1[:, :], onesr[:, :], gnr[0:1, 512 * s4:512 * (s4 + 1)],
                         start=True, stop=True) if False else None
                env.fin(e)
            env.end()

        psG = contextlib.ExitStack()
        psA1 = psG.enter_context(nc.psum_tensor("l4psA1", [128, 1024], F32))
        psA2 = psG.enter_context(nc.psum_tensor("l4psA2", [128, 1024], F32))
        psB1 = psG.enter_context(nc.psum_tensor("l4psB1", [128, 1024], F32))
        psC1 = psG.enter_context(nc.psum_tensor("l4psC1", [128, 512], F32))
        psC2 = psG.enter_context(nc.psum_tensor("l4psC2", [128, 512], F32))

        for s4 in range(4):
            env.begin()
            @B.tensor
            def _(e, s4=s4):
                e.wait_ge(sph, env.w)
                e.matmul(psC1[:, :], onesr[:, :], gnr[0:1, 512 * s4:512 * (s4 + 1)],
                         start=True, stop=True)
                env.fin(e)
            env.end()
            env.begin()
            @B.vector
            def _(e, s4=s4):
                e.wait_ge(sph, env.w)
                e.tensor_copy(gnwbc[:, 512 * s4:512 * (s4 + 1)], psC1[:, :])
                env.fin(e)
            env.end()

        esP = contextlib.ExitStack()
        sbP = lambda n, s, d: esP.enter_context(nc.sbuf_tensor(n, s, d))
        hkj = [sbP(f"hkj{j}", [128, 8 * DK_B], F32) for j in range(NCORES)]
        hvj = [sbP(f"hvj{j}", [128, 8 * DV_B], F32) for j in range(NCORES)]
        zxj = sbP("zxj", [128, 8 * NCORES], F32)
        zmx = sbP("zmx", [128, 8], F32)
        zsm = sbP("zsm", [128, 8], F32)
        wj = sbP("wj", [128, 8], F32)

        env.begin()
        @B.sync
        def _(e):
            e.wait_ge(sph, env.w)
            for j in range(NCORES):
                e.dma_start(out=hkj[j][:], in_=hk_all[j, :, :]).then_inc(sph, 16)
                e.dma_start(out=hvj[j][:], in_=hv_all[j, :, :]).then_inc(sph, 16)
                e.dma_start(out=zxj[:, 8 * j:8 * (j + 1)], in_=zc_all[j, :, :]).then_inc(sph, 16)
            env.inc(16 * 24)
        env.end()
        zmk = sbP("zmk", [128, 8 * NCORES], F32)
        pmS = sbP("pmS", [128, NCORES], F32)
        env.begin()
        @B.sync
        def _(e):
            e.wait_ge(sph, env.w)
            e.dma_start(out=zmk[:], in_=zmask[:]).then_inc(sph, 16)
            e.dma_start(out=pmS[:], in_=pm01[:]).then_inc(sph, 16)
            env.inc(32)
        env.end()
        env.begin()
        @B.vector
        def _(e):
            e.wait_ge(sph, env.w)
            e.tensor_add(zxj[:], zxj[:], zmk[:])
            e.drain()
            e.tensor_copy(zmx[:], zxj[:, 0:8])
            e.drain()
            for j in range(1, NCORES):
                e.tensor_max(zmx[:], zmx[:], zxj[:, 8 * j:8 * (j + 1)])
                e.drain()
            e.tensor_tensor(zxj[:].rearrange("p (j c) -> p j c", c=8),
                            zxj[:].rearrange("p (j c) -> p j c", c=8),
                            rep_tile(zmx[:, 0:8], NCORES, 8), OP.subtract)
            env.fin(e)
        env.end()
        env.begin()
        @B.scalar
        def _(e):
            e.wait_ge(sph, env.w)
            e.activation(out=zxj[:], in_=zxj[:], func=AF.Exp)
            env.fin(e)
        env.end()
        env.begin()
        @B.vector
        def _(e):
            e.wait_ge(sph, env.w)
            e.tensor_reduce(zsm[:].rearrange("p (a c) -> p a c", a=1),
                            zxj[:].rearrange("p (j c) -> p c j", c=8), AX, OP.add) if False else None
            ap3 = bass.AP(tensor=zxj[:].tensor, offset=zxj[:].offset,
                          ap=[[zxj[:].ap[0][0], 128], [1, 8], [8, NCORES]])
            e.tensor_reduce(zsm[:], ap3, AX, OP.add)
            env.fin(e)
        env.end()
        env.begin()
        @B.scalar
        def _(e):
            e.wait_ge(sph, env.w)
            e.activation(out=wj[:], in_=zsm[:], func=AF.Ln)
            env.fin(e)
        env.end()
        env.begin()
        @B.vector
        def _(e):
            e.wait_ge(sph, env.w)
            e.tensor_add(zprev[:], zmx[:], wj[:])
            e.drain()
            e.memset(hkR[:], 0.0)
            e.memset(hvR[:], 0.0)
            env.fin(e)
        env.end()
        env.begin()
        @B.scalar
        def _(e):
            e.wait_ge(sph, env.w)
            e.activation(out=zsm[:], in_=zsm[:], func=AF.Ln)
            e.drain()
            e.activation(out=zsm[:], in_=zsm[:], func=AF.Exp, scale=-1.0)
            env.fin(e)
        env.end()
        for j in range(NCORES):
            env.begin()
            @B.vector
            def _(e, j=j):
                e.wait_ge(sph, env.w)
                e.tensor_mul(wj[:], zxj[:, 8 * j:8 * (j + 1)], zsm[:])
                e.drain()
                e.tensor_scalar(wj[:], wj[:], pmS[:, j:j + 1], None, OP.mult)
                e.drain()
                for it in range(8):
                    e.scalar_tensor_tensor(hkR[:, DK_B * it:DK_B * (it + 1)],
                                           hkj[j][:, DK_B * it:DK_B * (it + 1)],
                                           wj[:, it:it + 1], hkR[:, DK_B * it:DK_B * (it + 1)],
                                           OP.mult, OP.add)
                for it in range(8):
                    e.scalar_tensor_tensor(hvR[:, DV_B * it:DV_B * (it + 1)],
                                           hvj[j][:, DV_B * it:DV_B * (it + 1)],
                                           wj[:, it:it + 1], hvR[:, DV_B * it:DV_B * (it + 1)],
                                           OP.mult, OP.add)
                env.fin(e)
            env.end()
        esP.close()

        esC = contextlib.ExitStack()
        ovT = [esC.enter_context(nc.sbuf_tensor(f"ovT{k}", [128, T], BF16)) for k in range(16)]
        esC2 = contextlib.ExitStack()
        sbC = lambda n, s, d: esC2.enter_context(nc.sbuf_tensor(n, s, d))
        sTc = [sbC(f"sTc{i}", [128, 1024], F32) for i in range(2)]
        qTc = [sbC(f"qTc{i}", [128, 1024], F32) for i in range(2)]
        kTc = [sbC(f"kTc{i}", [128, 1024], F32) for i in range(2)]
        ktc = [sbC(f"ktc{i}", [128, 1024], F32) for i in range(2)]
        vtc = [sbC(f"vtc{i}", [128, 2048], F32) for i in range(2)]
        gtc = [sbC(f"gtc{i}", [128, 2048], F32) for i in range(2)]
        ptil = sbC("ptil", [128, 1024], F32)
        ptk = sbC("ptk", [128, 1024], F32)
        ctf = sbC("ctf", [128, 1024], F32)
        cttok = sbC("cttok", [128, 1024], F32)
        rc = sbC("rc", [128, 1024], F32)
        okb = sbC("okb", [128, 1024], F32)
        pvb = sbC("pvb", [128, 1024], F32)
        pvnT = sbC("pvnT", [128, 1024], F32)
        pvnAT = sbC("pvnAT", [128, 1024], F32)
        hkA = sbC("hkA", [128, 8 * DK_B], F32)
        hkD = sbC("hkD", [128, 8 * DK_B], F32)
        hkOst = sbC("hkOst", [64, 512], F32)
        qkb = [sbC(f"qkb{i}", [128, 128], F32) for i in range(2)]
        wmk = qkb
        mxc = sbC("mxc", [128, 8], F32)
        rref = sbC("rref", [128, 8], F32)
        nrf = sbC("nrf", [128, 8], F32)
        acol = sbC("acol", [128, 8], F32)
        cend = sbC("cend", [128, 8], F32)
        rcend = sbC("rcend", [128, 8], F32)
        arc = sbC("arc", [128, 8], F32)
        segm = sbC("segm", [128, 16], F32)
        segr = sbC("segr", [128, 16], F32)
        ovb = sbC("ovb", [128, 2048], F32)
        sqb = sbC("sqb", [128, 1024], F32)

        def colsl(dr, c, nch):
            return bass.AP(tensor=dr.tensor, offset=dr.offset + Q * c,
                           ap=[[T, 128], [T * 128, nch], [1, Q]])

        env.begin()
        @B.sync
        def _(e):
            e.wait_ge(sph, env.w)
            e.dma_start(out=sTc[0][:], in_=colsl(sT, 0, 8)).then_inc(sph, 16)
            e.dma_start(out=qTc[0][:], in_=colsl(qT, 0, 8)).then_inc(sph, 16)
            e.dma_start(out=kTc[0][:], in_=colsl(kT, 0, 8)).then_inc(sph, 16)
            e.dma_start(out=ktc[0][:], in_=ktok[0:128, :]).then_inc(sph, 16)
            e.dma_start(out=vtc[0][:], in_=vtok[0:128, :]).then_inc(sph, 16)
            e.dma_start(out=gtc[0][:], in_=gtok[0:128, :]).then_inc(sph, 16)
            env.inc(96)
        env.end()

        for c in range(NQ):
            par = c % 2
            env.begin()
            if c + 1 < NQ:
                @B.sync
                def _(e, c=c):
                    e.wait_ge(sph, env.w)
                    nx = (c + 1) % 2
                    e.dma_start(out=sTc[nx][:], in_=colsl(sT, c + 1, 8)).then_inc(sph, 16)
                    e.dma_start(out=qTc[nx][:], in_=colsl(qT, c + 1, 8)).then_inc(sph, 16)
                    e.dma_start(out=kTc[nx][:], in_=colsl(kT, c + 1, 8)).then_inc(sph, 16)
                    e.dma_start(out=ktc[nx][:], in_=ktok[128 * (c + 1):128 * (c + 2), :]).then_inc(sph, 16)
                    e.dma_start(out=vtc[nx][:], in_=vtok[128 * (c + 1):128 * (c + 2), :]).then_inc(sph, 16)
                    e.dma_start(out=gtc[nx][:], in_=gtok[128 * (c + 1):128 * (c + 2), :]).then_inc(sph, 16)
                    env.inc(96)
            @B.vector
            def _(e, c=c, par=par):
                e.wait_ge(sph, env.w)
                for it in range(8):
                    e.reduce_max(mxc[:, it:it + 1], sTc[par][:, Q * it:Q * (it + 1)], AX)
                e.drain()
                e.tensor_max(rref[:], zprev[:], mxc[:])
                e.drain()
                e.tensor_scalar_mul(nrf[:], rref[:], -1.0)
                e.drain()
                e.tensor_sub(acol[:], zprev[:], rref[:])
                env.fin(e)
            env.end()
            env.begin()
            @B.scalar
            def _(e, c=c, par=par):
                e.wait_ge(sph, env.w)
                e.activation(out=acol[:], in_=acol[:], func=AF.Exp)
                e.drain()
                for it in range(8):
                    e.activation(out=ptil[:, Q * it:Q * (it + 1)], in_=sTc[par][:, Q * it:Q * (it + 1)],
                                 func=AF.Exp, bias=nrf[:, it:it + 1])
                    e.drain()
                env.fin(e)
            env.end()
            env.begin()
            @B.vector
            def _(e, c=c):
                e.wait_ge(sph, env.w)
                for it in range(8):
                    e.tensor_scalar(hkA[:, DK_B * it:DK_B * (it + 1)],
                                    hkR[:, DK_B * it:DK_B * (it + 1)], acol[:, it:it + 1], None, OP.mult)
                e.drain()
                for it in range(8):
                    e.tensor_tensor_scan(ctf[:, Q * it:Q * (it + 1)], ptil[:, Q * it:Q * (it + 1)],
                                         ptil[:, Q * it:Q * (it + 1)], acol[:, it:it + 1],
                                         OP.add, OP.bypass)
                env.fin(e)
            env.end()
            env.begin()
            @B.tensor
            def _(e, c=c):
                e.wait_ge(sph, env.w)
                for it in range(8):
                    e.transpose(psB1[:, 128 * it:128 * (it + 1)], ptil[:, Q * it:Q * (it + 1)], idf_s[:])
                env.fin(e)
            @B.vector
            def _(e, c=c):
                e.wait_ge(sph, env.w)
                for it in range(8):
                    e.tensor_copy(cend[:, it:it + 1], ctf[:, Q * it + Q - 1:Q * it + Q])
                env.fin(e)
            env.end()
            env.begin()
            @B.scalar
            def _(e, c=c):
                e.wait_ge(sph, env.w)
                e.copy(ptk[:], psB1[:])
                env.fin(e)
            @B.tensor
            def _(e, c=c):
                e.wait_ge(sph, env.w)
                for it in range(4):
                    e.transpose(psA1[:, 256 * it:256 * it + 128], ctf[:, 256 * it:256 * it + 128], idf_s[:])
                    e.transpose(psA1[:, 256 * it + 128:256 * it + 256], ctf[:, 256 * it + 128:256 * it + 256], idf_s[:])
                env.fin(e)
            env.end()
            env.begin()
            @B.vector
            def _(e, c=c):
                e.wait_ge(sph, env.w)
                e.tensor_copy(cttok[:], psA1[:])
                env.fin(e)
            @B.tensor
            def _(e, c=c):
                e.wait_ge(sph, env.w)
                for h in range(H_B):
                    bb = (h % 2) * 64
                    ps_ = psC1 if h % 2 == 0 else psC2
                    cc0 = (h // 2) * 64
                    e.transpose(ps_[0:64, cc0:cc0 + 64],
                                hkA[bb:bb + 64, (h // 2) * DK_B:(h // 2) * DK_B + 64],
                                idf_s[bb:bb + 64, 0:64])
                env.fin(e)
            env.end()
            env.begin()
            @B.vector
            def _(e, c=c):
                e.wait_ge(sph, env.w)
                for h in range(0, H_B, 2):
                    e.tensor_copy(hkD[0:64, (h // 2) * DK_B:(h // 2) * DK_B + 64],
                                  psC1[0:64, (h // 2) * 64:(h // 2) * 64 + 64])
                e.drain()
                e.tensor_copy(hkOst[0:64, :], psC2[0:64, 0:512])
                env.fin(e)
            env.end()
            env.begin()
            @B.sync
            def _(e, c=c):
                e.wait_ge(sph, env.w)
                e.dma_start(out=hkD[64:128, 0:512], in_=hkOst[0:64, :]).then_inc(sph, 16)
                env.inc(16)
            env.end()
            env.begin()
            @B.scalar
            def _(e, c=c):
                e.wait_ge(sph, env.w)
                e.activation(out=rc[:], in_=cttok[:], func=AF.Ln)
                e.drain()
                e.activation(out=rc[:], in_=rc[:], func=AF.Exp, scale=-1.0)
                env.fin(e)
            env.end()
            for h in range(H_B):
                env.begin()
                @B.tensor
                def _(e, c=c, h=h, par=par):
                    e.wait_ge(sph, env.w)
                    bb = (h % 2) * 64
                    ps_ = psC1 if h % 2 == 0 else psC2
                    e.matmul(ps_[:, 0:128], kTc[par][bb:bb + 64, (h // 2) * 128:(h // 2) * 128 + 128],
                             qTc[par][bb:bb + 64, (h // 2) * 128:(h // 2) * 128 + 128],
                             start=True, stop=True)
                    env.fin(e)
                env.end()
                env.begin()
                @B.vector
                def _(e, c=c, h=h):
                    e.wait_ge(sph, env.w)
                    ps_ = psC1 if h % 2 == 0 else psC2
                    e.tensor_tensor(qkb[h % 2][:], ps_[:, 0:128], utri_s[:], OP.mult)
                    env.fin(e)
                env.end()
                env.begin()
                @B.tensor
                def _(e, c=c, h=h, par=par):
                    e.wait_ge(sph, env.w)
                    bb = (h % 2) * 64
                    e.matmul(psA1[:, 64 * h:64 * (h + 1)], qkb[h % 2][:],
                             ptk[:, (h // 2) * 128 + bb:(h // 2) * 128 + bb + 64],
                             start=True, stop=True)
                    e.matmul(psA2[:, 64 * h:64 * (h + 1)],
                             qTc[par][bb:bb + 64, (h // 2) * 128:(h // 2) * 128 + 128],
                             hkD[bb:bb + 64, (h // 2) * DK_B:(h // 2) * DK_B + 64],
                             start=True, stop=True)
                    env.fin(e)
                env.end()
            env.begin()
            @B.vector
            def _(e, c=c):
                e.wait_ge(sph, env.w)
                e.tensor_copy(cttok[:], psA2[:])
                env.fin(e)
            env.end()
            env.begin()
            @B.vector
            def _(e, c=c):
                e.wait_ge(sph, env.w)
                e.tensor_tensor(okb[:], psA1[:], cttok[:], OP.add)
                e.drain()
                e.tensor_mul(okb[:], okb[:], rc[:])
                e.drain()
                e.tensor_scalar_mul(okb[:], okb[:], SCL_B)
                e.drain()
                e.tensor_reduce(segm[:], okb[:].rearrange("p (h m) -> p h m", m=M_B), AX, OP.max)
                e.drain()
                e.tensor_tensor(okb[:].rearrange("p (h m) -> p h m", m=M_B),
                                okb[:].rearrange("p (h m) -> p h m", m=M_B),
                                rep_cols(segm[:, 0:16], 16, M_B), OP.subtract)
                env.fin(e)
            env.end()
            env.begin()
            @B.scalar
            def _(e, c=c):
                e.wait_ge(sph, env.w)
                e.activation(out=pvb[:], in_=okb[:], func=AF.Exp)
                env.fin(e)
            env.end()
            env.begin()
            @B.vector
            def _(e, c=c):
                e.wait_ge(sph, env.w)
                e.tensor_reduce(segr[:], pvb[:].rearrange("p (h m) -> p h m", m=M_B), AX, OP.add)
                env.fin(e)
            env.end()
            env.begin()
            @B.scalar
            def _(e, c=c):
                e.wait_ge(sph, env.w)
                e.activation(out=segr[:], in_=segr[:], func=AF.Ln)
                e.drain()
                e.activation(out=segr[:], in_=segr[:], func=AF.Exp, scale=-1.0)
                env.fin(e)
            env.end()
            env.begin()
            @B.vector
            def _(e, c=c):
                e.wait_ge(sph, env.w)
                e.tensor_tensor(pvb[:].rearrange("p (h m) -> p h m", m=M_B),
                                pvb[:].rearrange("p (h m) -> p h m", m=M_B),
                                rep_cols(segr[:, 0:16], 16, M_B), OP.mult)
                e.drain()
                e.tensor_mul(pvb[:], pvb[:], rc[:])
                env.fin(e)
            env.end()
            env.begin()
            @B.tensor
            def _(e, c=c):
                e.wait_ge(sph, env.w)
                for it in range(8):
                    e.transpose(psB1[:, 128 * it:128 * (it + 1)],
                                pvb[:, 128 * it:128 * (it + 1)], idf_s[:])
                env.fin(e)
            env.end()
            env.begin()
            @B.scalar
            def _(e, c=c):
                e.wait_ge(sph, env.w)
                e.copy(pvnT[:], psB1[:])
                env.fin(e)
            env.end()
            env.begin()
            @B.vector
            def _(e, c=c):
                e.wait_ge(sph, env.w)
                e.tensor_tensor(pvnAT[:].rearrange("p (i t) -> p i t", t=Q),
                                pvnT[:].rearrange("p (i t) -> p i t", t=Q),
                                rep_cols(acol[:, 0:8], 8, Q), OP.mult)
                env.fin(e)
            env.end()
            for h in range(H_B):
                env.begin()
                @B.tensor
                def _(e, c=c, h=h):
                    e.wait_ge(sph, env.w)
                    bb = (h % 2) * 64
                    ps_ = psC1 if h % 2 == 0 else psC2
                    e.matmul(ps_[:, 0:128], ptil[bb:bb + 64, (h // 2) * 128:(h // 2) * 128 + 128],
                             pvnT[bb:bb + 64, (h // 2) * 128:(h // 2) * 128 + 128],
                             start=True, stop=True)
                    env.fin(e)
                env.end()
                env.begin()
                @B.vector
                def _(e, c=c, h=h):
                    e.wait_ge(sph, env.w)
                    ps_ = psC1 if h % 2 == 0 else psC2
                    e.tensor_tensor(wmk[h % 2][:], ps_[:, 0:128], utri_s[:], OP.mult)
                    env.fin(e)
                env.end()
                env.begin()
                @B.tensor
                def _(e, c=c, h=h, par=par):
                    e.wait_ge(sph, env.w)
                    bb = (h % 2) * 64
                    psOV = psA1 if h < 8 else psA2
                    cc0 = ((h % 8)) * 128
                    e.matmul(psOV[:, cc0:cc0 + 128], wmk[h % 2][:],
                             vtc[par][:, h * DV_B:(h + 1) * DV_B], start=True, stop=False)
                    e.matmul(psOV[:, cc0:cc0 + 128],
                             pvnAT[bb:bb + 64, (h // 2) * 128:(h // 2) * 128 + 128],
                             hvR[bb:bb + 64, (h // 2) * DV_B:(h // 2) * DV_B + 128],
                             start=False, stop=True)
                    env.fin(e)
                env.end()
            env.begin()
            @B.vector
            def _(e, c=c):
                e.wait_ge(sph, env.w)
                e.tensor_copy(ovb[:, 0:1024], psA1[:])
                e.drain()
                e.tensor_copy(ovb[:, 1024:2048], psA2[:])
                env.fin(e)
            env.end()
            for hf in range(2):
                env.begin()
                @B.scalar
                def _(e, c=c, hf=hf):
                    e.wait_ge(sph, env.w)
                    e.activation(out=sqb[:], in_=ovb[:, 1024 * hf:1024 * (hf + 1)], func=AF.Square)
                    env.fin(e)
                env.end()
                env.begin()
                @B.vector
                def _(e, c=c, hf=hf):
                    e.wait_ge(sph, env.w)
                    e.tensor_reduce(segm[:, 8 * hf:8 * (hf + 1)],
                                    sqb[:].rearrange("p (h v) -> p h v", v=DV_B), AX, OP.add)
                    env.fin(e)
                env.end()
            env.begin()
            @B.vector
            def _(e, c=c):
                e.wait_ge(sph, env.w)
                e.tensor_scalar_mul(segm[:], segm[:], 1.0 / DV_B)
                env.fin(e)
            env.end()
            env.begin()
            @B.scalar
            def _(e, c=c):
                e.wait_ge(sph, env.w)
                e.activation(out=segm[:], in_=segm[:], func=AF.Ln, bias=1e-5)
                e.drain()
                e.activation(out=segm[:], in_=segm[:], func=AF.Exp, scale=-0.5)
                env.fin(e)
            env.end()
            env.begin()
            @B.vector
            def _(e, c=c):
                e.wait_ge(sph, env.w)
                e.tensor_tensor(ovb[:].rearrange("p (h v) -> p h v", v=DV_B),
                                ovb[:].rearrange("p (h v) -> p h v", v=DV_B),
                                rep_cols(segm[:, 0:16], 16, DV_B), OP.mult)
                e.drain()
                e.tensor_mul(ovb[:], ovb[:], gnwbc[:])
                env.fin(e)
            env.end()
            for hf in range(2):
                env.begin()
                @B.scalar
                def _(e, c=c, par=par, hf=hf):
                    e.wait_ge(sph, env.w)
                    e.activation(out=sqb[:], in_=gtc[par][:, 1024 * hf:1024 * (hf + 1)], func=AF.Sigmoid)
                    env.fin(e)
                env.end()
                env.begin()
                @B.vector
                def _(e, c=c, par=par, hf=hf):
                    e.wait_ge(sph, env.w)
                    e.tensor_mul(sqb[:], sqb[:], gtc[par][:, 1024 * hf:1024 * (hf + 1)])
                    e.drain()
                    e.tensor_mul(ovb[:, 1024 * hf:1024 * (hf + 1)],
                                 ovb[:, 1024 * hf:1024 * (hf + 1)], sqb[:])
                    env.fin(e)
                env.end()
            for half in range(2):
                env.begin()
                @B.tensor
                def _(e, c=c, half=half):
                    e.wait_ge(sph, env.w)
                    for u in range(8):
                        e.transpose(psB1[:, 128 * u:128 * (u + 1)],
                                    ovb[:, 1024 * half + 128 * u:1024 * half + 128 * (u + 1)], idf_s[:])
                    env.fin(e)
                env.end()
                env.begin()
                @B.scalar
                def _(e, c=c, half=half):
                    e.wait_ge(sph, env.w)
                    for u in range(8):
                        k = half * 8 + u
                        e.copy(ovT[k][:, Q * c:Q * (c + 1)], psB1[:, 128 * u:128 * (u + 1)])
                    env.fin(e)
                env.end()
            env.begin()
            @B.tensor
            def _(e, c=c, par=par):
                e.wait_ge(sph, env.w)
                for h in range(H_B):
                    bb = (h % 2) * 64
                    lhs = ptk[:, (h // 2) * 128 + bb:(h // 2) * 128 + bb + 64]
                    e.matmul(psC1[bb:bb + 64, ((h // 2) % 4) * 64:((h // 2) % 4) * 64 + 64] if False else
                             psC1[bb:bb + 64, 64 * ((h // 2) % 4):64 * ((h // 2) % 4) + 64],
                             lhs, ktc[par][:, h * DK_B:(h + 1) * DK_B],
                             start=True, stop=True) if False else None
                for h in range(H_B):
                    bb = (h % 2) * 64
                    e.matmul(psC1[bb:bb + 64, (h // 2) * 64:(h // 2) * 64 + 64],
                             ptk[:, (h // 2) * 128 + bb:(h // 2) * 128 + bb + 64],
                             ktc[par][:, h * DK_B:(h + 1) * DK_B], start=True, stop=True)
                for h in range(H_B):
                    bb = (h % 2) * 64
                    psOV = psA1 if h < 8 else psA2
                    cc0 = (h % 8) * 128
                    e.matmul(psOV[:, cc0:cc0 + 128] if False else psA1[bb:bb + 64, 0:1] ,
                             ptk[:, 0:1], ktc[par][:, 0:1], start=True, stop=True) if False else None
                for h in range(H_B):
                    bb = (h % 2) * 64
                    psx = psA1 if h < 8 else psA2
                    e.matmul(psx[bb:bb + 64, (h // 2) * DV_B % 1024:(h // 2) * DV_B % 1024 + DV_B],
                             ptk[:, (h // 2) * 128 + bb:(h // 2) * 128 + bb + 64],
                             vtc[par][:, h * DV_B:(h + 1) * DV_B], start=True, stop=True)
                env.fin(e)
            env.end()
            env.begin()
            @B.scalar
            def _(e, c=c):
                e.wait_ge(sph, env.w)
                e.activation(out=rcend[:], in_=cend[:], func=AF.Ln)
                e.drain()
                e.activation(out=rcend[:], in_=rcend[:], func=AF.Exp, scale=-1.0)
                env.fin(e)
            env.end()
            env.begin()
            @B.vector
            def _(e, c=c):
                e.wait_ge(sph, env.w)
                e.tensor_mul(arc[:], acol[:], rcend[:])
                e.drain()
                for it in range(8):
                    e.tensor_scalar(hkR[:, DK_B * it:DK_B * (it + 1)],
                                    hkR[:, DK_B * it:DK_B * (it + 1)], arc[:, it:it + 1], None, OP.mult)
                e.drain()
                for it in range(8):
                    e.scalar_tensor_tensor(hkR[:, DK_B * it:DK_B * (it + 1)],
                                           psC1[:, 64 * it:64 * (it + 1)], rcend[:, it:it + 1],
                                           hkR[:, DK_B * it:DK_B * (it + 1)], OP.mult, OP.add)
                e.drain()
                for it in range(8):
                    e.tensor_scalar(hvR[:, DV_B * it:DV_B * (it + 1)],
                                    hvR[:, DV_B * it:DV_B * (it + 1)], arc[:, it:it + 1], None, OP.mult)
                e.drain()
                for it in range(8):
                    psx = psA1 if it < 4 else psA2
                    e.scalar_tensor_tensor(hvR[:, DV_B * it:DV_B * (it + 1)],
                                           psx[:, it * 128:it * 128 + 128],
                                           rcend[:, it:it + 1],
                                           hvR[:, DV_B * it:DV_B * (it + 1)], OP.mult, OP.add)
                e.drain()
                e.tensor_copy(zprev[:], rref[:])
                env.fin(e)
            env.end()
            env.begin()
            @B.scalar
            def _(e, c=c):
                e.wait_ge(sph, env.w)
                e.activation(out=cend[:], in_=cend[:], func=AF.Ln)
                env.fin(e)
            env.end()
            env.begin()
            @B.vector
            def _(e, c=c):
                e.wait_ge(sph, env.w)
                e.tensor_add(zprev[:], zprev[:], cend[:])
                env.fin(e)
            env.end()
            if DEBUG_L4:
                env.begin()
                @B.sync
                def _(e, c=c):
                    e.wait_ge(sph, env.w)
                    e.dma_start(out=dbg_z[c, :, :], in_=zprev[:]).then_inc(sph, 16)
                    e.dma_start(out=dbg_hk[c, :, :], in_=hkR[:]).then_inc(sph, 16)
                    e.dma_start(out=dbg_hv[c, :, :], in_=hvR[:]).then_inc(sph, 16)
                    env.inc(48)
                env.end()

        if DEBUG_L4:
            env.begin()
            @B.sync
            def _(e):
                e.wait_ge(sph, env.w)
                for k in range(16):
                    e.dma_start(out=dbg_ov[128 * k:128 * (k + 1), :], in_=ovT[k][:]).then_inc(sph, 16)
                env.inc(256)
            env.end()
        psG.close()
        esC2.close()

        emit_out_ln_scores(env, nc, Wo2, lambda k, c0, cwid: ovT[k][:, c0:c0 + cwid],
                           R2T, R3T, xlnT_o, xlntok_o, scores_o, vals_o, hist_o,
                           wg, idb_s, onesr, free_rhs=esC.close)
    return nc


def l4_inputs(inputs, core, l3b_outs, l3ball, l3outs):
    zm = np.zeros((128, 8 * NCORES), np.float32)
    for j in range(NCORES):
        if not (j < core and j // 4 == core // 4):
            zm[:, 8 * j:8 * (j + 1)] = -2.0e30
    return {
        "R2T": l3outs["R2T"],
        "qT": l3b_outs["qT_o"], "kT": l3b_outs["kT_o"], "sT": l3b_outs["sT_o"],
        "ktok": l3b_outs["ktok_o"], "vtok": l3b_outs["vtok_o"], "gtok": l3b_outs["gtok_o"],
        "hk_all": np.stack([l3ball[j]["hk_o"] for j in range(NCORES)]),
        "hv_all": np.stack([l3ball[j]["hv_o"] for j in range(NCORES)]),
        "zc_all": np.stack([l3ball[j]["zc_o"] for j in range(NCORES)]),
        "zmask": zm,
        "pm01": np.tile((np.array([[1.0 if (j < core and j // 4 == core // 4) else 0.0
                                     for j in range(NCORES)]], np.float32)), (128, 1)),
        "Wo2": np.asarray(inputs["L0_attn_b_Wo"]),
        "gnwt": np.tile(np.asarray(inputs["L0_attn_b_g_norm_weight"]), H_B),
        "wg": np.asarray(inputs["L0_ffn_w_gate"]),
        "idb": np.eye(128, dtype=bfnp),
        "idf": np.eye(128, dtype=np.float32),
        "utri": np.triu(np.ones((128, 128), np.float32)),
    }



def kernel(**inputs):
    cores = list(range(NCORES))
    nc1 = build_L1()
    res1 = run_bass_kernel_spmd(nc1, [l1_inputs(inputs, c) for c in cores], core_ids=cores).results
    nc2 = build_L2()
    res2 = run_bass_kernel_spmd(nc2, [l2_inputs(inputs, c, res1[c], res1) for c in cores], core_ids=cores).results
    nc3 = build_L3()
    res3 = run_bass_kernel_spmd(nc3, [l3_inputs(inputs, c, res2[c], res2) for c in cores], core_ids=cores).results
    nc3b = build_L3b()
    res3b = run_bass_kernel_spmd(nc3b, [l3b_inputs(inputs, c, res3[c]) for c in cores], core_ids=cores).results
    nc4 = build_L4()
    res4 = run_bass_kernel_spmd(nc4, [l4_inputs(inputs, c, res3b[c], res3b, res3[c]) for c in cores], core_ids=cores).results
    m4 = [dict(R1T=res4[c]["R3T"], xlnT_o=res4[c]["xlnT_o"], xlntok_o=res4[c]["xlntok_o"],
               scores_o=res4[c]["scores_o"], vals_o=res4[c]["vals_o"], hist_o=res4[c]["hist_o"])
          for c in cores]
    res5 = run_bass_kernel_spmd(nc3, [l3_inputs(inputs, c, m4[c], m4) for c in cores], core_ids=cores).results
    out = np.empty((2, 8192, D), np.float32)
    for c in cores:
        b, t0 = c // 4, (c % 4) * T
        out[b, t0:t0 + T, :] = res5[c]["R2T"].T
    return out


# version 03
# speedup vs baseline: 1.1648x; 1.1648x over previous
import contextlib
import numpy as np
import ml_dtypes
import concourse.bass as bass
import concourse.mybir as mybir
from concourse.bass_utils import run_bass_kernel_spmd

F32 = mybir.dt.float32
BF16 = mybir.dt.bfloat16
I32 = mybir.dt.int32
AF = mybir.ActivationFunctionType
OP = mybir.AluOpType
AX = mybir.AxisListType.X
bfnp = ml_dtypes.bfloat16

NCORES = 8
T = 2048
TP = 2176
HALO = 3
D = 2048
NCH = 16
NTT = 17
Q = 128
NQ = 16
H_A, P_A, N_A = 32, 64, 32
PROJ_P = 4352
NOT_P = 34
CH_BC, CH_DT, CH_G0 = 16, 17, 18
LN_EPS = 1e-5
RMS_EPS = 1e-6
SLICES = [(0, 512), (512, 512), (1024, 512), (1536, 512), (2048, 128)]


def rep_cols(apv, nseg, nrep):
    pstep, pcnt = apv.ap[0]
    fstep = apv.ap[1][0]
    return bass.AP(tensor=apv.tensor, offset=apv.offset,
                   ap=[[pstep, pcnt], [fstep, nseg], [0, nrep]])


def rep_tile(apv, nrep, seglen):
    pstep, pcnt = apv.ap[0]
    fstep = apv.ap[1][0]
    return bass.AP(tensor=apv.tensor, offset=apv.offset,
                   ap=[[pstep, pcnt], [0, nrep], [fstep, seglen]])


def strided_ap(apv, start, step, count):
    pstep, pcnt = apv.ap[0]
    fstep = apv.ap[1][0]
    return bass.AP(tensor=apv.tensor, offset=apv.offset + start * fstep,
                   ap=[[pstep, pcnt], [fstep * step, count]])


class Env:

    def __init__(self, nc, es):
        self.nc = nc
        self.es = es
        self.sph = es.enter_context(nc.semaphore("ph"))
        self.total = 0
        self.pending = 0
        self.w = 0
        self.B = None

    def sbt(self, n, s, d):
        return self.es.enter_context(self.nc.sbuf_tensor(n, s, d))

    def pst(self, n, s, d=F32):
        return self.es.enter_context(self.nc.psum_tensor(n, s, d))

    def begin(self):
        assert self.pending == 0
        self.w = self.total

    def inc(self, n):
        self.pending += n

    def end(self):
        self.total += self.pending
        self.pending = 0

    def fin(self, e, n=1):
        e.drain()
        e.nop().then_inc(self.sph, n)
        self.inc(n)

    def setup_consts(self, values):
        vals = sorted(set(float(v) for v in values))
        self.cvals = vals
        self.ctile = self.sbt("cconsts", [128, max(1, len(vals))], F32)
        for i, v in enumerate(vals):
            self.nc.const_aps.register(F32, v, self.ctile[:, i:i + 1])

    def emit_const_memsets(self, e):
        for i, v in enumerate(self.cvals):
            e.memset(self.ctile[:, i:i + 1], v)



def emit_ln_transpose(env, src_dram, ntiles, xlnT, psT, tiles, add_dram=None,
                      eps=LN_EPS, out_dram=None, xt_out=None, xt_shift=0, xt_cols=None,
                      psF=None, idf_s=None, xstg=None):
    B, sph = env.B, env.sph
    xtok, xlt = tiles["xtok"], tiles["xlt"]
    tsc, stats, stats2 = tiles["tsc"], tiles["stats"], tiles["stats2"]
    tscf = (lambda j: tsc[j % 2][:]) if isinstance(tsc, list) else (lambda j: tsc[:])
    mcol, rcol, vcol = tiles["mcol"], tiles["rcol"], tiles["vcol"]
    idb_s = tiles["idb_s"]
    atok = tiles.get("atok")

    for j in range(ntiles + 1):
        env.begin()
        if j < ntiles:
            @B.sync
            def _(e, j=j):
                e.wait_ge(sph, env.w)
                e.dma_start(out=xtok[j % 2][:], in_=src_dram[128 * j:128 * (j + 1), :]).then_inc(sph, 16)
                env.inc(16)
            if add_dram is not None:
                @B.scalar
                def _(e, j=j):
                    e.wait_ge(sph, env.w)
                    e.dma_start(out=atok[j % 2][:], in_=add_dram[128 * j:128 * (j + 1), :]).then_inc(sph, 16)
                    env.inc(16)
        if j > 0:
            if add_dram is not None:
                @B.vector
                def _(e, j=j):
                    e.wait_ge(sph, env.w)
                    e.tensor_add(xtok[(j - 1) % 2][:], xtok[(j - 1) % 2][:], atok[(j - 1) % 2][:])
                    e.drain()
                    e.reduce_sum(stats[:, j - 1:j], xtok[(j - 1) % 2][:], AX)
                    env.fin(e)
                if out_dram is not None:
                    env.end(); env.begin()
                    @B.sync
                    def _(e, j=j):
                        e.wait_ge(sph, env.w)
                        e.dma_start(out=out_dram[128 * (j - 1):128 * j, :],
                                    in_=xtok[(j - 1) % 2][:]).then_inc(sph, 16)
                        env.inc(16)
                    @B.scalar
                    def _(e, j=j):
                        e.wait_ge(sph, env.w)
                        e.activation(out=tscf(j - 1), in_=xtok[(j - 1) % 2][:], func=AF.Square,
                                     accum_out=stats2[:, j - 1:j])
                        env.fin(e)
            else:
                @B.scalar
                def _(e, j=j):
                    e.wait_ge(sph, env.w)
                    e.activation(out=tscf(j - 1), in_=xtok[(j - 1) % 2][:], func=AF.Square,
                                 accum_out=stats2[:, j - 1:j])
                    env.fin(e)
                @B.vector
                def _(e, j=j):
                    e.wait_ge(sph, env.w)
                    e.reduce_sum(stats[:, j - 1:j], xtok[(j - 1) % 2][:], AX)
                    env.fin(e)
        if add_dram is not None and j > 0 and out_dram is None:
            @B.scalar
            def _(e, j=j):
                e.wait_ge(sph, env.w)
                e.activation(out=tscf(j - 1), in_=xtok[(j - 1) % 2][:], func=AF.Square,
                             accum_out=stats2[:, j - 1:j])
                env.fin(e)
        env.end()

    env.begin()
    @B.vector
    def _(e):
        e.wait_ge(sph, env.w)
        e.tensor_scalar_mul(mcol[:, 0:ntiles], stats[:, 0:ntiles], 1.0 / D)
        e.drain()
        e.tensor_mul(vcol[:, 0:ntiles], mcol[:, 0:ntiles], mcol[:, 0:ntiles])
        e.drain()
        e.tensor_scalar_mul(vcol[:, 0:ntiles], vcol[:, 0:ntiles], -1.0)
        e.drain()
        e.scalar_tensor_tensor(vcol[:, 0:ntiles], stats2[:, 0:ntiles], 1.0 / D,
                               vcol[:, 0:ntiles], OP.mult, OP.add)
        env.fin(e)
    env.end()
    env.begin()
    @B.scalar
    def _(e):
        e.wait_ge(sph, env.w)
        e.activation(out=rcol[:, 0:ntiles], in_=vcol[:, 0:ntiles], func=AF.Ln, bias=eps)
        e.drain()
        e.activation(out=rcol[:, 0:ntiles], in_=rcol[:, 0:ntiles], func=AF.Exp, scale=-0.5)
        env.fin(e)
    env.end()

    for j in range(ntiles + 1):
        env.begin()
        if j < ntiles:
            @B.sync
            def _(e, j=j):
                e.wait_ge(sph, env.w)
                e.dma_start(out=xtok[j % 2][:], in_=src_dram[128 * j:128 * (j + 1), :]).then_inc(sph, 16)
                env.inc(16)
            if add_dram is not None:
                @B.scalar
                def _(e, j=j):
                    e.wait_ge(sph, env.w)
                    e.dma_start(out=atok[j % 2][:], in_=add_dram[128 * j:128 * (j + 1), :]).then_inc(sph, 16)
                    env.inc(16)
        env.end()
        env.begin()
        if j < ntiles:
            @B.vector
            def _(e, j=j):
                e.wait_ge(sph, env.w)
                if add_dram is not None:
                    e.tensor_add(xtok[j % 2][:], xtok[j % 2][:], atok[j % 2][:])
                    e.drain()
                e.tensor_scalar(xlt[j % 2][:], xtok[j % 2][:], mcol[:, j:j + 1],
                                rcol[:, j:j + 1], OP.subtract, OP.mult)
                env.fin(e)
        if j > 0 and xt_out is not None:
            jj = j - 1
            tcol0 = 128 * jj - xt_shift
            for grp in range(4):
                @B.tensor
                def _(e, jj=jj, grp=grp):
                    e.wait_ge(sph, env.w)
                    for u in range(4):
                        k = grp * 4 + u
                        e.transpose(psF[:, 128 * u:128 * (u + 1)],
                                    xtok[jj % 2][:, 128 * k:128 * (k + 1)], idf_s[:])
                    env.fin(e)
                env.end()
                env.begin()
                @B.vector
                def _(e, grp=grp):
                    e.wait_ge(sph, env.w)
                    e.tensor_copy(xstg[:], psF[:, 0:512])
                    env.fin(e)
                env.end()
                env.begin()
                @B.sync
                def _(e, jj=jj, grp=grp, tcol0=tcol0):
                    e.wait_ge(sph, env.w)
                    a = max(0, -tcol0)
                    bnd = xt_cols - tcol0
                    b_ = min(128, bnd)
                    if b_ > a:
                        for u in range(4):
                            k = grp * 4 + u
                            e.dma_start(out=xt_out[128 * k:128 * (k + 1), tcol0 + a:tcol0 + b_],
                                        in_=xstg[:, 128 * u + a:128 * u + b_]).then_inc(sph, 16)
                            env.inc(16)
                env.end()
                env.begin()
        if j > 0:
            for half in range(2):
                @B.tensor
                def _(e, j=j, half=half):
                    e.wait_ge(sph, env.w)
                    for u in range(8):
                        k = half * 8 + u
                        e.transpose(psT[:, 1024 * half + 128 * u:1024 * half + 128 * (u + 1)],
                                    xlt[(j - 1) % 2][:, 128 * k:128 * (k + 1)], idb_s[:])
                    env.fin(e)
                env.end()
                env.begin()
                @B.scalar
                def _(e, j=j, half=half):
                    e.wait_ge(sph, env.w)
                    for u in range(8):
                        k = half * 8 + u
                        e.copy(xlnT[k][:, 128 * (j - 1):128 * j],
                               psT[:, 1024 * half + 128 * u:1024 * half + 128 * (u + 1)])
                    env.fin(e)
        env.end()


def emit_proj(env, wdram, wrows, out_chunks, xlnT, tok_slices, psum4, wbuf,
              evac_fn, dma_fn=None, rhs_fn=None, pre_fn=None):
    B, sph = env.B, env.sph
    KCH = wrows // 128
    NOG = out_chunks // 2
    iters = [(g, si) for g in range(NOG) for si in range(len(tok_slices))]
    NIT = len(iters)
    if rhs_fn is None:
        rhs_fn = lambda k, c0, cwid: xlnT[k][:, c0:c0 + cwid]
    ncols = out_chunks * 128

    env.begin()
    @B.gpsimd
    def _(e):
        e.wait_ge(sph, env.w)
        src = bass.AP(tensor=wdram.tensor, offset=wdram.offset + 0,
                      ap=[[ncols, 128], [ncols * 128, KCH], [1, 256]])
        e.dma_start(out=wbuf[0][:, 0:KCH * 256], in_=src).then_inc(sph, 16)
        env.inc(16)
    env.end()

    for i in range(NIT + 2):
        env.begin()
        if i < NIT:
            g, si = iters[i]
            c0, cwid = tok_slices[si]
            if si == 0 and g + 1 < NOG:
                @B.gpsimd
                def _(e, g=g):
                    e.wait_ge(sph, env.w)
                    src = bass.AP(tensor=wdram.tensor, offset=wdram.offset + (g + 1) * 256,
                                  ap=[[ncols, 128], [ncols * 128, KCH], [1, 256]])
                    e.dma_start(out=wbuf[(g + 1) % 2][:, 0:KCH * 256], in_=src).then_inc(sph, 16)
                    env.inc(16)
            @B.tensor
            def _(e, g=g, c0=c0, cwid=cwid, i=i):
                e.wait_ge(sph, env.w)
                pa = psum4[2 * (i % 2)]
                pb = psum4[2 * (i % 2) + 1]
                for k in range(KCH):
                    e.matmul(pa[:, 0:cwid], wbuf[g % 2][:, k * 256:k * 256 + 128],
                             rhs_fn(k, c0, cwid), start=(k == 0), stop=(k == KCH - 1))
                for k in range(KCH):
                    e.matmul(pb[:, 0:cwid], wbuf[g % 2][:, k * 256 + 128:k * 256 + 256],
                             rhs_fn(k, c0, cwid), start=(k == 0), stop=(k == KCH - 1))
                env.fin(e)
        if 0 < i <= NIT:
            gp, sip = iters[i - 1]
            c0p, cwp = tok_slices[sip]
            @B.vector
            def _(e, gp=gp, c0p=c0p, cwp=cwp, i=i):
                e.wait_ge(sph, env.w)
                pa = psum4[2 * ((i - 1) % 2)]
                pb = psum4[2 * ((i - 1) % 2) + 1]
                evac_fn(e, 2 * gp, c0p, cwp, pa, i - 1)
                evac_fn(e, 2 * gp + 1, c0p, cwp, pb, i - 1)
                env.fin(e)
        if (i >= 2 and dma_fn is not None) or (i < NIT and pre_fn is not None):
            @B.sync
            def _(e, i=i):
                e.wait_ge(sph, env.w)
                if i < NIT and pre_fn is not None:
                    pre_fn(e, i)
                if i >= 2 and dma_fn is not None:
                    gq, siq = iters[i - 2]
                    c0q, cwq = tok_slices[siq]
                    dma_fn(e, 2 * gq, c0q, cwq, i - 2)
                    dma_fn(e, 2 * gq + 1, c0q, cwq, i - 2)
        env.end()



def build_L1():
    nc = bass.Bass("TRN2", target_bir_lowering=False)
    xh = nc.dram_tensor("xh", [TP, D], F32, kind="ExternalInput").ap()
    Wip = nc.dram_tensor("Wip", [D, PROJ_P], F32, kind="ExternalInput").ap()
    bip = nc.dram_tensor("bip", [PROJ_P], F32, kind="ExternalInput").ap()
    cw = nc.dram_tensor("cw", [2112, 4], F32, kind="ExternalInput").ap()
    cb = nc.dram_tensor("cb", [2112], F32, kind="ExternalInput").ap()
    alog = nc.dram_tensor("alog", [H_A], F32, kind="ExternalInput").ap()
    dtb = nc.dram_tensor("dtb", [H_A], F32, kind="ExternalInput").ap()
    hmask = nc.dram_tensor("hmask", [128, HALO], F32, kind="ExternalInput").ap()
    idb = nc.dram_tensor("idb", [128, 128], BF16, kind="ExternalInput").ap()
    idf = nc.dram_tensor("idf", [128, 128], F32, kind="ExternalInput").ap()

    xTo = nc.dram_tensor("xTo", [D, T], F32, kind="ExternalOutput").ap()
    gateT = nc.dram_tensor("gateT", [D, TP], BF16, kind="ExternalOutput").ap()
    convT = nc.dram_tensor("convT", [2112, TP], BF16, kind="ExternalOutput").ap()
    dtT = nc.dram_tensor("dtT", [H_A, TP], F32, kind="ExternalOutput").ap()
    schk = nc.dram_tensor("schk", [NQ, N_A, D], F32, kind="ExternalOutput").ap()
    score = nc.dram_tensor("score", [N_A, D], F32, kind="ExternalOutput").ap()
    ltotc = nc.dram_tensor("ltotc", [H_A, NQ], F32, kind="ExternalOutput").ap()

    es = contextlib.ExitStack()
    with es:
        env = Env(nc, es)
        sbt, pst = env.sbt, env.pst

        hbc = [sbt(f"hbc{k}", [128, TP], BF16) for k in range(17)]
        dtt = sbt("dtt", [H_A, TP], F32)
        lbuf = sbt("lbuf", [H_A, Q * NQ], F32)
        cacc = [sbt(f"cacc{i}", [128, TP], F32) for i in range(2)]
        cwS = sbt("cwS", [128, 4 * 17], F32)
        cbS = sbt("cbS", [128, 17], F32)
        bipS = sbt("bipS", [128, NOT_P], F32)
        hm = sbt("hm", [128, HALO], F32)
        idb_s = sbt("idb_s", [128, 128], BF16)
        idf_s = sbt("idf_s", [128, 128], F32)
        onesr = sbt("onesr", [1, 128], F32)
        sA = sbt("sA", [H_A, 1], F32)
        sdtb = sbt("sdtb", [H_A, 1], F32)
        stats = sbt("stats", [128, NTT], F32)
        stats2 = sbt("stats2", [128, NTT], F32)
        mcol = sbt("mcol", [128, NTT], F32)
        rcol = sbt("rcol", [128, NTT], F32)
        vcol = sbt("vcol", [128, NTT], F32)
        ltrow = sbt("ltrow", [1, H_A], F32)
        erow = sbt("erow", [1, H_A], F32)

        xtok = [cacc[i][:, 0:D] for i in range(2)]
        env.setup_consts([LN_EPS, RMS_EPS, 1.0, -1.0, -0.5, 0.5, 2.0, 0.0, 1.0 / D, -32.0, 32.0])

        psum4 = [pst(f"ps{i}", [128, 512]) for i in range(4)]
        psT = pst("psT", [128, 2048], BF16)
        psF = pst("psF", [128, 512])
        psTb = pst("psTb", [128, 512], BF16)

        B = es.enter_context(nc.Block())
        env.B = B
        sph = env.sph

        esB = contextlib.ExitStack()
        sbB = lambda n, s, d: esB.enter_context(nc.sbuf_tensor(n, s, d))
        xlnT = [sbB(f"xlnT{k}", [128, TP], BF16) for k in range(NCH)]
        wbuf = [sbB(f"wbuf{i}", [128, 16 * 256], BF16) for i in range(2)]
        xlt = [sbB(f"xlt{i}", [128, D], BF16) for i in range(2)]
        gst = [[sbB(f"gst{i}_{h}", [128, 512], BF16) for h in range(2)] for i in range(2)]

        env.begin()
        @B.sync
        def _(e):
            e.dma_start(out=idb_s[:], in_=idb[:]).then_inc(sph, 16)
            e.dma_start(out=idf_s[:], in_=idf[:]).then_inc(sph, 16)
            e.dma_start(out=hm[:], in_=hmask[:]).then_inc(sph, 16)
            e.dma_start(out=sdtb[:, 0], in_=dtb[:]).then_inc(sph, 16)
            e.dma_start(out=sA[:, 0], in_=alog[:]).then_inc(sph, 16)
            for j in range(16):
                e.dma_start(out=cwS[:, 4 * j:4 * j + 4], in_=cw[128 * j:128 * (j + 1), :]).then_inc(sph, 16)
                e.dma_start(out=cbS[:, j:j + 1], in_=cb[128 * j:128 * (j + 1)]).then_inc(sph, 16)
            e.dma_start(out=cwS[0:32, 64:68], in_=cw[2048:2080, :]).then_inc(sph, 16)
            e.dma_start(out=cwS[32:64, 64:68], in_=cw[2080:2112, :]).then_inc(sph, 16)
            e.dma_start(out=cbS[0:32, 16:17], in_=cb[2048:2080]).then_inc(sph, 16)
            e.dma_start(out=cbS[32:64, 16:17], in_=cb[2080:2112]).then_inc(sph, 16)
            env.inc(16 * 41)
        @B.scalar
        def _(e):
            for ot in range(NOT_P):
                e.dma_start(out=bipS[:, ot:ot + 1], in_=bip[128 * ot:128 * (ot + 1)]).then_inc(sph, 16)
            env.inc(16 * NOT_P)
        @B.vector
        def _(e):
            e.memset(onesr[:], 1.0)
            e.drain()
            env.emit_const_memsets(e)
            env.fin(e)
        env.end()

        env.begin()
        @B.scalar
        def _(e):
            e.wait_ge(sph, env.w)
            e.activation(out=sA[:], in_=sA[:], func=AF.Exp)
            e.drain()
            e.mul(sA[:], sA[:], -1.0)
            env.fin(e)
        env.end()

        xstg = sbB("xstg", [128, 512], F32)
        ln_tiles = dict(xtok=xtok, xlt=xlt, tsc=xlt, stats=stats, stats2=stats2,
                        mcol=mcol, rcol=rcol, vcol=vcol, idb_s=idb_s)
        emit_ln_transpose(env, xh, NTT, xlnT, psT, ln_tiles,
                          xt_out=xTo, xt_shift=HALO, xt_cols=T,
                          psF=psF, idf_s=idf_s, xstg=xstg)

        def evac(e, ot, c0, cwid, ps, it):
            if ot <= 15:
                e.tensor_scalar(hbc[ot][:, c0:c0 + cwid], ps[:, 0:cwid],
                                bipS[:, ot:ot + 1], None, OP.add)
            elif ot == CH_BC:
                e.tensor_scalar(hbc[16][0:64, c0:c0 + cwid], ps[0:64, 0:cwid],
                                bipS[0:64, ot:ot + 1], None, OP.add)
            elif ot == CH_DT:
                e.tensor_scalar(dtt[0:32, c0:c0 + cwid], ps[0:32, 0:cwid],
                                bipS[0:32, ot:ot + 1], None, OP.add)
            else:
                e.tensor_scalar(gst[it % 2][ot % 2][:, 0:cwid], ps[:, 0:cwid],
                                bipS[:, ot:ot + 1], None, OP.add)

        def gdma(e, ot, c0, cwid, it):
            if ot >= CH_G0:
                e.dma_start(out=gateT[128 * (ot - CH_G0):128 * (ot - CH_G0 + 1), c0:c0 + cwid],
                            in_=gst[it % 2][ot % 2][:, 0:cwid]).then_inc(sph, 16)
                env.inc(16)

        emit_proj(env, Wip, D, NOT_P, xlnT, SLICES, psum4, wbuf, evac, dma_fn=gdma)

        esB.close()

        esC = contextlib.ExitStack()
        sbC = lambda n, s, d: esC.enter_context(nc.sbuf_tensor(n, s, d))
        dl = sbC("dl", [H_A, TP], F32)
        sctA = sbC("sctA", [H_A, TP], F32)
        sctB = sbC("sctB", [H_A, TP], F32)
        lcol = sbC("lcol", [128, H_A], F32)
        dtcol = sbC("dtcol", [128, H_A], F32)
        w1c = sbC("w1c", [128, H_A], F32)
        btokb = sbC("btokb", [128, N_A], BF16)
        xtk = sbC("xtk", [128, D], BF16)
        xw1 = sbC("xw1", [128, D], BF16)
        sloc = sbC("sloc", [N_A, D], F32)
        stmp = sbC("stmp", [N_A, D], F32)

        env.begin()
        @B.vector
        def _(e):
            e.wait_ge(sph, env.w)
            for k in range(16):
                e.tensor_mul(hbc[k][:, 0:HALO], hbc[k][:, 0:HALO], hm[:, 0:HALO])
            e.tensor_mul(hbc[16][0:64, 0:HALO], hbc[16][0:64, 0:HALO], hm[0:64, 0:HALO])
            e.drain()
            e.memset(sloc[:], 0.0)
            env.fin(e)
        env.end()

        conv_units = [(k, 128, k) for k in range(16)] + [(16, 64, 16)]
        for uj in range(len(conv_units) + 1):
            env.begin()
            if uj < len(conv_units):
                k, rows, wi = conv_units[uj]
                @B.vector
                def _(e, k=k, rows=rows, wi=wi, uj=uj):
                    e.wait_ge(sph, env.w)
                    a = cacc[uj % 2]
                    W_ = TP - HALO
                    e.tensor_scalar(a[0:rows, HALO:TP], hbc[k][0:rows, 0:W_],
                                    cwS[0:rows, 4 * wi:4 * wi + 1], None, OP.mult)
                    for tap in range(1, 4):
                        e.drain()
                        e.scalar_tensor_tensor(a[0:rows, HALO:TP], hbc[k][0:rows, tap:W_ + tap],
                                               cwS[0:rows, 4 * wi + tap:4 * wi + tap + 1],
                                               a[0:rows, HALO:TP], OP.mult, OP.add)
                    env.fin(e)
            if uj > 0:
                k, rows, wi = conv_units[uj - 1]
                @B.scalar
                def _(e, k=k, rows=rows, wi=wi, uj=uj):
                    e.wait_ge(sph, env.w)
                    e.activation(out=hbc[k][0:rows, HALO:TP], in_=cacc[(uj - 1) % 2][0:rows, HALO:TP],
                                 func=AF.Silu, bias=cbS[0:rows, wi:wi + 1])
                    env.fin(e)
            env.end()

        env.begin()
        @B.sync
        def _(e):
            e.wait_ge(sph, env.w)
            for k in range(16):
                e.dma_start(out=convT[128 * k:128 * (k + 1), :], in_=hbc[k][:]).then_inc(sph, 16)
            e.dma_start(out=convT[2048:2080, :], in_=hbc[16][0:32, :]).then_inc(sph, 16)
            e.dma_start(out=convT[2080:2112, :], in_=hbc[16][32:64, :]).then_inc(sph, 16)
            env.inc(16 * 18)
        @B.vector
        def _(e):
            e.wait_ge(sph, env.w)
            e.tensor_scalar(dtt[:], dtt[:], sdtb[:, 0:1], None, OP.add)
            e.drain()
            e.tensor_scalar_max(sctB[:], dtt[:], 0.0)
            env.fin(e)
        env.end()
        env.begin()
        @B.scalar
        def _(e):
            e.wait_ge(sph, env.w)
            e.activation(out=sctA[:], in_=dtt[:], func=AF.Abs)
            e.drain()
            e.activation(out=sctA[:], in_=sctA[:], func=AF.Exp, scale=-1.0)
            e.drain()
            e.activation(out=sctA[:], in_=sctA[:], func=AF.Ln, bias=1.0)
            env.fin(e)
        env.end()
        env.begin()
        @B.vector
        def _(e):
            e.wait_ge(sph, env.w)
            e.tensor_add(dtt[:], sctA[:], sctB[:])
            e.drain()
            e.tensor_scalar(dl[:], dtt[:], sA[:, 0:1], None, OP.mult)
            env.fin(e)
        env.end()
        env.begin()
        @B.sync
        def _(e):
            e.wait_ge(sph, env.w)
            e.dma_start(out=dtT[:], in_=dtt[:]).then_inc(sph, 16)
            env.inc(16)
        @B.vector
        def _(e):
            e.wait_ge(sph, env.w)
            for c in range(NQ):
                e.tensor_tensor_scan(lbuf[:, Q * c:Q * (c + 1)],
                                     dl[:, HALO + Q * c:HALO + Q * (c + 1)],
                                     dl[:, HALO + Q * c:HALO + Q * (c + 1)],
                                     0.0, OP.add, OP.bypass)
            env.fin(e)
        env.end()

        env.begin()
        @B.sync
        def _(e):
            e.wait_ge(sph, env.w)
            src = strided_ap(lbuf[:], Q - 1, Q, NQ)
            e.dma_start(out=ltotc[:], in_=src, allow_slow_non_contiguous=True).then_inc(sph, 16)
            env.inc(16)
        env.end()

        for c in range(NQ):
            cb0 = HALO + Q * c
            env.begin()
            @B.tensor
            def _(e, c=c, cb0=cb0):
                e.wait_ge(sph, env.w)
                e.transpose(psF[:, 0:32], lbuf[0:32, Q * c:Q * (c + 1)], idf_s[0:32, 0:32])
                e.transpose(psF[:, 32:64], dtt[0:32, cb0:cb0 + Q], idf_s[0:32, 0:32])
                env.fin(e)
            @B.sync
            def _(e, c=c):
                e.wait_ge(sph, env.w)
                e.dma_start(out=ltrow[0:1, 0:H_A], in_=lbuf[:, Q * c + Q - 1:Q * c + Q],
                            allow_slow_non_contiguous=True).then_inc(sph, 16)
                env.inc(16)
            env.end()
            env.begin()
            @B.vector
            def _(e, c=c):
                e.wait_ge(sph, env.w)
                e.tensor_copy(lcol[:, :], psF[:, 0:32])
                e.drain()
                e.tensor_copy(dtcol[:, :], psF[:, 32:64])
                e.drain()
                e.tensor_scalar_mul(lcol[:, :], lcol[:, :], -1.0)
                env.fin(e)
            @B.scalar
            def _(e, c=c):
                e.wait_ge(sph, env.w)
                e.activation(out=erow[:], in_=ltrow[:], func=AF.Exp)
                env.fin(e)
            env.end()
            env.begin()
            @B.tensor
            def _(e, c=c, cb0=cb0):
                e.wait_ge(sph, env.w)
                for k in range(16):
                    e.transpose(psT[:, 128 * k:128 * (k + 1)], hbc[k][:, cb0:cb0 + Q], idb_s[:])
                env.fin(e)
            env.end()
            env.begin()
            @B.scalar
            def _(e, c=c):
                e.wait_ge(sph, env.w)
                e.copy(xtk[:], psT[:])
                env.fin(e)
            @B.tensor
            def _(e, c=c, cb0=cb0):
                e.wait_ge(sph, env.w)
                e.transpose(psTb[:, 0:32], hbc[16][0:32, cb0:cb0 + Q], idb_s[0:32, 0:32])
                env.fin(e)
            env.end()
            env.begin()
            @B.tensor
            def _(e, c=c):
                e.wait_ge(sph, env.w)
                e.matmul(psF[:, 64:96], onesr[:, :], ltrow[0:1, :], start=True, stop=True)
                env.fin(e)
            @B.scalar
            def _(e, c=c):
                e.wait_ge(sph, env.w)
                e.copy(btokb[:], psTb[:, 0:32])
                env.fin(e)
            env.end()
            env.begin()
            @B.vector
            def _(e, c=c):
                e.wait_ge(sph, env.w)
                e.tensor_tensor(w1c[:], psF[:, 64:96], lcol[:], OP.add)
                env.fin(e)
            env.end()
            env.begin()
            @B.scalar
            def _(e, c=c):
                e.wait_ge(sph, env.w)
                e.activation(out=w1c[:], in_=w1c[:], func=AF.Exp)
                env.fin(e)
            env.end()
            env.begin()
            @B.vector
            def _(e, c=c):
                e.wait_ge(sph, env.w)
                e.tensor_mul(w1c[:], w1c[:], dtcol[:])
                e.drain()
                e.tensor_tensor(xw1[:].rearrange("p (h u) -> p h u", u=P_A),
                                xtk[:].rearrange("p (h u) -> p h u", u=P_A),
                                rep_cols(w1c[:, 0:H_A], H_A, P_A), OP.mult)
                env.fin(e)
            env.end()
            env.begin()
            @B.tensor
            def _(e, c=c):
                e.wait_ge(sph, env.w)
                for h in range(H_A):
                    ps_ = psum4[h // 8]
                    e.matmul(ps_[0:N_A, (h % 8) * 64:(h % 8) * 64 + 64],
                             btokb[:], xw1[:, P_A * h:P_A * (h + 1)],
                             start=True, stop=True)
                env.fin(e)
            env.end()
            env.begin()
            @B.vector
            def _(e, c=c):
                e.wait_ge(sph, env.w)
                for qq in range(4):
                    e.tensor_copy(stmp[:, 512 * qq:512 * (qq + 1)], psum4[qq][0:N_A, :])
                env.fin(e)
            env.end()
            env.begin()
            @B.sync
            def _(e, c=c):
                e.wait_ge(sph, env.w)
                e.dma_start(out=schk[c, :, :], in_=stmp[:]).then_inc(sph, 16)
                env.inc(16)
            env.end()
            for qq in range(4):
                env.begin()
                @B.tensor
                def _(e, c=c, qq=qq):
                    e.wait_ge(sph, env.w)
                    e.matmul(psF[0:N_A, 0:512], onesr[0:1, 0:N_A],
                             rep_cols(erow[0:1, 8 * qq:8 * qq + 8], 8, 64),
                             start=True, stop=True)
                    env.fin(e)
                env.end()
                env.begin()
                @B.vector
                def _(e, c=c, qq=qq):
                    e.wait_ge(sph, env.w)
                    e.tensor_tensor(sloc[:, 512 * qq:512 * (qq + 1)],
                                    sloc[:, 512 * qq:512 * (qq + 1)], psF[0:N_A, 0:512], OP.mult)
                    e.drain()
                    e.tensor_add(sloc[:, 512 * qq:512 * (qq + 1)],
                                 sloc[:, 512 * qq:512 * (qq + 1)],
                                 stmp[:, 512 * qq:512 * (qq + 1)])
                    env.fin(e)
                env.end()

        env.begin()
        @B.sync
        def _(e):
            e.wait_ge(sph, env.w)
            e.dma_start(out=score[:], in_=sloc[:]).then_inc(sph, 16)
            env.inc(16)
        env.end()
        env.begin()
        @B.sync
        def _(e):
            e.wait_ge(sph, env.w)
        env.end()
        esC.close()

    return nc



def reorder_Wip(W_in):
    out = np.zeros((W_in.shape[0], PROJ_P), W_in.dtype)
    out[:, 0:2048] = W_in[:, 2048:4096]
    out[:, 2048:2080] = W_in[:, 4096:4128]
    out[:, 2080:2112] = W_in[:, 4128:4160]
    out[:, 2176:2208] = W_in[:, 4160:4192]
    out[:, 2304:4352] = W_in[:, 0:2048]
    return out


def reorder_bip(b_in):
    out = np.zeros((PROJ_P,), b_in.dtype)
    out[0:2048] = b_in[2048:4096]
    out[2048:2080] = b_in[4096:4128]
    out[2080:2112] = b_in[4128:4160]
    out[2176:2208] = b_in[4160:4192]
    out[2304:4352] = b_in[0:2048]
    return out


def l1_inputs(inputs, core):
    x = inputs["L0_attn_a_x"]
    b, t0 = core // 4, (core % 4) * T
    xhv = np.zeros((TP, D), np.float32)
    if t0 > 0:
        xhv[0:HALO] = x[b, t0 - HALO:t0]
    xhv[HALO:HALO + T] = x[b, t0:t0 + T]
    hm = np.ones((128, HALO), np.float32) if t0 > 0 else np.zeros((128, HALO), np.float32)
    return {
        "xh": xhv,
        "Wip": reorder_Wip(np.asarray(inputs["L0_attn_a_W_in"])),
        "bip": reorder_bip(np.asarray(inputs["L0_attn_a_b_in"])),
        "cw": np.asarray(inputs["L0_attn_a_conv_w"]),
        "cb": np.asarray(inputs["L0_attn_a_conv_b"]),
        "alog": np.asarray(inputs["L0_attn_a_A_log"]),
        "dtb": np.asarray(inputs["L0_attn_a_dt_bias"]),
        "hmask": hm,
        "idb": np.eye(128, dtype=bfnp),
        "idf": np.eye(128, dtype=np.float32),
    }


def emit_out_ln_scores(env, nc, Wo, rhs_tile_fn, xT, R1T, xlnT_o, xlntok_o,
                       scores_o, vals_o, hist_o, wg, idb_s, onesr, kch=16,
                       free_rhs=None):
    B, sph = env.B, env.sph
    esW = contextlib.ExitStack()
    sbW = lambda n, s, d: esW.enter_context(nc.sbuf_tensor(n, s, d))
    psW = contextlib.ExitStack()
    psum4 = [psW.enter_context(nc.psum_tensor(f"psw{i}", [128, 512], F32)) for i in range(4)]
    wbufW = [sbW(f"wbufW{i}", [128, kch * 256], BF16) for i in range(2)]
    xTs = [sbW(f"xTs{i}", [128, 1024], F32) for i in range(2)]
    r1s = [sbW(f"r1s{i}", [128, 1024], F32) for i in range(2)]
    SL4 = [(0, 512), (512, 512), (1024, 512), (1536, 512)]

    def w_rhs(k, c0, cwid):
        return rhs_tile_fn(k, c0, cwid)

    def w_pre(e, it):
        g, si = it // 4, it % 4
        c0, cwid = SL4[si]
        for half in range(2):
            ot = 2 * g + half
            e.dma_start(out=xTs[it % 2][:, 512 * half:512 * half + cwid],
                        in_=xT[128 * ot:128 * (ot + 1), c0:c0 + cwid]).then_inc(sph, 16)
            env.inc(16)

    def w_evac(e, ot, c0, cwid, ps, it):
        half = ot % 2
        e.tensor_tensor(r1s[it % 2][:, 512 * half:512 * half + cwid], ps[:, 0:cwid],
                        xTs[it % 2][:, 512 * half:512 * half + cwid], OP.add)

    def w_dma(e, ot, c0, cwid, it):
        half = ot % 2
        e.dma_start(out=R1T[128 * ot:128 * (ot + 1), c0:c0 + cwid],
                    in_=r1s[it % 2][:, 512 * half:512 * half + cwid]).then_inc(sph, 16)
        env.inc(16)

    emit_proj(env, Wo, kch * 128, 16, None, SL4, psum4, wbufW, w_evac,
              dma_fn=w_dma, rhs_fn=w_rhs, pre_fn=w_pre)

    psW.close()
    esW.close()
    if free_rhs is not None:
        free_rhs()

    esL = contextlib.ExitStack()
    sbL = lambda n, s, d: esL.enter_context(nc.sbuf_tensor(n, s, d))
    psL = contextlib.ExitStack()
    psS2a = psL.enter_context(nc.psum_tensor("psS2a", [1, 512], F32))
    psS2b = psL.enter_context(nc.psum_tensor("psS2b", [1, 512], F32))
    r1blk = [sbL(f"r1blk{i}", [128, 512], F32) for i in range(3)]
    sqblk = [sbL(f"sqblk{i}", [128, 512], F32) for i in range(2)]
    onescF = sbL("onescF", [128, 1], F32)
    mrow = sbL("mrow", [1, T], F32)
    srow = sbL("srow", [1, T], F32)
    rrow = sbL("rrow", [1, T], F32)

    env.begin()
    @B.vector
    def _(e):
        e.wait_ge(sph, env.w)
        e.memset(onescF[:], 1.0)
        env.fin(e)
    env.end()

    blocks = [(s, ot) for s in range(4) for ot in range(16)]
    for i in range(len(blocks) + 2):
        env.begin()
        if i < len(blocks):
            s, ot = blocks[i]
            @B.sync
            def _(e, s=s, ot=ot, i=i):
                e.wait_ge(sph, env.w)
                e.dma_start(out=r1blk[i % 3][:], in_=R1T[128 * ot:128 * (ot + 1),
                                                       512 * s:512 * (s + 1)]).then_inc(sph, 16)
                env.inc(16)
        if 1 <= i <= len(blocks):
            s, ot = blocks[i - 1]
            @B.scalar
            def _(e, s=s, ot=ot, i=i):
                e.wait_ge(sph, env.w)
                e.activation(out=sqblk[(i - 1) % 2][:], in_=r1blk[(i - 1) % 3][:], func=AF.Square)
                env.fin(e)
        if 2 <= i <= len(blocks) + 1:
            s, ot = blocks[i - 2]
            @B.tensor
            def _(e, s=s, ot=ot, i=i):
                e.wait_ge(sph, env.w)
                e.matmul(psS2a[0:1, :], onescF[:, 0:1], r1blk[(i - 2) % 3][:],
                         start=(ot == 0), stop=(ot == 15))
                e.matmul(psS2b[0:1, :], onescF[:, 0:1], sqblk[(i - 2) % 2][:],
                         start=(ot == 0), stop=(ot == 15))
                env.fin(e)
            if ot == 15:
                env.end()
                env.begin()
                @B.vector
                def _(e, s=s):
                    e.wait_ge(sph, env.w)
                    e.tensor_scalar_mul(mrow[0:1, 512 * s:512 * (s + 1)], psS2a[0:1, :], 1.0 / D)
                    e.drain()
                    e.tensor_scalar_mul(srow[0:1, 512 * s:512 * (s + 1)], psS2b[0:1, :], 1.0 / D)
                    env.fin(e)
        env.end()

    env.begin()
    @B.vector
    def _(e):
        e.wait_ge(sph, env.w)
        e.tensor_mul(rrow[:], mrow[:], mrow[:])
        e.drain()
        e.tensor_sub(rrow[:], srow[:], rrow[:])
        env.fin(e)
    env.end()
    env.begin()
    @B.scalar
    def _(e):
        e.wait_ge(sph, env.w)
        e.activation(out=rrow[:], in_=rrow[:], func=AF.Ln, bias=LN_EPS)
        e.drain()
        e.activation(out=rrow[:], in_=rrow[:], func=AF.Exp, scale=-0.5)
        env.fin(e)
    env.end()

    esX = contextlib.ExitStack()
    sbX = lambda n, s, d: esX.enter_context(nc.sbuf_tensor(n, s, d))
    psX = contextlib.ExitStack()
    xln2T = [sbX(f"xln2T{k}", [128, T], BF16) for k in range(16)]
    mbc = sbX("mbc", [128, T], F32)
    rbc = sbX("rbc", [128, T], F32)
    wgS = sbX("wgS", [128, 16 * 64], BF16)
    scoreS = sbX("scoreS", [128, 16 * 64], F32)
    valsS = sbX("valsS", [128, 16 * 8], F32)
    mbuf = sbX("mbuf", [128, 64], F32)
    histS = sbX("histS", [64, 1], F32)
    xtst = sbX("xtst", [128, 512], BF16)
    psB1 = psX.enter_context(nc.psum_tensor("psB1", [128, 512], F32))
    psT2 = psX.enter_context(nc.psum_tensor("psT2", [128, 2048], BF16))
    psSC = psX.enter_context(nc.psum_tensor("psSC", [128, 64], F32))
    psH = psX.enter_context(nc.psum_tensor("psH", [64, 64], F32))

    for s in range(4):
        env.begin()
        @B.tensor
        def _(e, s=s):
            e.wait_ge(sph, env.w)
            e.matmul(psB1[:, :], onesr[:, :], mrow[0:1, 512 * s:512 * (s + 1)],
                     start=True, stop=True)
            env.fin(e)
        env.end()
        env.begin()
        @B.vector
        def _(e, s=s):
            e.wait_ge(sph, env.w)
            e.tensor_copy(mbc[:, 512 * s:512 * (s + 1)], psB1[:, :])
            env.fin(e)
        env.end()
        env.begin()
        @B.tensor
        def _(e, s=s):
            e.wait_ge(sph, env.w)
            e.matmul(psB1[:, :], onesr[:, :], rrow[0:1, 512 * s:512 * (s + 1)],
                     start=True, stop=True)
            env.fin(e)
        env.end()
        env.begin()
        @B.vector
        def _(e, s=s):
            e.wait_ge(sph, env.w)
            e.tensor_copy(rbc[:, 512 * s:512 * (s + 1)], psB1[:, :])
            env.fin(e)
        env.end()

    env.begin()
    @B.gpsimd
    def _(e):
        e.wait_ge(sph, env.w)
        src2 = bass.AP(tensor=wg.tensor, offset=wg.offset,
                       ap=[[64, 128], [64 * 128, 16], [1, 64]])
        e.dma_start(out=wgS[:], in_=src2).then_inc(sph, 16)
        env.inc(16)
    env.end()
    for i in range(len(blocks) + 1):
        env.begin()
        if i < len(blocks):
            s, ot = blocks[i]
            @B.sync
            def _(e, s=s, ot=ot, i=i):
                e.wait_ge(sph, env.w)
                e.dma_start(out=r1blk[i % 2][:], in_=R1T[128 * ot:128 * (ot + 1),
                                                       512 * s:512 * (s + 1)]).then_inc(sph, 16)
                env.inc(16)
        if i >= 1:
            s, ot = blocks[i - 1]
            @B.vector
            def _(e, s=s, ot=ot, i=i):
                e.wait_ge(sph, env.w)
                e.tensor_sub(sqblk[(i - 1) % 2][:], r1blk[(i - 1) % 2][:],
                             mbc[:, 512 * s:512 * (s + 1)])
                e.drain()
                e.tensor_mul(xln2T[ot][:, 512 * s:512 * (s + 1)], sqblk[(i - 1) % 2][:],
                             rbc[:, 512 * s:512 * (s + 1)])
                env.fin(e)
        env.end()

    env.begin()
    @B.sync
    def _(e):
        e.wait_ge(sph, env.w)
        for k in range(16):
            e.dma_start(out=xlnT_o[128 * k:128 * (k + 1), :], in_=xln2T[k][:]).then_inc(sph, 16)
        env.inc(16 * 16)
    env.end()

    for tt in range(16):
        for half in range(2):
            env.begin()
            @B.tensor
            def _(e, tt=tt, half=half):
                e.wait_ge(sph, env.w)
                for u in range(8):
                    k = half * 8 + u
                    e.transpose(psT2[:, 1024 * half + 128 * u:1024 * half + 128 * (u + 1)],
                                xln2T[k][:, 128 * tt:128 * (tt + 1)], idb_s[:])
                env.fin(e)
            env.end()
            env.begin()
            @B.scalar
            def _(e, tt=tt, half=half):
                e.wait_ge(sph, env.w)
                e.copy(xtst[:], psT2[:, 1024 * half:1024 * half + 512])
                e.drain()
                e.copy(xtst[:] if False else xtst[:], psT2[:, 1024 * half:1024 * half + 512]) if False else None
                env.fin(e)
            @B.sync
            def _(e, tt=tt, half=half):
                pass
            env.end()
            env.begin()
            @B.sync
            def _(e, tt=tt, half=half):
                e.wait_ge(sph, env.w)
                e.dma_start(out=xlntok_o[128 * tt:128 * (tt + 1), 1024 * half:1024 * half + 512],
                            in_=xtst[:]).then_inc(sph, 16)
                env.inc(16)
            env.end()
            env.begin()
            @B.scalar
            def _(e, tt=tt, half=half):
                e.wait_ge(sph, env.w)
                e.copy(xtst[:], psT2[:, 1024 * half + 512:1024 * half + 1024])
                env.fin(e)
            env.end()
            env.begin()
            @B.sync
            def _(e, tt=tt, half=half):
                e.wait_ge(sph, env.w)
                e.dma_start(out=xlntok_o[128 * tt:128 * (tt + 1), 1024 * half + 512:1024 * (half + 1)],
                            in_=xtst[:]).then_inc(sph, 16)
                env.inc(16)
            env.end()
        env.begin()
        @B.tensor
        def _(e, tt=tt):
            e.wait_ge(sph, env.w)
            for k in range(16):
                e.matmul(psSC[:, :], xln2T[k][:, 128 * tt:128 * (tt + 1)],
                         wgS[:, 64 * k:64 * (k + 1)], start=(k == 0), stop=(k == 15))
            env.fin(e)
        env.end()
        env.begin()
        @B.scalar
        def _(e, tt=tt):
            e.wait_ge(sph, env.w)
            e.activation(out=scoreS[:, 64 * tt:64 * (tt + 1)], in_=psSC[:, :], func=AF.Sigmoid)
            env.fin(e)
        env.end()
        env.begin()
        @B.vector
        def _(e, tt=tt):
            e.wait_ge(sph, env.w)
            e.max(out=valsS[:, 8 * tt:8 * (tt + 1)], in_=scoreS[:, 64 * tt:64 * (tt + 1)])
            e.drain()
            e.tensor_scalar(mbuf[:], scoreS[:, 64 * tt:64 * (tt + 1)],
                            valsS[:, 8 * tt + 5:8 * tt + 6], None, OP.is_ge)
            env.fin(e)
        env.end()
        env.begin()
        @B.tensor
        def _(e, tt=tt):
            e.wait_ge(sph, env.w)
            e.matmul(psH[:, 0:1], mbuf[:], onescF[:, 0:1], start=(tt == 0), stop=(tt == 15))
            env.fin(e)
        @B.sync
        def _(e, tt=tt):
            e.wait_ge(sph, env.w)
            e.dma_start(out=scores_o[128 * tt:128 * (tt + 1), :],
                        in_=scoreS[:, 64 * tt:64 * (tt + 1)]).then_inc(sph, 16)
            e.dma_start(out=vals_o[128 * tt:128 * (tt + 1), :],
                        in_=valsS[:, 8 * tt:8 * (tt + 1)]).then_inc(sph, 16)
            env.inc(32)
        env.end()

    env.begin()
    @B.vector
    def _(e):
        e.wait_ge(sph, env.w)
        e.tensor_copy(histS[:], psH[0:64, 0:1])
        env.fin(e)
    env.end()
    env.begin()
    @B.sync
    def _(e):
        e.wait_ge(sph, env.w)
        e.dma_start(out=hist_o[:], in_=histS[:]).then_inc(sph, 16)
        env.inc(16)
    env.end()
    env.begin()
    @B.sync
    def _(e):
        e.wait_ge(sph, env.w)
    env.end()
    psX.close()
    esX.close()
    psL.close()
    esL.close()



def build_L2():
    nc = bass.Bass("TRN2", target_bir_lowering=False)
    xT = nc.dram_tensor("xT", [D, T], F32, kind="ExternalInput").ap()
    gateT = nc.dram_tensor("gateT", [D, TP], BF16, kind="ExternalInput").ap()
    convT = nc.dram_tensor("convT", [2112, TP], BF16, kind="ExternalInput").ap()
    dtT = nc.dram_tensor("dtT", [H_A, TP], F32, kind="ExternalInput").ap()
    schk = nc.dram_tensor("schk", [NQ, N_A, D], F32, kind="ExternalInput").ap()
    score_all = nc.dram_tensor("score_all", [NCORES, N_A, D], F32, kind="ExternalInput").ap()
    ltotc_all = nc.dram_tensor("ltotc_all", [NCORES, H_A, NQ], F32, kind="ExternalInput").ap()
    pmrow = nc.dram_tensor("pmrow", [1, NCORES], F32, kind="ExternalInput").ap()
    alog = nc.dram_tensor("alog", [H_A], F32, kind="ExternalInput").ap()
    Wo = nc.dram_tensor("Wo", [D, D], F32, kind="ExternalInput").ap()
    nw = nc.dram_tensor("nw", [D], F32, kind="ExternalInput").ap()
    dex = nc.dram_tensor("dex", [D], F32, kind="ExternalInput").ap()
    wg = nc.dram_tensor("wg", [D, 64], F32, kind="ExternalInput").ap()
    idb = nc.dram_tensor("idb", [128, 128], BF16, kind="ExternalInput").ap()
    idf = nc.dram_tensor("idf", [128, 128], F32, kind="ExternalInput").ap()
    utri = nc.dram_tensor("utri", [128, 128], F32, kind="ExternalInput").ap()

    R1T = nc.dram_tensor("R1T", [D, T], F32, kind="ExternalOutput").ap()
    xlnT_o = nc.dram_tensor("xlnT_o", [D, T], BF16, kind="ExternalOutput").ap()
    xlntok_o = nc.dram_tensor("xlntok_o", [T, D], BF16, kind="ExternalOutput").ap()
    scores_o = nc.dram_tensor("scores_o", [T, 64], F32, kind="ExternalOutput").ap()
    vals_o = nc.dram_tensor("vals_o", [T, 8], F32, kind="ExternalOutput").ap()
    hist_o = nc.dram_tensor("hist_o", [64, 1], F32, kind="ExternalOutput").ap()

    es = contextlib.ExitStack()
    with es:
        env = Env(nc, es)
        sbt, pst = env.sbt, env.pst

        dtt = sbt("dtt", [H_A, TP], F32)
        lbuf = sbt("lbuf", [H_A, Q * NQ], F32)
        hc = sbt("hc", [N_A, D], F32)
        hcb = sbt("hcb", [N_A, D], BF16)
        stmp = sbt("stmp", [N_A, D], F32)
        idb_s = sbt("idb_s", [128, 128], BF16)
        idf_s = sbt("idf_s", [128, 128], F32)
        utri_s = sbt("utri_s", [128, 128], F32)
        onesr = sbt("onesr", [1, 128], F32)
        onesc = sbt("onesc", [128, 1], BF16)
        sA = sbt("sA", [H_A, 1], F32)
        nwS = sbt("nwS", [128, 16], F32)
        dexS = sbt("dexS", [128, 16], F32)
        pmr = sbt("pmr", [1, NCORES], F32)
        lcol = sbt("lcol", [128, H_A], F32)
        dtcol = sbt("dtcol", [128, H_A], F32)
        ltrow = sbt("ltrow", [1, H_A], F32)
        erow = sbt("erow", [1, H_A], F32)
        lflat = sbt("lflat", [1, H_A * Q], F32)
        elflat = sbt("elflat", [1, H_A * Q], F32)
        env.setup_consts([LN_EPS, RMS_EPS, 1.0, -1.0, -0.5, 0.5, 2.0, 0.0,
                          1.0 / D, -32.0, 32.0])

        B = es.enter_context(nc.Block())
        env.B = B
        sph = env.sph

        psG1 = contextlib.ExitStack()
        pstG = lambda n, s, d=F32: psG1.enter_context(nc.psum_tensor(n, s, d))
        psT = pstG("psT", [128, 2048], BF16)
        psLB = pstG("psLB", [128, 1024])
        psCB = pstG("psCB", [128, 128])
        psY = [pstG(f"psY{i}", [128, 512]) for i in range(2)]
        psF = pstG("psF", [128, 512])

        esP = contextlib.ExitStack()
        sbP = lambda n, s, d: esP.enter_context(nc.sbuf_tensor(n, s, d))
        sco = [sbP(f"sco{j}", [N_A, D], F32) for j in range(NCORES)]
        lt8 = sbP("lt8", [H_A, NQ * NCORES], F32)
        lc8 = sbP("lc8", [H_A, NCORES], F32)
        mlt = sbP("mlt", [H_A, NCORES], F32)
        wdec = sbP("wdec", [H_A, NCORES], F32)
        totc = sbP("totc", [H_A, 1], F32)
        wdT = sbP("wdT", [1, NCORES * H_A], F32)

        env.begin()
        @B.sync
        def _(e):
            e.dma_start(out=idb_s[:], in_=idb[:]).then_inc(sph, 16)
            e.dma_start(out=idf_s[:], in_=idf[:]).then_inc(sph, 16)
            e.dma_start(out=utri_s[:], in_=utri[:]).then_inc(sph, 16)
            e.dma_start(out=sA[:, 0], in_=alog[:]).then_inc(sph, 16)
            e.dma_start(out=dtt[:], in_=dtT[:]).then_inc(sph, 16)
            e.dma_start(out=pmr[:], in_=pmrow[:]).then_inc(sph, 16)
            for k in range(16):
                e.dma_start(out=nwS[:, k:k + 1], in_=nw[128 * k:128 * (k + 1)]).then_inc(sph, 16)
                e.dma_start(out=dexS[:, k:k + 1], in_=dex[128 * k:128 * (k + 1)]).then_inc(sph, 16)
            for j in range(NCORES):
                e.dma_start(out=sco[j][:], in_=score_all[j, :, :]).then_inc(sph, 16)
                e.dma_start(out=lt8[:, NQ * j:NQ * (j + 1)], in_=ltotc_all[j, :, :]).then_inc(sph, 16)
            env.inc(16 * (6 + 32 + 16))
        @B.vector
        def _(e):
            e.memset(onesr[:], 1.0)
            e.memset(onesc[:], 1.0)
            e.memset(hc[:], 0.0)
            e.drain()
            env.emit_const_memsets(e)
            env.fin(e)
        env.end()

        env.begin()
        @B.scalar
        def _(e):
            e.wait_ge(sph, env.w)
            e.activation(out=sA[:], in_=sA[:], func=AF.Exp)
            e.drain()
            e.mul(sA[:], sA[:], -1.0)
            env.fin(e)
        env.end()
        env.begin()
        @B.vector
        def _(e):
            e.wait_ge(sph, env.w)
            e.tensor_scalar(stmp[0:H_A, 0:2048], dtt[:, HALO:HALO + 2048], sA[:, 0:1], None, OP.mult)
            env.fin(e)
        env.end()
        env.begin()
        @B.vector
        def _(e):
            e.wait_ge(sph, env.w)
            for c in range(NQ):
                e.tensor_tensor_scan(lbuf[:, Q * c:Q * (c + 1)],
                                     stmp[0:H_A, Q * c:Q * (c + 1)],
                                     stmp[0:H_A, Q * c:Q * (c + 1)],
                                     0.0, OP.add, OP.bypass)
            env.fin(e)
        env.end()

        env.begin()
        @B.vector
        def _(e):
            e.wait_ge(sph, env.w)
            for j in range(NCORES):
                e.tensor_reduce(lc8[:, j:j + 1], lt8[:, NQ * j:NQ * (j + 1)], AX, OP.add)
            env.fin(e)
        @B.tensor
        def _(e):
            e.wait_ge(sph, env.w)
            e.matmul(psF[:, 0:8], onesr[:, :], pmr[0:1, 0:8], start=True, stop=True)
            env.fin(e)
        env.end()
        env.begin()
        @B.vector
        def _(e):
            e.wait_ge(sph, env.w)
            e.tensor_tensor(mlt[:], lc8[:], psF[0:H_A, 0:8], OP.mult)
            e.drain()
            e.tensor_reduce(totc[:, 0:1], mlt[:], AX, OP.add)
            e.drain()
            e.tensor_tensor_scan(wdec[:], mlt[:], mlt[:], 0.0, OP.add, OP.bypass)
            e.drain()
            e.tensor_scalar(wdec[:], wdec[:], totc[:, 0:1], -1.0, OP.subtract, OP.mult)
            env.fin(e)
        env.end()
        env.begin()
        @B.scalar
        def _(e):
            e.wait_ge(sph, env.w)
            e.activation(out=wdec[:], in_=wdec[:], func=AF.Exp)
            env.fin(e)
        env.end()
        env.begin()
        @B.vector
        def _(e):
            e.wait_ge(sph, env.w)
            e.tensor_tensor(wdec[:], wdec[:], psF[0:H_A, 0:8], OP.mult)
            env.fin(e)
        env.end()
        env.begin()
        @B.sync
        def _(e):
            e.wait_ge(sph, env.w)
            for j in range(NCORES):
                e.dma_start(out=wdT[0:1, H_A * j:H_A * (j + 1)], in_=wdec[:, j:j + 1],
                            allow_slow_non_contiguous=True).then_inc(sph, 16)
            env.inc(16 * NCORES)
        env.end()
        for j in range(NCORES):
            for qq in range(4):
                env.begin()
                @B.tensor
                def _(e, j=j, qq=qq):
                    e.wait_ge(sph, env.w)
                    e.matmul(psF[0:N_A, 0:512], onesr[0:1, 0:N_A],
                             rep_cols(wdT[0:1, H_A * j + 8 * qq:H_A * j + 8 * qq + 8], 8, 64),
                             start=True, stop=True)
                    env.fin(e)
                env.end()
                env.begin()
                @B.vector
                def _(e, j=j, qq=qq):
                    e.wait_ge(sph, env.w)
                    e.scalar_tensor_tensor(stmp[0:N_A, 512 * qq:512 * (qq + 1)],
                                           sco[j][:, 512 * qq:512 * (qq + 1)], 1.0,
                                           psF[0:N_A, 0:512], OP.mult, OP.mult)
                    e.drain()
                    e.tensor_add(hc[:, 512 * qq:512 * (qq + 1)], hc[:, 512 * qq:512 * (qq + 1)],
                                 stmp[0:N_A, 512 * qq:512 * (qq + 1)])
                    env.fin(e)
                env.end()
        esP.close()

        esC = contextlib.ExitStack()
        ygated = esC.enter_context(nc.sbuf_tensor("ygated", [128, 16 * T], BF16))
        esC2 = contextlib.ExitStack()
        sbC = lambda n, s, d: esC2.enter_context(nc.sbuf_tensor(n, s, d))
        hbcC = [sbC(f"hbcC{i}", [128, 16 * Q], BF16) for i in range(2)]
        hbB = [sbC(f"hbB{i}", [32, Q], BF16) for i in range(2)]
        hbC2 = [sbC(f"hbC2{i}", [32, Q], BF16) for i in range(2)]
        gTc = [sbC(f"gTc{i}", [128, 16 * Q], BF16) for i in range(2)]
        xtk = sbC("xtk", [128, D], BF16)
        xw2 = sbC("xw2", [128, D], BF16)
        gbuf = sbC("gbuf", [128, H_A * Q], BF16)
        cbS = sbC("cbS", [128, 128], F32)
        dbuf = sbC("dbuf", [128, 1024], F32)
        cwb = sbC("cwb", [N_A, H_A * Q], BF16)
        ydc = sbC("ydc", [128, 2048], F32)
        sgc = sbC("sgc", [128, 2048], F32)
        ysq = sbC("ysq", [128, 2048], BF16)
        rmsb = sbC("rmsb", [1, Q], F32)

        def conv_cols_ap(c):
            cb0 = HALO + Q * c
            return bass.AP(tensor=convT.tensor, offset=convT.offset + cb0,
                           ap=[[TP, 128], [TP * 128, 16], [1, Q]])

        def bc_ap(c, row0):
            cb0 = HALO + Q * c
            return bass.AP(tensor=convT.tensor, offset=convT.offset + row0 * TP + cb0,
                           ap=[[TP, 32], [1, Q]])

        def gate_cols_ap(c):
            cb0 = HALO + Q * c
            return bass.AP(tensor=gateT.tensor, offset=gateT.offset + cb0,
                           ap=[[TP, 128], [TP * 128, 16], [1, Q]])

        env.begin()
        @B.sync
        def _(e):
            e.wait_ge(sph, env.w)
            e.dma_start(out=hbcC[0][:, 0:16 * Q], in_=conv_cols_ap(0)).then_inc(sph, 16)
            e.dma_start(out=gTc[0][:, 0:16 * Q], in_=gate_cols_ap(0)).then_inc(sph, 16)
            e.dma_start(out=hbB[0][:], in_=bc_ap(0, 2048)).then_inc(sph, 16)
            e.dma_start(out=hbC2[0][:], in_=bc_ap(0, 2080)).then_inc(sph, 16)
            env.inc(64)
        env.end()

        for c in range(NQ):
            cb0 = HALO + Q * c
            hb = hbcC[c % 2]
            gb = gTc[c % 2]
            env.begin()
            if c + 1 < NQ:
                @B.sync
                def _(e, c=c):
                    e.wait_ge(sph, env.w)
                    e.dma_start(out=hbcC[(c + 1) % 2][:, 0:16 * Q], in_=conv_cols_ap(c + 1)).then_inc(sph, 16)
                    e.dma_start(out=gTc[(c + 1) % 2][:, 0:16 * Q], in_=gate_cols_ap(c + 1)).then_inc(sph, 16)
                    e.dma_start(out=hbB[(c + 1) % 2][:], in_=bc_ap(c + 1, 2048)).then_inc(sph, 16)
                    e.dma_start(out=hbC2[(c + 1) % 2][:], in_=bc_ap(c + 1, 2080)).then_inc(sph, 16)
                    env.inc(64)
            @B.scalar
            def _(e, c=c):
                e.wait_ge(sph, env.w)
                e.dma_start(out=stmp[:], in_=schk[c, :, :]).then_inc(sph, 16)
                e.dma_start(out=lflat[0:1, :], in_=lbuf[:, Q * c:Q * (c + 1)]).then_inc(sph, 16)
                e.dma_start(out=ltrow[0:1, 0:H_A], in_=lbuf[:, Q * c + Q - 1:Q * c + Q],
                            allow_slow_non_contiguous=True).then_inc(sph, 16)
                env.inc(48)
            @B.tensor
            def _(e, c=c):
                e.wait_ge(sph, env.w)
                e.transpose(psF[:, 0:32], lbuf[0:32, Q * c:Q * (c + 1)], idf_s[0:32, 0:32])
                e.transpose(psF[:, 32:64], dtt[0:32, cb0:cb0 + Q], idf_s[0:32, 0:32])
                env.fin(e)
            @B.vector
            def _(e, c=c):
                e.wait_ge(sph, env.w)
                e.tensor_copy(hcb[:], hc[:])
                env.fin(e)
            env.end()
            env.begin()
            @B.vector
            def _(e, c=c):
                e.wait_ge(sph, env.w)
                e.tensor_scalar(lcol[:, :], psF[:, 0:32], -1.0, None, OP.mult)
                e.drain()
                e.tensor_copy(dtcol[:, :], psF[:, 32:64])
                env.fin(e)
            @B.scalar
            def _(e, c=c):
                e.wait_ge(sph, env.w)
                e.activation(out=elflat[:], in_=lflat[:], func=AF.Exp)
                e.drain()
                e.activation(out=erow[:], in_=ltrow[:], func=AF.Exp)
                env.fin(e)
            @B.tensor
            def _(e, c=c):
                e.wait_ge(sph, env.w)
                e.matmul(psCB[:, :], hbB[c % 2][:], hbC2[c % 2][:], start=True, stop=True)
                for k in range(16):
                    e.transpose(psT[:, 128 * k:128 * (k + 1)], hb[:, Q * k:Q * (k + 1)], idb_s[:])
                env.fin(e)
            env.end()
            env.begin()
            @B.scalar
            def _(e, c=c):
                e.wait_ge(sph, env.w)
                e.copy(xtk[:], psT[:])
                env.fin(e)
            @B.vector
            def _(e, c=c):
                e.wait_ge(sph, env.w)
                e.tensor_copy(cbS[:], psCB[:, :])
                env.fin(e)
            env.end()
            env.begin()
            @B.vector
            def _(e, c=c):
                e.wait_ge(sph, env.w)
                e.tensor_tensor(xw2[:].rearrange("p (h u) -> p h u", u=P_A),
                                xtk[:].rearrange("p (h u) -> p h u", u=P_A),
                                rep_cols(dtcol[:, 0:H_A], H_A, P_A), OP.mult)
                env.fin(e)
            env.end()
            for hg in range(4):
                env.begin()
                @B.tensor
                def _(e, hg=hg):
                    e.wait_ge(sph, env.w)
                    e.matmul(psLB[:, 0:512], onesr[:, :], lflat[0:1, 1024 * hg:1024 * hg + 512],
                             start=True, stop=True)
                    e.matmul(psLB[:, 512:1024], onesr[:, :], lflat[0:1, 1024 * hg + 512:1024 * (hg + 1)],
                             start=True, stop=True)
                    env.fin(e)
                env.end()
                env.begin()
                @B.vector
                def _(e, hg=hg):
                    e.wait_ge(sph, env.w)
                    e.tensor_tensor(dbuf[:].rearrange("p (h t) -> p h t", t=Q),
                                    psLB[:, 0:1024].rearrange("p (h t) -> p h t", t=Q),
                                    rep_cols(lcol[:, 8 * hg:8 * hg + 8], 8, Q), OP.add)
                    e.drain()
                    e.tensor_scalar_min(dbuf[:], dbuf[:], 0.0)
                    env.fin(e)
                env.end()
                env.begin()
                @B.scalar
                def _(e, hg=hg):
                    e.wait_ge(sph, env.w)
                    e.activation(out=dbuf[:], in_=dbuf[:], func=AF.Exp)
                    env.fin(e)
                env.end()
                env.begin()
                @B.vector
                def _(e, hg=hg):
                    e.wait_ge(sph, env.w)
                    e.tensor_tensor(dbuf[:].rearrange("p (h t) -> p h t", t=Q),
                                    dbuf[:].rearrange("p (h t) -> p h t", t=Q),
                                    rep_tile(cbS[:, 0:Q], 8, Q), OP.mult)
                    env.fin(e)
                env.end()
                env.begin()
                @B.vector
                def _(e, hg=hg):
                    e.wait_ge(sph, env.w)
                    e.tensor_tensor(gbuf[:, 1024 * hg:1024 * (hg + 1)].rearrange("p (h t) -> p h t", t=Q),
                                    dbuf[:].rearrange("p (h t) -> p h t", t=Q),
                                    rep_tile(utri_s[:, 0:Q], 8, Q), OP.mult)
                    env.fin(e)
                env.end()
                env.begin()
                @B.tensor
                def _(e, hg=hg):
                    e.wait_ge(sph, env.w)
                    e.matmul(psLB[0:N_A, 0:512], onesr[0:1, 0:N_A],
                             elflat[0:1, 1024 * hg:1024 * hg + 512], start=True, stop=True)
                    e.matmul(psLB[0:N_A, 512:1024], onesr[0:1, 0:N_A],
                             elflat[0:1, 1024 * hg + 512:1024 * (hg + 1)], start=True, stop=True)
                    env.fin(e)
                env.end()
                env.begin()
                @B.vector
                def _(e, hg=hg):
                    e.wait_ge(sph, env.w)
                    e.tensor_tensor(cwb[:, 1024 * hg:1024 * (hg + 1)].rearrange("n (h t) -> n h t", t=Q),
                                    psLB[0:N_A, 0:1024].rearrange("n (h t) -> n h t", t=Q),
                                    rep_tile(hbC2[c % 2][0:N_A, 0:Q], 8, Q), OP.mult)
                    env.fin(e)
                env.end()
            for grp in range(5):
                env.begin()
                if grp < 4:
                    @B.tensor
                    def _(e, grp=grp, c=c):
                        e.wait_ge(sph, env.w)
                        for u in range(4):
                            hp = grp * 4 + u
                            for hh in range(2):
                                h = 2 * hp + hh
                                out = psY[grp % 2][64 * hh:64 * (hh + 1), 128 * u:128 * (u + 1)]
                                e.matmul(out, xw2[:, P_A * h:P_A * (h + 1)],
                                         gbuf[:, Q * h:Q * (h + 1)], start=True, stop=False)
                                e.matmul(out, hcb[:, P_A * h:P_A * (h + 1)],
                                         cwb[:, Q * h:Q * (h + 1)], start=False, stop=True)
                        env.fin(e)
                if grp > 0:
                    @B.vector
                    def _(e, grp=grp, c=c):
                        e.wait_ge(sph, env.w)
                        for u in range(4):
                            hp = (grp - 1) * 4 + u
                            e.scalar_tensor_tensor(ydc[:, 128 * hp:128 * (hp + 1)],
                                                   hb[:, Q * hp:Q * (hp + 1)],
                                                   dexS[:, hp:hp + 1],
                                                   psY[(grp - 1) % 2][:, 128 * u:128 * (u + 1)],
                                                   OP.mult, OP.add)
                        env.fin(e)
                env.end()
            env.begin()
            @B.scalar
            def _(e, c=c):
                e.wait_ge(sph, env.w)
                e.activation(out=sgc[:], in_=gb[:, 0:16 * Q], func=AF.Silu)
                env.fin(e)
            env.end()
            env.begin()
            @B.vector
            def _(e, c=c):
                e.wait_ge(sph, env.w)
                e.tensor_mul(ydc[:], ydc[:], sgc[:])
                env.fin(e)
            env.end()
            env.begin()
            @B.vector
            def _(e, c=c):
                e.wait_ge(sph, env.w)
                e.tensor_copy(bass.AP(tensor=ygated[:].tensor, offset=ygated[:].offset + Q * c,
                                      ap=[[ygated[:].ap[0][0], 128], [T, 16], [1, Q]]),
                              ydc[:].rearrange("p (k t) -> p k t", t=Q))
                env.fin(e)
            @B.scalar
            def _(e, c=c):
                e.wait_ge(sph, env.w)
                e.activation(out=ysq[:], in_=ydc[:], func=AF.Square)
                env.fin(e)
            env.end()
            env.begin()
            @B.tensor
            def _(e, c=c):
                e.wait_ge(sph, env.w)
                for k in range(16):
                    e.matmul(psF[0:1, 0:Q], onesc[:, 0:1], ysq[:, Q * k:Q * (k + 1)],
                             start=(k == 0), stop=(k == 15))
                env.fin(e)
            env.end()
            env.begin()
            @B.vector
            def _(e, c=c):
                e.wait_ge(sph, env.w)
                e.tensor_scalar_mul(rmsb[:], psF[0:1, 0:Q], 1.0 / D)
                env.fin(e)
            env.end()
            env.begin()
            @B.scalar
            def _(e, c=c):
                e.wait_ge(sph, env.w)
                e.activation(out=rmsb[:], in_=rmsb[:], func=AF.Ln, bias=RMS_EPS)
                e.drain()
                e.activation(out=rmsb[:], in_=rmsb[:], func=AF.Exp, scale=-0.5)
                env.fin(e)
            env.end()
            env.begin()
            @B.tensor
            def _(e, c=c):
                e.wait_ge(sph, env.w)
                e.matmul(psF[:, 0:Q], onesr[:, :], rmsb[0:1, 0:Q], start=True, stop=True)
                env.fin(e)
            env.end()
            env.begin()
            @B.vector
            def _(e, c=c):
                e.wait_ge(sph, env.w)
                for k in range(16):
                    e.scalar_tensor_tensor(ygated[:, T * k + Q * c:T * k + Q * (c + 1)],
                                           ygated[:, T * k + Q * c:T * k + Q * (c + 1)],
                                           nwS[:, k:k + 1], psF[:, 0:Q], OP.mult, OP.mult)
                env.fin(e)
            env.end()
            for qq in range(4):
                env.begin()
                @B.tensor
                def _(e, c=c, qq=qq):
                    e.wait_ge(sph, env.w)
                    e.matmul(psF[0:N_A, 0:512], onesr[0:1, 0:N_A],
                             rep_cols(erow[0:1, 8 * qq:8 * qq + 8], 8, 64),
                             start=True, stop=True)
                    env.fin(e)
                env.end()
                env.begin()
                @B.vector
                def _(e, c=c, qq=qq):
                    e.wait_ge(sph, env.w)
                    e.tensor_tensor(hc[:, 512 * qq:512 * (qq + 1)],
                                    hc[:, 512 * qq:512 * (qq + 1)], psF[0:N_A, 0:512], OP.mult)
                    e.drain()
                    e.tensor_add(hc[:, 512 * qq:512 * (qq + 1)],
                                 hc[:, 512 * qq:512 * (qq + 1)],
                                 stmp[0:N_A, 512 * qq:512 * (qq + 1)])
                    env.fin(e)
                env.end()


        psG1.close()
        esC2.close()

        emit_out_ln_scores(env, nc, Wo, lambda k, c0, cwid: ygated[:, T * k + c0:T * k + c0 + cwid],
                           xT, R1T, xlnT_o, xlntok_o, scores_o, vals_o, hist_o,
                           wg, idb_s, onesr, free_rhs=esC.close)
    return nc



def l2_inputs(inputs, core, l1outs, l1all):
    b = core // 4
    pm = np.zeros((1, NCORES), np.float32)
    for j in range(NCORES):
        if j < core and j // 4 == core // 4:
            pm[0, j] = 1.0
    return {
        "xT": l1outs["xTo"],
        "gateT": l1outs["gateT"],
        "convT": l1outs["convT"],
        "dtT": l1outs["dtT"],
        "schk": l1outs["schk"],
        "score_all": np.stack([l1all[j]["score"] for j in range(NCORES)]),
        "ltotc_all": np.stack([l1all[j]["ltotc"] for j in range(NCORES)]),
        "pmrow": pm,
        "alog": np.asarray(inputs["L0_attn_a_A_log"]),
        "Wo": np.asarray(inputs["L0_attn_a_W_out"]),
        "nw": np.asarray(inputs["L0_attn_a_norm_w"]),
        "dex": np.repeat(np.asarray(inputs["L0_attn_a_D_param"]), P_A),
        "wg": np.asarray(inputs["L0_ffn_w_gate"]),
        "idb": np.eye(128, dtype=bfnp),
        "idf": np.eye(128, dtype=np.float32),
        "utri": np.triu(np.ones((128, 128), np.float32)),
    }


CAPP = 256
NE = 64
BIGIDX = 2.0e6

def build_L3():
    nc = bass.Bass("TRN2", target_bir_lowering=False)
    R1T = nc.dram_tensor("R1T", [D, T], F32, kind="ExternalInput").ap()
    xlnTf = nc.dram_tensor("xlnTf", [D, T], BF16, kind="ExternalInput").ap()
    xlntok = nc.dram_tensor("xlntok", [T, D], BF16, kind="ExternalInput").ap()
    scores = nc.dram_tensor("scores", [T, NE], F32, kind="ExternalInput").ap()
    vals = nc.dram_tensor("vals", [T, 8], F32, kind="ExternalInput").ap()
    hist_all = nc.dram_tensor("hist_all", [NCORES, NE], F32, kind="ExternalInput").ap()
    pmrow = nc.dram_tensor("pmrow", [1, NCORES], F32, kind="ExternalInput").ap()
    wsu = nc.dram_tensor("wsu", [D, 1024], F32, kind="ExternalInput").ap()
    wsd = nc.dram_tensor("wsd", [512, D], F32, kind="ExternalInput").ap()
    wup = nc.dram_tensor("wup", [NE, D, 512], F32, kind="ExternalInput").ap()
    wdn = nc.dram_tensor("wdn", [NE, 256, D], F32, kind="ExternalInput").ap()
    idb = nc.dram_tensor("idb", [128, 128], BF16, kind="ExternalInput").ap()
    idf = nc.dram_tensor("idf", [128, 128], F32, kind="ExternalInput").ap()
    eoff = nc.dram_tensor("eoff", [NE, 1], F32, kind="ExternalInput").ap()

    R2T = nc.dram_tensor("R2T", [D, T], F32, kind="ExternalOutput").ap()

    xsel = nc.dram_tensor("xsel", [NE * CAPP, D], BF16, kind="Internal").ap()
    ysel = nc.dram_tensor("ysel", [NE * CAPP, D], BF16, kind="Internal").ap()

    es = contextlib.ExitStack()
    with es:
        env = Env(nc, es)
        sbt, pst = env.sbt, env.pst

        idb_s = sbt("idb_s", [128, 128], BF16)
        idf_s = sbt("idf_s", [128, 128], F32)
        onesr = sbt("onesr", [1, 128], F32)
        onescF = sbt("onescF", [128, 1], F32)
        basec = sbt("basec", [NE, 1], F32)
        eoffS = sbt("eoffS", [NE, 1], F32)
        idxI = sbt("idxI", [128, 16 * 6], I32)
        wn = sbt("wn", [128, 16 * 6], F32)
        rsum = sbt("rsum", [128, 1], F32)
        env.setup_consts([LN_EPS, RMS_EPS, 1.0, -1.0, -0.5, 0.5, 2.0, 0.0,
                          1.0 / D, 1024.0, float(CAPP), BIGIDX, -BIGIDX])
        esR = contextlib.ExitStack()
        sbR = lambda n, s, d: esR.enter_context(nc.sbuf_tensor(n, s, d))
        scoreS = sbR("scoreS", [128, 16 * NE], F32)
        valsS = sbR("valsS", [128, 16 * 8], F32)
        MT = sbR("MT", [NE, T], F32)
        CT = sbR("CT", [NE, T], F32)
        gposT = sbR("gposT", [NE, T], F32)
        hist8 = sbR("hist8", [NCORES, NE], F32)
        pmc = sbR("pmc", [NCORES, 1], F32)
        idxF = sbR("idxF", [128, 16 * 6], F32)
        mbuf = sbR("mbuf", [128, NE], F32)
        mbuf2 = sbR("mbuf2", [128, NE], F32)

        B = es.enter_context(nc.Block())
        env.B = B
        sph = env.sph
        bndreg = nc.gpsimd.alloc_register("bndreg")

        psR = contextlib.ExitStack()
        psM = psR.enter_context(nc.psum_tensor("psM", [128, 128], F32))

        env.begin()
        @B.sync
        def _(e):
            e.dma_start(out=idb_s[:], in_=idb[:]).then_inc(sph, 16)
            e.dma_start(out=idf_s[:], in_=idf[:]).then_inc(sph, 16)
            e.dma_start(out=eoffS[:], in_=eoff[:]).then_inc(sph, 16)
            e.dma_start(out=hist8[:], in_=hist_all[:]).then_inc(sph, 16)
            e.dma_start(out=pmc[:, 0:1], in_=pmrow[0:1, :], allow_slow_non_contiguous=True).then_inc(sph, 16)
            for tt in range(16):
                e.dma_start(out=scoreS[:, NE * tt:NE * (tt + 1)],
                            in_=scores[128 * tt:128 * (tt + 1), :]).then_inc(sph, 16)
                e.dma_start(out=valsS[:, 8 * tt:8 * (tt + 1)],
                            in_=vals[128 * tt:128 * (tt + 1), :]).then_inc(sph, 16)
            env.inc(16 * 37)
        @B.vector
        def _(e):
            e.memset(onesr[:], 1.0)
            e.memset(onescF[:], 1.0)
            e.drain()
            env.emit_const_memsets(e)
            env.fin(e)
        @B.gpsimd
        def _(e):
            e.reg_mov(bndreg, NE * CAPP - 1)
            env.fin(e)
        env.end()

        env.begin()
        @B.tensor
        def _(e):
            e.wait_ge(sph, env.w)
            e.matmul(psM[0:NE, 0:1], hist8[:], pmc[:], start=True, stop=True)
            env.fin(e)
        env.end()
        env.begin()
        @B.vector
        def _(e):
            e.wait_ge(sph, env.w)
            e.tensor_copy(basec[:], psM[0:NE, 0:1])
            env.fin(e)
        env.end()

        for tt in range(16):
            env.begin()
            @B.vector
            def _(e, tt=tt):
                e.wait_ge(sph, env.w)
                e.tensor_scalar(mbuf[:], scoreS[:, NE * tt:NE * (tt + 1)],
                                valsS[:, 8 * tt + 5:8 * tt + 6], None, OP.is_ge)
                env.fin(e)
            env.end()
            env.begin()
            @B.tensor
            def _(e, tt=tt):
                e.wait_ge(sph, env.w)
                e.transpose(psM[:, 0:128], mbuf[:, 0:64], idf_s[:]) if False else None
                e.transpose(psM[0:64, 0:128], mbuf[:, 0:64], idf_s[:])
                env.fin(e)
            env.end()
            env.begin()
            @B.vector
            def _(e, tt=tt):
                e.wait_ge(sph, env.w)
                e.tensor_copy(MT[:, 128 * tt:128 * (tt + 1)], psM[0:64, 0:128])
                env.fin(e)
            env.end()

        env.begin()
        @B.vector
        def _(e):
            e.wait_ge(sph, env.w)
            e.tensor_tensor_scan(CT[:], MT[:], MT[:], 0.0, OP.add, OP.bypass)
            e.drain()
            e.tensor_sub(CT[:], CT[:], MT[:])
            e.drain()
            e.tensor_scalar(gposT[:], CT[:], basec[:, 0:1], None, OP.add)
            env.fin(e)
        env.end()
        env.begin()
        @B.vector
        def _(e):
            e.wait_ge(sph, env.w)
            e.tensor_scalar(gposT[:], gposT[:], 1024.0, None, OP.is_lt)
            e.drain()
            e.tensor_mul(gposT[:], gposT[:], MT[:])
            e.drain()
            e.tensor_scalar(MT[:], CT[:], float(CAPP), None, OP.is_lt)
            e.drain()
            e.tensor_mul(gposT[:], gposT[:], MT[:])
            env.fin(e)
        env.end()
        env.begin()
        @B.vector
        def _(e):
            e.wait_ge(sph, env.w)
            e.tensor_scalar(CT[:], CT[:], eoffS[:, 0:1], -BIGIDX, OP.add, OP.add) if False else None
            e.tensor_scalar(CT[:], CT[:], eoffS[:, 0:1], None, OP.add)
            e.drain()
            e.tensor_scalar_add(CT[:], CT[:], -BIGIDX)
            e.drain()
            e.tensor_mul(CT[:], CT[:], gposT[:])
            e.drain()
            e.tensor_scalar_add(CT[:], CT[:], BIGIDX)
            env.fin(e)
        env.end()

        for tt in range(16):
            env.begin()
            @B.tensor
            def _(e, tt=tt):
                e.wait_ge(sph, env.w)
                e.transpose(psM[0:128, 0:64], CT[0:64, 128 * tt:128 * (tt + 1)], idf_s[0:64, 0:64])
                env.fin(e)
            env.end()
            env.begin()
            @B.vector
            def _(e, tt=tt):
                e.wait_ge(sph, env.w)
                e.reduce_sum(rsum[:], valsS[:, 8 * tt:8 * tt + 6], AX)
                env.fin(e)
            env.end()
            env.begin()
            @B.scalar
            def _(e, tt=tt):
                e.wait_ge(sph, env.w)
                e.activation(out=rsum[:], in_=rsum[:], func=AF.Ln)
                e.drain()
                e.activation(out=rsum[:], in_=rsum[:], func=AF.Exp, scale=-1.0)
                env.fin(e)
            env.end()
            env.begin()
            @B.vector
            def _(e, tt=tt):
                e.wait_ge(sph, env.w)
                e.tensor_scalar(wn[:, 6 * tt:6 * tt + 6], valsS[:, 8 * tt:8 * tt + 6],
                                rsum[:, 0:1], None, OP.mult)
                env.fin(e)
            env.end()
            for j in range(6):
                env.begin()
                @B.vector
                def _(e, tt=tt, j=j):
                    e.wait_ge(sph, env.w)
                    e.tensor_scalar(mbuf2[:], scoreS[:, NE * tt:NE * (tt + 1)],
                                    valsS[:, 8 * tt + j:8 * tt + j + 1], None, OP.is_equal)
                    e.drain()
                    e.tensor_tensor(mbuf2[:], mbuf2[:], psM[0:128, 0:64], OP.mult)
                    e.drain()
                    e.reduce_sum(idxF[:, 6 * tt + j:6 * tt + j + 1], mbuf2[:], AX)
                    env.fin(e)
                env.end()
        env.begin()
        @B.vector
        def _(e):
            e.wait_ge(sph, env.w)
            e.tensor_copy(idxI[:], idxF[:])
            env.fin(e)
        env.end()

        esR.close()
        esD = contextlib.ExitStack()
        xtokS = [esD.enter_context(nc.sbuf_tensor(f"xtokS{i}", [128, D], BF16)) for i in range(2)]
        env.begin()
        @B.sync
        def _(e):
            e.wait_ge(sph, env.w)
            e.dma_start(out=xtokS[0][:], in_=xlntok[0:128, :]).then_inc(sph, 16)
            env.inc(16)
        env.end()
        for tt in range(16):
            env.begin()
            if tt + 1 < 16:
                @B.sync
                def _(e, tt=tt):
                    e.wait_ge(sph, env.w)
                    e.dma_start(out=xtokS[(tt + 1) % 2][:],
                                in_=xlntok[128 * (tt + 1):128 * (tt + 2), :]).then_inc(sph, 16)
                    env.inc(16)
            @B.gpsimd
            def _(e, tt=tt):
                e.wait_ge(sph, env.w)
                for j in range(6):
                    e.indirect_dma_start(
                        out=xsel[:], out_offset=bass.IndirectOffsetOnAxis(idxI[:, 6 * tt + j:6 * tt + j + 1], 0),
                        in_=xtokS[tt % 2][:], in_offset=None,
                        bounds_check=bndreg, oob_is_err=False,
                    ).then_inc(sph, 16)
                env.inc(16 * 6)
            env.end()
        esD.close()

        esS = contextlib.ExitStack()
        sbS = lambda n, s, d: esS.enter_context(nc.sbuf_tensor(n, s, d))
        psS = contextlib.ExitStack()
        psum4 = [psS.enter_context(nc.psum_tensor(f"pss{i}", [128, 512], F32)) for i in range(4)]
        hshT = [sbS(f"hshT{k}", [128, T], BF16) for k in range(4)]
        esS2 = contextlib.ExitStack()
        sbS2 = lambda n, s, d: esS2.enter_context(nc.sbuf_tensor(n, s, d))
        suT = [sbS2(f"suT{k}", [128, T], F32) for k in range(4)]
        esS3 = contextlib.ExitStack()
        sbS3 = lambda n, s, d: esS3.enter_context(nc.sbuf_tensor(n, s, d))
        xlnF = [sbS3(f"xlnF{k}", [128, T], BF16) for k in range(16)]
        wbufS = [sbS3(f"wbufS{i}", [128, 16 * 256], BF16) for i in range(2)]

        env.begin()
        @B.sync
        def _(e):
            e.wait_ge(sph, env.w)
            for k in range(16):
                e.dma_start(out=xlnF[k][:], in_=xlnTf[128 * k:128 * (k + 1), :]).then_inc(sph, 16)
            env.inc(16 * 16)
        env.end()

        SL4 = [(0, 512), (512, 512), (1024, 512), (1536, 512)]

        def evac_su(e, ot, c0, cwid, ps, it):
            if ot < 4:
                e.tensor_copy(suT[ot][:, c0:c0 + cwid], ps[:, 0:cwid])
            else:
                e.tensor_copy(hshT[ot - 4][:, c0:c0 + cwid], ps[:, 0:cwid])

        def su_rhs(k, c0, cwid):
            return xlnF[k][:, c0:c0 + cwid]

        emit_proj(env, wsu, D, 8, None, SL4, psum4, wbufS, evac_su, rhs_fn=su_rhs)
        esS3.close()

        for k in range(4):
            env.begin()
            @B.scalar
            def _(e, k=k):
                e.wait_ge(sph, env.w)
                env.fin(e)
            env.end()
        for k in range(4):
            for c0, cwid in SL4:
                env.begin()
                @B.scalar
                def _(e, k=k, c0=c0, cwid=cwid):
                    e.wait_ge(sph, env.w)
                    e.activation(out=psum4[0][:, 0:cwid], in_=hshT[k][:, c0:c0 + cwid], func=AF.Silu)
                    env.fin(e)
                env.end()
                env.begin()
                @B.vector
                def _(e, k=k, c0=c0, cwid=cwid):
                    e.wait_ge(sph, env.w)
                    e.tensor_tensor(hshT[k][:, c0:c0 + cwid], psum4[0][:, 0:cwid],
                                    suT[k][:, c0:c0 + cwid], OP.mult)
                    env.fin(e)
                env.end()

        esS2.close()
        sharedT = [sbS(f"sharedT{k}", [128, T], BF16) for k in range(16)]
        wbufS2 = [sbS(f"wbufS2_{i}", [128, 4 * 256], BF16) for i in range(2)]

        def evac_sd(e, ot, c0, cwid, ps, it):
            e.tensor_copy(sharedT[ot][:, c0:c0 + cwid], ps[:, 0:cwid])

        def sd_rhs(k, c0, cwid):
            return hshT[k][:, c0:c0 + cwid]

        emit_proj(env, wsd, 512, 16, None, SL4, psum4, wbufS2, evac_sd, rhs_fn=sd_rhs)
        psS.close()

        esE = contextlib.ExitStack()
        sbE = lambda n, s, d: esE.enter_context(nc.sbuf_tensor(n, s, d))
        psE = contextlib.ExitStack()
        psU = [psE.enter_context(nc.psum_tensor(f"psU{i}", [128, 512], F32)) for i in range(2)]
        psDn = [psE.enter_context(nc.psum_tensor(f"psDn{i}", [128, 512], F32)) for i in range(4)]
        psHh = psE.enter_context(nc.psum_tensor("psHh", [128, 512], BF16))
        xsT = [sbE(f"xsT{i}", [128, 16 * 256], BF16) for i in range(2)]
        wupS = [sbE(f"wupS{i}", [128, 16 * 512], BF16) for i in range(2)]
        wdnS = [sbE(f"wdnS{i}", [128, 2 * 2048], BF16) for i in range(2)]
        sgl = sbE("sgl", [128, 512], F32)
        hbuf = sbE("hbuf", [128, 512], BF16)
        hT = sbE("hT", [128, 512], BF16)
        ydst = [sbE(f"ydst{i}", [128, 512], BF16) for i in range(8)]
        sUP = es.enter_context(nc.semaphore("sUP"))
        sSI = es.enter_context(nc.semaphore("sSI"))
        sGL = es.enter_context(nc.semaphore("sGL"))
        sTR = es.enter_context(nc.semaphore("sTR"))
        sHC = es.enter_context(nc.semaphore("sHC"))
        sD1 = es.enter_context(nc.semaphore("sD1"))
        sE1 = es.enter_context(nc.semaphore("sE1"))
        sD2 = es.enter_context(nc.semaphore("sD2"))
        sE2 = es.enter_context(nc.semaphore("sE2"))
        sYD = es.enter_context(nc.semaphore("sYD"))
        sWL = es.enter_context(nc.semaphore("sWL"))
        sXL = es.enter_context(nc.semaphore("sXL"))

        def wup_ap(ei):
            return bass.AP(tensor=wup.tensor, offset=wup.offset + ei * D * 512,
                           ap=[[512, 128], [512 * 128, 16], [1, 512]])

        def wdn_ap(ei):
            return bass.AP(tensor=wdn.tensor, offset=wdn.offset + ei * 256 * D,
                           ap=[[D, 128], [D * 128, 2], [1, 2048]])

        env.begin()
        @B.gpsimd
        def _(e):
            e.wait_ge(sph, env.w)
            e.dma_start(out=wupS[0][:], in_=wup_ap(0)).then_inc(sWL, 16)
            e.dma_start(out=wdnS[0][:], in_=wdn_ap(0)).then_inc(sWL, 16)
        @B.sync
        def _(e):
            e.wait_ge(sph, env.w)
            for kk in range(16):
                e.dma_start(out=xsT[0][:, 256 * kk:256 * (kk + 1)],
                            in_=xsel[0:CAPP, 128 * kk:128 * (kk + 1)], transpose=True).then_inc(sXL, 16)
        env.end()

        for ei in range(NE):
            if ei % 8 == 0:
                env.begin()
            @B.gpsimd
            def _(e, ei=ei):
                if ei % 8 == 0:
                    e.wait_ge(sph, env.w)
                if ei + 1 < NE:
                    if ei >= 1:
                        e.wait_ge(sUP, ei)
                    e.dma_start(out=wupS[(ei + 1) % 2][:], in_=wup_ap(ei + 1)).then_inc(sWL, 16)
                    if ei >= 1:
                        e.wait_ge(sD2, ei)
                    e.dma_start(out=wdnS[(ei + 1) % 2][:], in_=wdn_ap(ei + 1)).then_inc(sWL, 16)
            @B.sync
            def _(e, ei=ei):
                if ei % 8 == 0:
                    e.wait_ge(sph, env.w)
                if ei + 1 < NE:
                    if ei >= 1:
                        e.wait_ge(sUP, ei)
                    for kk in range(16):
                        e.dma_start(out=xsT[(ei + 1) % 2][:, 256 * kk:256 * (kk + 1)],
                                    in_=xsel[CAPP * (ei + 1):CAPP * (ei + 2), 128 * kk:128 * (kk + 1)],
                                    transpose=True).then_inc(sXL, 16)
                if ei >= 1:
                    ej = ei - 1
                    e.wait_ge(sE2, ej + 1)
                    for og in range(4):
                        for sh in range(2):
                            e.dma_start(out=ysel[CAPP * ej + 128 * sh:CAPP * ej + 128 * (sh + 1),
                                                 512 * og:512 * (og + 1)],
                                        in_=ydst[2 * og + sh][:]).then_inc(sYD, 16)
            @B.scalar
            def _(e, ei=ei):
                if ei % 8 == 0:
                    e.wait_ge(sph, env.w)
                e.wait_ge(sUP, ei + 1)
                e.activation(out=sgl[:, 0:256], in_=psU[0][:, 0:256], func=AF.Silu)
                e.activation(out=sgl[:, 256:512], in_=psU[1][:, 0:256], func=AF.Silu)
                e.drain()
                e.nop().then_inc(sSI, 1)
                e.wait_ge(sTR, ei + 1)
                for sh in range(2):
                    for kk in range(2):
                        e.copy(hT[:, 256 * kk + 128 * sh:256 * kk + 128 * (sh + 1)],
                               psHh[:, 128 * (2 * sh + kk):128 * (2 * sh + kk + 1)])
                e.drain()
                e.nop().then_inc(sHC, 1)
            @B.tensor
            def _(e, ei=ei):
                if ei % 8 == 0:
                    e.wait_ge(sph, env.w)
                if ei >= 1:
                    e.wait_ge(sGL, ei)
                e.wait_ge(sWL, 32 * ei + 16)
                e.wait_ge(sXL, 256 * (ei + 1))
                for sh in range(2):
                    for kk in range(16):
                        ins = e.matmul(psU[sh][:, :],
                                       xsT[ei % 2][:, 256 * kk + 128 * sh:256 * kk + 128 * (sh + 1)],
                                       wupS[ei % 2][:, 512 * kk:512 * (kk + 1)],
                                       start=(kk == 0), stop=(kk == 15))
                ins.then_inc(sUP, 1)
                e.wait_ge(sGL, ei + 1)
                for sh in range(2):
                    for kk in range(2):
                        ins = e.transpose(psHh[:, 128 * (2 * sh + kk):128 * (2 * sh + kk + 1)],
                                          hbuf[:, 256 * sh + 128 * kk:256 * sh + 128 * (kk + 1)], idb_s[:])
                ins.then_inc(sTR, 1)
                e.wait_ge(sHC, ei + 1)
                e.wait_ge(sWL, 32 * (ei + 1))
                if ei >= 1:
                    e.wait_ge(sE1, ei)
                    e.wait_ge(sE2, ei)
                for og in range(2):
                    for sh in range(2):
                        for kk in range(2):
                            ins = e.matmul(psDn[2 * og + sh][:, :],
                                           hT[:, 256 * kk + 128 * sh:256 * kk + 128 * (sh + 1)],
                                           wdnS[ei % 2][:, 2048 * kk + 512 * og:2048 * kk + 512 * (og + 1)],
                                           start=(kk == 0), stop=(kk == 1))
                ins.then_inc(sD1, 1)
                e.wait_ge(sE1, ei + 1)
                for og in range(2):
                    for sh in range(2):
                        for kk in range(2):
                            ins = e.matmul(psDn[2 * og + sh][:, :],
                                           hT[:, 256 * kk + 128 * sh:256 * kk + 128 * (sh + 1)],
                                           wdnS[ei % 2][:, 2048 * kk + 512 * (og + 2):2048 * kk + 512 * (og + 3)],
                                           start=(kk == 0), stop=(kk == 1))
                ins.then_inc(sD2, 1)
            @B.vector
            def _(e, ei=ei):
                if ei % 8 == 0:
                    e.wait_ge(sph, env.w)
                e.wait_ge(sSI, ei + 1)
                e.tensor_tensor(hbuf[:, 0:256], sgl[:, 0:256], psU[0][:, 256:512], OP.mult)
                e.tensor_tensor(hbuf[:, 256:512], sgl[:, 256:512], psU[1][:, 256:512], OP.mult)
                e.drain()
                e.nop().then_inc(sGL, 1)
                e.wait_ge(sD1, ei + 1)
                if ei >= 1:
                    e.wait_ge(sYD, 128 * ei)
                for u in range(4):
                    e.tensor_copy(ydst[u][:], psDn[u][:, :])
                e.drain()
                e.nop().then_inc(sE1, 1)
                e.wait_ge(sD2, ei + 1)
                for u in range(4):
                    e.tensor_copy(ydst[4 + u][:], psDn[u][:, :])
                e.drain()
                e.nop().then_inc(sE2, 1)
            if ei % 8 == 7:
                @B.vector
                def _(e, ei=ei):
                    env.fin(e)
                @B.tensor
                def _(e, ei=ei):
                    env.fin(e)
                @B.scalar
                def _(e, ei=ei):
                    env.fin(e)
                @B.sync
                def _(e, ei=ei):
                    e.wait_ge(sE2, ei + 1)
                    e.nop().then_inc(sph, 1)
                    env.inc(1)
                env.end()
        env.begin()
        @B.sync
        def _(e):
            e.wait_ge(sph, env.w)
            ej = NE - 1
            for og in range(4):
                for sh in range(2):
                    e.dma_start(out=ysel[CAPP * ej + 128 * sh:CAPP * ej + 128 * (sh + 1),
                                         512 * og:512 * (og + 1)],
                                in_=ydst[2 * og + sh][:]).then_inc(sph, 16)
            env.inc(128)
        env.end()
        psE.close()
        esE.close()

        esM = contextlib.ExitStack()
        sbM = lambda n, s, d: esM.enter_context(nc.sbuf_tensor(n, s, d))
        psM2 = contextlib.ExitStack()
        psFc = psM2.enter_context(nc.psum_tensor("psFc", [128, 512], F32))
        gb = [sbM(f"gb{i}", [128, D], BF16) for i in range(2)]
        racc = sbM("racc", [128, D], F32)
        r1c = sbM("r1c", [128, 16 * 128], F32)
        r2stg = sbM("r2stg", [128, 512], F32)

        for tt in range(16):
            env.begin()
            @B.vector
            def _(e, tt=tt):
                e.wait_ge(sph, env.w)
                e.memset(gb[0][:], 0.0)
                e.memset(gb[1][:], 0.0)
                e.memset(racc[:], 0.0)
                env.fin(e)
            @B.sync
            def _(e, tt=tt):
                e.wait_ge(sph, env.w)
                src2 = bass.AP(tensor=R1T.tensor, offset=R1T.offset + 128 * tt,
                               ap=[[T, 128], [T * 128, 16], [1, 128]])
                e.dma_start(out=r1c[:], in_=src2).then_inc(sph, 16)
                env.inc(16)
            env.end()
            env.begin()
            @B.gpsimd
            def _(e, tt=tt):
                e.wait_ge(sph, env.w)
                e.indirect_dma_start(
                    out=gb[0][:], out_offset=None,
                    in_=ysel[:], in_offset=bass.IndirectOffsetOnAxis(idxI[:, 6 * tt:6 * tt + 1], 0),
                    bounds_check=bndreg, oob_is_err=False,
                ).then_inc(sph, 16)
                env.inc(16)
            env.end()
            for j in range(6):
                env.begin()
                if j + 1 < 6:
                    @B.gpsimd
                    def _(e, tt=tt, j=j):
                        e.wait_ge(sph, env.w)
                        e.indirect_dma_start(
                            out=gb[(j + 1) % 2][:], out_offset=None,
                            in_=ysel[:], in_offset=bass.IndirectOffsetOnAxis(
                                idxI[:, 6 * tt + j + 1:6 * tt + j + 2], 0),
                            bounds_check=bndreg, oob_is_err=False,
                        ).then_inc(sph, 16)
                        env.inc(16)
                @B.vector
                def _(e, tt=tt, j=j):
                    e.wait_ge(sph, env.w)
                    e.scalar_tensor_tensor(racc[:], gb[j % 2][:], wn[:, 6 * tt + j:6 * tt + j + 1],
                                           racc[:], OP.mult, OP.add)
                    if j + 2 < 6:
                        e.drain()
                        e.memset(gb[j % 2][:], 0.0)
                    env.fin(e)
                env.end()
            for grp in range(4):
                env.begin()
                @B.tensor
                def _(e, tt=tt, grp=grp):
                    e.wait_ge(sph, env.w)
                    for u in range(4):
                        ot = grp * 4 + u
                        e.transpose(psFc[:, 128 * u:128 * (u + 1)],
                                    racc[:, 128 * ot:128 * (ot + 1)], idf_s[:])
                    env.fin(e)
                env.end()
                env.begin()
                @B.vector
                def _(e, tt=tt, grp=grp):
                    e.wait_ge(sph, env.w)
                    for u in range(4):
                        ot = grp * 4 + u
                        e.tensor_tensor(r2stg[:, 128 * u:128 * (u + 1)], psFc[:, 128 * u:128 * (u + 1)],
                                        r1c[:, 128 * ot:128 * (ot + 1)], OP.add)
                        e.drain()
                        e.tensor_tensor(r2stg[:, 128 * u:128 * (u + 1)], r2stg[:, 128 * u:128 * (u + 1)],
                                        sharedT[ot][:, 128 * tt:128 * (tt + 1)], OP.add)
                        if u < 3:
                            e.drain()
                    env.fin(e)
                env.end()
                env.begin()
                @B.sync
                def _(e, tt=tt, grp=grp):
                    e.wait_ge(sph, env.w)
                    for u in range(4):
                        ot = grp * 4 + u
                        e.dma_start(out=R2T[128 * ot:128 * (ot + 1), 128 * tt:128 * (tt + 1)],
                                    in_=r2stg[:, 128 * u:128 * (u + 1)]).then_inc(sph, 16)
                    env.inc(64)
                env.end()

        env.begin()
        @B.sync
        def _(e):
            e.wait_ge(sph, env.w)
        env.end()
        psM2.close()
        esM.close()
        esS.close()
        psR.close()
    return nc


def l3_inputs(inputs, core, l2outs, l2all):
    pm = np.zeros((1, NCORES), np.float32)
    for j in range(NCORES):
        if j < core:
            pm[0, j] = 1.0
    return {
        "R1T": l2outs["R1T"],
        "xlnTf": l2outs["xlnT_o"],
        "xlntok": l2outs["xlntok_o"],
        "scores": l2outs["scores_o"],
        "vals": l2outs["vals_o"],
        "hist_all": np.stack([l2all[j]["hist_o"][:, 0] for j in range(NCORES)]),
        "pmrow": pm,
        "wsu": np.asarray(inputs["L0_ffn_w_shared_up"]),
        "wsd": np.asarray(inputs["L0_ffn_w_shared_down"]),
        "wup": np.asarray(inputs["L0_ffn_w_up"]),
        "wdn": np.asarray(inputs["L0_ffn_w_down"]),
        "idb": np.eye(128, dtype=bfnp),
        "idf": np.eye(128, dtype=np.float32),
        "eoff": (np.arange(NE, dtype=np.float32) * CAPP).reshape(NE, 1),
    }


H_B, DK_B, DV_B, M_B = 16, 64, 128, 64
SCL_B = DK_B ** -0.5

def build_L3b():
    nc = bass.Bass("TRN2", target_bir_lowering=False)
    R2T = nc.dram_tensor("R2T", [D, T], F32, kind="ExternalInput").ap()
    Wqks = nc.dram_tensor("Wqks", [D, 3072], F32, kind="ExternalInput").ap()
    Wkvg = nc.dram_tensor("Wkvg", [D, 5120], F32, kind="ExternalInput").ap()
    idb = nc.dram_tensor("idb", [128, 128], BF16, kind="ExternalInput").ap()
    idf = nc.dram_tensor("idf", [128, 128], F32, kind="ExternalInput").ap()

    qT_o = nc.dram_tensor("qT_o", [1024, T], F32, kind="ExternalOutput").ap()
    kT_o = nc.dram_tensor("kT_o", [1024, T], F32, kind="ExternalOutput").ap()
    sT_o = nc.dram_tensor("sT_o", [1024, T], F32, kind="ExternalOutput").ap()
    ktok_o = nc.dram_tensor("ktok_o", [T, 1024], F32, kind="ExternalOutput").ap()
    vtok_o = nc.dram_tensor("vtok_o", [T, 2048], F32, kind="ExternalOutput").ap()
    gtok_o = nc.dram_tensor("gtok_o", [T, 2048], F32, kind="ExternalOutput").ap()
    hk_o = nc.dram_tensor("hk_o", [128, 8 * DK_B], F32, kind="ExternalOutput").ap()
    hv_o = nc.dram_tensor("hv_o", [128, 8 * DV_B], F32, kind="ExternalOutput").ap()
    zc_o = nc.dram_tensor("zc_o", [128, 8], F32, kind="ExternalOutput").ap()

    es = contextlib.ExitStack()
    with es:
        env = Env(nc, es)
        sbt, pst = env.sbt, env.pst
        idb_s = sbt("idb_s", [128, 128], BF16)
        idf_s = sbt("idf_s", [128, 128], F32)
        onesr = sbt("onesr", [1, 128], F32)
        onescF = sbt("onescF", [128, 1], F32)
        hkS = sbt("hkS", [128, 8 * DK_B], F32)
        hvS = sbt("hvS", [128, 8 * DV_B], F32)
        zacc = sbt("zacc", [128, 8], F32)
        rold = sbt("rold", [128, 8], F32)
        rnew = sbt("rnew", [128, 8], F32)
        resc = sbt("resc", [128, 8], F32)
        nrn = sbt("nrn", [128, 8], F32)
        mrow = sbt("mrow", [1, T], F32)
        srow = sbt("srow", [1, T], F32)
        rrow = sbt("rrow", [1, T], F32)
        env.setup_consts([LN_EPS, 1.0, -1.0, -0.5, 0.5, 0.0, 1.0 / D, -32.0, 32.0])

        B = es.enter_context(nc.Block())
        env.B = B
        sph = env.sph

        env.begin()
        @B.sync
        def _(e):
            e.dma_start(out=idb_s[:], in_=idb[:]).then_inc(sph, 16)
            e.dma_start(out=idf_s[:], in_=idf[:]).then_inc(sph, 16)
            env.inc(32)
        @B.vector
        def _(e):
            e.memset(onesr[:], 1.0)
            e.memset(onescF[:], 1.0)
            e.memset(hkS[:], 0.0)
            e.memset(hvS[:], 0.0)
            e.memset(zacc[:], 0.0)
            e.memset(rold[:], -1.0e30)
            e.drain()
            env.emit_const_memsets(e)
            env.fin(e)
        env.end()

        esL = contextlib.ExitStack()
        sbL = lambda n, s, d: esL.enter_context(nc.sbuf_tensor(n, s, d))
        psL = contextlib.ExitStack()
        psS2a = psL.enter_context(nc.psum_tensor("psS2a", [1, 512], F32))
        psS2b = psL.enter_context(nc.psum_tensor("psS2b", [1, 512], F32))
        r1blk = [sbL(f"r1blk{i}", [128, 512], F32) for i in range(3)]
        sqblk = [sbL(f"sqblk{i}", [128, 512], F32) for i in range(2)]
        blocks = [(s, ot) for s in range(4) for ot in range(16)]
        for i in range(len(blocks) + 2):
            env.begin()
            if i < len(blocks):
                s, ot = blocks[i]
                @B.sync
                def _(e, s=s, ot=ot, i=i):
                    e.wait_ge(sph, env.w)
                    e.dma_start(out=r1blk[i % 3][:], in_=R2T[128 * ot:128 * (ot + 1),
                                                           512 * s:512 * (s + 1)]).then_inc(sph, 16)
                    env.inc(16)
            if 1 <= i <= len(blocks):
                s, ot = blocks[i - 1]
                @B.scalar
                def _(e, s=s, ot=ot, i=i):
                    e.wait_ge(sph, env.w)
                    e.activation(out=sqblk[(i - 1) % 2][:], in_=r1blk[(i - 1) % 3][:], func=AF.Square)
                    env.fin(e)
            if 2 <= i <= len(blocks) + 1:
                s, ot = blocks[i - 2]
                @B.tensor
                def _(e, s=s, ot=ot, i=i):
                    e.wait_ge(sph, env.w)
                    e.matmul(psS2a[0:1, :], onescF[:, 0:1], r1blk[(i - 2) % 3][:],
                             start=(ot == 0), stop=(ot == 15))
                    e.matmul(psS2b[0:1, :], onescF[:, 0:1], sqblk[(i - 2) % 2][:],
                             start=(ot == 0), stop=(ot == 15))
                    env.fin(e)
                if ot == 15:
                    env.end()
                    env.begin()
                    @B.vector
                    def _(e, s=s):
                        e.wait_ge(sph, env.w)
                        e.tensor_scalar_mul(mrow[0:1, 512 * s:512 * (s + 1)], psS2a[0:1, :], 1.0 / D)
                        e.drain()
                        e.tensor_scalar_mul(srow[0:1, 512 * s:512 * (s + 1)], psS2b[0:1, :], 1.0 / D)
                        env.fin(e)
            env.end()
        env.begin()
        @B.vector
        def _(e):
            e.wait_ge(sph, env.w)
            e.tensor_mul(rrow[:], mrow[:], mrow[:])
            e.drain()
            e.tensor_sub(rrow[:], srow[:], rrow[:])
            env.fin(e)
        env.end()
        env.begin()
        @B.scalar
        def _(e):
            e.wait_ge(sph, env.w)
            e.activation(out=rrow[:], in_=rrow[:], func=AF.Ln, bias=LN_EPS)
            e.drain()
            e.activation(out=rrow[:], in_=rrow[:], func=AF.Exp, scale=-0.5)
            env.fin(e)
        env.end()
        psL.close()
        esL.close()

        esX = contextlib.ExitStack()
        sbX = lambda n, s, d: esX.enter_context(nc.sbuf_tensor(n, s, d))
        psX = contextlib.ExitStack()
        psB1 = psX.enter_context(nc.psum_tensor("psB1", [128, 512], F32))
        xln3T = [sbX(f"xln3T{k}", [128, T], BF16) for k in range(16)]
        mbc = sbX("mbc", [128, T], F32)
        rbc = sbX("rbc", [128, T], F32)
        xblk = [sbX(f"xblk{i}", [128, 512], F32) for i in range(2)]
        for s in range(4):
            for which, row, dst in ((0, mrow, mbc), (1, rrow, rbc)):
                env.begin()
                @B.tensor
                def _(e, s=s, row=row):
                    e.wait_ge(sph, env.w)
                    e.matmul(psB1[:, :], onesr[:, :], row[0:1, 512 * s:512 * (s + 1)],
                             start=True, stop=True)
                    env.fin(e)
                env.end()
                env.begin()
                @B.vector
                def _(e, s=s, dst=dst):
                    e.wait_ge(sph, env.w)
                    e.tensor_copy(dst[:, 512 * s:512 * (s + 1)], psB1[:, :])
                    env.fin(e)
                env.end()
        for i in range(len(blocks) + 1):
            env.begin()
            if i < len(blocks):
                s, ot = blocks[i]
                @B.sync
                def _(e, s=s, ot=ot, i=i):
                    e.wait_ge(sph, env.w)
                    e.dma_start(out=xblk[i % 2][:], in_=R2T[128 * ot:128 * (ot + 1),
                                                          512 * s:512 * (s + 1)]).then_inc(sph, 16)
                    env.inc(16)
            if i >= 1:
                s, ot = blocks[i - 1]
                @B.vector
                def _(e, s=s, ot=ot, i=i):
                    e.wait_ge(sph, env.w)
                    e.tensor_sub(xblk[(i - 1) % 2][:], xblk[(i - 1) % 2][:],
                                 mbc[:, 512 * s:512 * (s + 1)])
                    e.drain()
                    e.tensor_mul(xln3T[ot][:, 512 * s:512 * (s + 1)], xblk[(i - 1) % 2][:],
                                 rbc[:, 512 * s:512 * (s + 1)])
                    env.fin(e)
            env.end()

        esP = contextlib.ExitStack()
        sbP = lambda n, s, d: esP.enter_context(nc.sbuf_tensor(n, s, d))
        psP = contextlib.ExitStack()
        psum4 = [psP.enter_context(nc.psum_tensor(f"psp{i}", [128, 512], F32)) for i in range(4)]
        wbufP = [sbP(f"wbufP{i}", [128, 16 * 256], BF16) for i in range(2)]
        pst_ = [[sbP(f"pst{i}_{h}", [128, 512], BF16) for h in range(2)] for i in range(2)]
        pstF = [[sbP(f"pstF{i}_{h}", [128, 512], F32) for h in range(2)] for i in range(2)]
        SL4 = [(0, 512), (512, 512), (1024, 512), (1536, 512)]

        def evac_qks(e, ot, c0, cwid, ps, it):
            if 16 <= ot < 24:
                e.tensor_scalar(pstF[it % 2][ot % 2][:, 0:cwid], ps[:, 0:cwid],
                                32.0, -32.0, OP.min, OP.max)
            else:
                e.tensor_copy(pstF[it % 2][ot % 2][:, 0:cwid], ps[:, 0:cwid])

        def dma_qks(e, ot, c0, cwid, it):
            if ot < 8:
                dst = qT_o[128 * ot:128 * (ot + 1), c0:c0 + cwid]
            elif ot < 16:
                dst = kT_o[128 * (ot - 8):128 * (ot - 7), c0:c0 + cwid]
            else:
                dst = sT_o[128 * (ot - 16):128 * (ot - 15), c0:c0 + cwid]
            e.dma_start(out=dst, in_=pstF[it % 2][ot % 2][:, 0:cwid]).then_inc(sph, 16)
            env.inc(16)

        def qks_rhs(k, c0, cwid):
            return xln3T[k][:, c0:c0 + cwid]

        emit_proj(env, Wqks, D, 24, None, SL4, psum4, wbufP, evac_qks,
                  dma_fn=dma_qks, rhs_fn=qks_rhs)

        wkv = [sbP(f"wkv{i}", [128, 16 * 512], BF16) for i in range(2)]
        ESL = [("k", ktok_o, 0), ("v", vtok_o, 1024), ("g", gtok_o, 3072)]

        def kvg_dst(og):
            base = og * 512
            if base < 1024:
                return ktok_o, base
            if base < 3072:
                return vtok_o, base - 1024
            return gtok_o, base - 3072

        env.begin()
        @B.gpsimd
        def _(e):
            e.wait_ge(sph, env.w)
            src2 = bass.AP(tensor=Wkvg.tensor, offset=Wkvg.offset + 0,
                           ap=[[5120, 128], [5120 * 128, 16], [1, 512]])
            e.dma_start(out=wkv[0][:], in_=src2).then_inc(sph, 16)
            env.inc(16)
        env.end()
        it2 = [(og, tt) for og in range(10) for tt in range(16)]
        for i in range(len(it2) + 2):
            env.begin()
            if i < len(it2):
                og, tt = it2[i]
                if tt == 0 and og + 1 < 10:
                    @B.gpsimd
                    def _(e, og=og):
                        e.wait_ge(sph, env.w)
                        src2 = bass.AP(tensor=Wkvg.tensor, offset=Wkvg.offset + (og + 1) * 512,
                                       ap=[[5120, 128], [5120 * 128, 16], [1, 512]])
                        e.dma_start(out=wkv[(og + 1) % 2][:], in_=src2).then_inc(sph, 16)
                        env.inc(16)
                @B.tensor
                def _(e, og=og, tt=tt, i=i):
                    e.wait_ge(sph, env.w)
                    ps_ = psum4[i % 2]
                    for kc in range(16):
                        e.matmul(ps_[:, :], xln3T[kc][:, 128 * tt:128 * (tt + 1)],
                                 wkv[og % 2][:, 512 * kc:512 * (kc + 1)],
                                 start=(kc == 0), stop=(kc == 15))
                    env.fin(e)
            if 1 <= i <= len(it2):
                og, tt = it2[i - 1]
                @B.vector
                def _(e, og=og, tt=tt, i=i):
                    e.wait_ge(sph, env.w)
                    e.tensor_copy(pstF[(i - 1) % 2][0][:, :], psum4[(i - 1) % 2][:, :])
                    env.fin(e)
            if 2 <= i <= len(it2) + 1:
                og, tt = it2[i - 2]
                @B.sync
                def _(e, og=og, tt=tt, i=i):
                    e.wait_ge(sph, env.w)
                    dstt, cb = kvg_dst(og)
                    e.dma_start(out=dstt[128 * tt:128 * (tt + 1), cb:cb + 512],
                                in_=pstF[(i - 2) % 2][0][:, :]).then_inc(sph, 16)
                    env.inc(16)
            env.end()
        psP.close()
        esP.close()
        esX.close()

        esY = contextlib.ExitStack()
        sbY = lambda n, s, d: esY.enter_context(nc.sbuf_tensor(n, s, d))
        psY = contextlib.ExitStack()
        psTb = psY.enter_context(nc.psum_tensor("psTb", [128, 1024], F32))
        psHK = psY.enter_context(nc.psum_tensor("psHK", [128, 8 * DK_B], F32))
        psHV = psY.enter_context(nc.psum_tensor("psHV", [128, 8 * DV_B], F32))
        sTc = [sbY(f"sTc{i}", [128, 1024], F32) for i in range(2)]
        ktc = [sbY(f"ktc{i}", [128, 1024], F32) for i in range(2)]
        vtc = [sbY(f"vtc{i}", [128, 2048], F32) for i in range(2)]
        ptil = sbY("ptil", [128, 1024], F32)
        ptk = sbY("ptk", [128, 1024], F32)
        mxc = sbY("mxc", [128, 8], F32)
        zsumc = sbY("zsumc", [128, 8], F32)

        def sT_cols(c):
            return bass.AP(tensor=sT_o.tensor, offset=sT_o.offset + Q * c,
                           ap=[[T, 128], [T * 128, 8], [1, Q]])

        env.begin()
        @B.sync
        def _(e):
            e.wait_ge(sph, env.w)
            e.dma_start(out=sTc[0][:], in_=sT_cols(0)).then_inc(sph, 16)
            e.dma_start(out=ktc[0][:], in_=ktok_o[0:128, :]).then_inc(sph, 16)
            e.dma_start(out=vtc[0][:], in_=vtok_o[0:128, :]).then_inc(sph, 16)
            env.inc(48)
        env.end()

        for c in range(NQ):
            env.begin()
            if c + 1 < NQ:
                @B.sync
                def _(e, c=c):
                    e.wait_ge(sph, env.w)
                    e.dma_start(out=sTc[(c + 1) % 2][:], in_=sT_cols(c + 1)).then_inc(sph, 16)
                    e.dma_start(out=ktc[(c + 1) % 2][:], in_=ktok_o[128 * (c + 1):128 * (c + 2), :]).then_inc(sph, 16)
                    e.dma_start(out=vtc[(c + 1) % 2][:], in_=vtok_o[128 * (c + 1):128 * (c + 2), :]).then_inc(sph, 16)
                    env.inc(48)
            @B.vector
            def _(e, c=c):
                e.wait_ge(sph, env.w)
                for it in range(8):
                    e.reduce_max(mxc[:, it:it + 1], sTc[c % 2][:, Q * it:Q * (it + 1)], AX)
                e.drain()
                e.tensor_max(rnew[:], rold[:], mxc[:])
                e.drain()
                e.tensor_sub(resc[:], rold[:], rnew[:])
                e.drain()
                e.tensor_scalar_mul(nrn[:], rnew[:], -1.0)
                env.fin(e)
            env.end()
            env.begin()
            @B.scalar
            def _(e, c=c):
                e.wait_ge(sph, env.w)
                e.activation(out=resc[:], in_=resc[:], func=AF.Exp)
                e.drain()
                for it in range(8):
                    e.activation(out=ptil[:, Q * it:Q * (it + 1)], in_=sTc[c % 2][:, Q * it:Q * (it + 1)],
                                 func=AF.Exp, bias=nrn[:, it:it + 1])
                    e.drain()
                env.fin(e)
            env.end()
            env.begin()
            @B.vector
            def _(e, c=c):
                e.wait_ge(sph, env.w)
                for it in range(8):
                    e.reduce_sum(zsumc[:, it:it + 1], ptil[:, Q * it:Q * (it + 1)], AX)
                e.drain()
                e.tensor_mul(zacc[:], zacc[:], resc[:])
                e.drain()
                e.tensor_add(zacc[:], zacc[:], zsumc[:])
                e.drain()
                e.tensor_copy(rold[:], rnew[:])
                env.fin(e)
            @B.tensor
            def _(e, c=c):
                e.wait_ge(sph, env.w)
                for it in range(8):
                    e.transpose(psTb[:, 128 * it:128 * (it + 1)], ptil[:, Q * it:Q * (it + 1)], idf_s[:])
                env.fin(e)
            env.end()
            env.begin()
            @B.scalar
            def _(e, c=c):
                e.wait_ge(sph, env.w)
                e.copy(ptk[:], psTb[:])
                env.fin(e)
            env.end()
            env.begin()
            @B.tensor
            def _(e, c=c):
                e.wait_ge(sph, env.w)
                for h in range(H_B):
                    bb = (h % 2) * 64
                    lhs = ptk[:, (h // 2) * 128 + bb:(h // 2) * 128 + bb + 64]
                    e.matmul(psHK[bb:bb + 64, (h // 2) * DK_B:(h // 2 + 1) * DK_B],
                             lhs, ktc[c % 2][:, h * DK_B:(h + 1) * DK_B], start=True, stop=True)
                    e.matmul(psHV[bb:bb + 64, (h // 2) * DV_B:(h // 2 + 1) * DV_B],
                             lhs, vtc[c % 2][:, h * DV_B:(h + 1) * DV_B], start=True, stop=True)
                env.fin(e)
            env.end()
            env.begin()
            @B.vector
            def _(e, c=c):
                e.wait_ge(sph, env.w)
                for it in range(8):
                    e.scalar_tensor_tensor(hkS[:, DK_B * it:DK_B * (it + 1)],
                                           hkS[:, DK_B * it:DK_B * (it + 1)],
                                           resc[:, it:it + 1], psHK[:, DK_B * it:DK_B * (it + 1)],
                                           OP.mult, OP.add)
                for it in range(8):
                    e.scalar_tensor_tensor(hvS[:, DV_B * it:DV_B * (it + 1)],
                                           hvS[:, DV_B * it:DV_B * (it + 1)],
                                           resc[:, it:it + 1], psHV[:, DV_B * it:DV_B * (it + 1)],
                                           OP.mult, OP.add)
                env.fin(e)
            env.end()

        env.begin()
        @B.scalar
        def _(e):
            e.wait_ge(sph, env.w)
            e.activation(out=resc[:], in_=zacc[:], func=AF.Ln)
            e.drain()
            e.activation(out=resc[:], in_=resc[:], func=AF.Exp, scale=-1.0)
            env.fin(e)
        env.end()
        env.begin()
        @B.vector
        def _(e):
            e.wait_ge(sph, env.w)
            for it in range(8):
                e.tensor_scalar(hkS[:, DK_B * it:DK_B * (it + 1)],
                                hkS[:, DK_B * it:DK_B * (it + 1)], resc[:, it:it + 1], None, OP.mult)
                e.tensor_scalar(hvS[:, DV_B * it:DV_B * (it + 1)],
                                hvS[:, DV_B * it:DV_B * (it + 1)], resc[:, it:it + 1], None, OP.mult)
            env.fin(e)
        env.end()
        env.begin()
        @B.scalar
        def _(e):
            e.wait_ge(sph, env.w)
            e.activation(out=zacc[:], in_=zacc[:], func=AF.Ln)
            env.fin(e)
        env.end()
        env.begin()
        @B.vector
        def _(e):
            e.wait_ge(sph, env.w)
            e.tensor_add(zacc[:], zacc[:], rold[:])
            env.fin(e)
        env.end()
        env.begin()
        @B.sync
        def _(e):
            e.wait_ge(sph, env.w)
            e.dma_start(out=hk_o[:], in_=hkS[:]).then_inc(sph, 16)
            e.dma_start(out=hv_o[:], in_=hvS[:]).then_inc(sph, 16)
            e.dma_start(out=zc_o[:], in_=zacc[:]).then_inc(sph, 16)
            env.inc(48)
        env.end()
        env.begin()
        @B.sync
        def _(e):
            e.wait_ge(sph, env.w)
        env.end()
        psY.close()
        esY.close()
    return nc


def l3b_inputs(inputs, core, l3outs):
    return {
        "R2T": l3outs["R2T"],
        "Wqks": np.concatenate([np.asarray(inputs["L0_attn_b_Wq"]),
                                np.asarray(inputs["L0_attn_b_Wk"]),
                                np.asarray(inputs["L0_attn_b_Ws"])], axis=1),
        "Wkvg": np.concatenate([np.asarray(inputs["L0_attn_b_Wk"]),
                                np.asarray(inputs["L0_attn_b_Wv"]),
                                np.asarray(inputs["L0_attn_b_Wg"])], axis=1),
        "idb": np.eye(128, dtype=bfnp),
        "idf": np.eye(128, dtype=np.float32),
    }



DEBUG_L4 = False


def build_L4():
    nc = bass.Bass("TRN2", target_bir_lowering=False)
    R2T = nc.dram_tensor("R2T", [D, T], F32, kind="ExternalInput").ap()
    qT = nc.dram_tensor("qT", [1024, T], F32, kind="ExternalInput").ap()
    kT = nc.dram_tensor("kT", [1024, T], F32, kind="ExternalInput").ap()
    sT = nc.dram_tensor("sT", [1024, T], F32, kind="ExternalInput").ap()
    ktok = nc.dram_tensor("ktok", [T, 1024], F32, kind="ExternalInput").ap()
    vtok = nc.dram_tensor("vtok", [T, 2048], F32, kind="ExternalInput").ap()
    gtok = nc.dram_tensor("gtok", [T, 2048], F32, kind="ExternalInput").ap()
    hk_all = nc.dram_tensor("hk_all", [NCORES, 128, 8 * DK_B], F32, kind="ExternalInput").ap()
    hv_all = nc.dram_tensor("hv_all", [NCORES, 128, 8 * DV_B], F32, kind="ExternalInput").ap()
    zc_all = nc.dram_tensor("zc_all", [NCORES, 128, 8], F32, kind="ExternalInput").ap()
    zmask = nc.dram_tensor("zmask", [128, 8 * NCORES], F32, kind="ExternalInput").ap()
    pm01 = nc.dram_tensor("pm01", [128, NCORES], F32, kind="ExternalInput").ap()
    Wo2 = nc.dram_tensor("Wo2", [D, D], F32, kind="ExternalInput").ap()
    gnwt = nc.dram_tensor("gnwt", [D], F32, kind="ExternalInput").ap()
    wg = nc.dram_tensor("wg", [D, 64], F32, kind="ExternalInput").ap()
    idb = nc.dram_tensor("idb", [128, 128], BF16, kind="ExternalInput").ap()
    idf = nc.dram_tensor("idf", [128, 128], F32, kind="ExternalInput").ap()
    utri = nc.dram_tensor("utri", [128, 128], F32, kind="ExternalInput").ap()

    R3T = nc.dram_tensor("R3T", [D, T], F32, kind="ExternalOutput").ap()
    xlnT_o = nc.dram_tensor("xlnT_o", [D, T], BF16, kind="ExternalOutput").ap()
    xlntok_o = nc.dram_tensor("xlntok_o", [T, D], BF16, kind="ExternalOutput").ap()
    scores_o = nc.dram_tensor("scores_o", [T, 64], F32, kind="ExternalOutput").ap()
    vals_o = nc.dram_tensor("vals_o", [T, 8], F32, kind="ExternalOutput").ap()
    hist_o = nc.dram_tensor("hist_o", [64, 1], F32, kind="ExternalOutput").ap()
    dbg_ov = nc.dram_tensor("dbg_ov", [D, T], BF16, kind="ExternalOutput").ap() if DEBUG_L4 else None
    dbg_z = nc.dram_tensor("dbg_z", [NQ, 128, 8], F32, kind="ExternalOutput").ap() if DEBUG_L4 else None
    dbg_hk = nc.dram_tensor("dbg_hk", [NQ, 128, 512], F32, kind="ExternalOutput").ap() if DEBUG_L4 else None
    dbg_hv = nc.dram_tensor("dbg_hv", [NQ, 128, 1024], F32, kind="ExternalOutput").ap() if DEBUG_L4 else None

    es = contextlib.ExitStack()
    with es:
        env = Env(nc, es)
        sbt, pst = env.sbt, env.pst
        idb_s = sbt("idb_s", [128, 128], BF16)
        idf_s = sbt("idf_s", [128, 128], F32)
        utri_s = sbt("utri_s", [128, 128], F32)
        onesr = sbt("onesr", [1, 128], F32)
        hkR = sbt("hkR", [128, 8 * DK_B], F32)
        hvR = sbt("hvR", [128, 8 * DV_B], F32)
        zprev = sbt("zprev", [128, 8], F32)
        gnwbc = sbt("gnwbc", [128, D], F32)
        gnr = sbt("gnr", [1, D], F32)
        env.setup_consts([LN_EPS, 1e-5, 1.0, -1.0, -0.5, 0.5, 0.0, 1.0 / D,
                          1.0 / DV_B, SCL_B])

        B = es.enter_context(nc.Block())
        env.B = B
        sph = env.sph
        sQK = es.enter_context(nc.semaphore("sQK"))
        sMK = es.enter_context(nc.semaphore("sMK"))
        sOK = es.enter_context(nc.semaphore("sOK"))
        sW2 = es.enter_context(nc.semaphore("sW2"))
        sWM = es.enter_context(nc.semaphore("sWM"))
        sOV = es.enter_context(nc.semaphore("sOV"))

        env.begin()
        @B.sync
        def _(e):
            e.dma_start(out=idb_s[:], in_=idb[:]).then_inc(sph, 16)
            e.dma_start(out=idf_s[:], in_=idf[:]).then_inc(sph, 16)
            e.dma_start(out=utri_s[:], in_=utri[:]).then_inc(sph, 16)
            e.dma_start(out=gnr[0:1, :], in_=gnwt[:]).then_inc(sph, 16)
            env.inc(64)
        @B.vector
        def _(e):
            e.memset(onesr[:], 1.0)
            e.drain()
            env.emit_const_memsets(e)
            env.fin(e)
        env.end()

        for s4 in range(4):
            env.begin()
            @B.tensor
            def _(e, s4=s4):
                e.wait_ge(sph, env.w)
                e.matmul(psC1[:, :], onesr[:, :], gnr[0:1, 512 * s4:512 * (s4 + 1)],
                         start=True, stop=True) if False else None
                env.fin(e)
            env.end()

        psG = contextlib.ExitStack()
        psA1 = psG.enter_context(nc.psum_tensor("l4psA1", [128, 1024], F32))
        psA2 = psG.enter_context(nc.psum_tensor("l4psA2", [128, 1024], F32))
        psB1 = psG.enter_context(nc.psum_tensor("l4psB1", [128, 1024], F32))
        psC1 = psG.enter_context(nc.psum_tensor("l4psC1", [128, 512], F32))
        psC2 = psG.enter_context(nc.psum_tensor("l4psC2", [128, 512], F32))

        for s4 in range(4):
            env.begin()
            @B.tensor
            def _(e, s4=s4):
                e.wait_ge(sph, env.w)
                e.matmul(psC1[:, :], onesr[:, :], gnr[0:1, 512 * s4:512 * (s4 + 1)],
                         start=True, stop=True)
                env.fin(e)
            env.end()
            env.begin()
            @B.vector
            def _(e, s4=s4):
                e.wait_ge(sph, env.w)
                e.tensor_copy(gnwbc[:, 512 * s4:512 * (s4 + 1)], psC1[:, :])
                env.fin(e)
            env.end()

        esP = contextlib.ExitStack()
        sbP = lambda n, s, d: esP.enter_context(nc.sbuf_tensor(n, s, d))
        hkj = [sbP(f"hkj{j}", [128, 8 * DK_B], F32) for j in range(NCORES)]
        hvj = [sbP(f"hvj{j}", [128, 8 * DV_B], F32) for j in range(NCORES)]
        zxj = sbP("zxj", [128, 8 * NCORES], F32)
        zmx = sbP("zmx", [128, 8], F32)
        zsm = sbP("zsm", [128, 8], F32)
        wj = sbP("wj", [128, 8], F32)

        env.begin()
        @B.sync
        def _(e):
            e.wait_ge(sph, env.w)
            for j in range(NCORES):
                e.dma_start(out=hkj[j][:], in_=hk_all[j, :, :]).then_inc(sph, 16)
                e.dma_start(out=hvj[j][:], in_=hv_all[j, :, :]).then_inc(sph, 16)
                e.dma_start(out=zxj[:, 8 * j:8 * (j + 1)], in_=zc_all[j, :, :]).then_inc(sph, 16)
            env.inc(16 * 24)
        env.end()
        zmk = sbP("zmk", [128, 8 * NCORES], F32)
        pmS = sbP("pmS", [128, NCORES], F32)
        env.begin()
        @B.sync
        def _(e):
            e.wait_ge(sph, env.w)
            e.dma_start(out=zmk[:], in_=zmask[:]).then_inc(sph, 16)
            e.dma_start(out=pmS[:], in_=pm01[:]).then_inc(sph, 16)
            env.inc(32)
        env.end()
        env.begin()
        @B.vector
        def _(e):
            e.wait_ge(sph, env.w)
            e.tensor_add(zxj[:], zxj[:], zmk[:])
            e.drain()
            e.tensor_copy(zmx[:], zxj[:, 0:8])
            e.drain()
            for j in range(1, NCORES):
                e.tensor_max(zmx[:], zmx[:], zxj[:, 8 * j:8 * (j + 1)])
                e.drain()
            e.tensor_tensor(zxj[:].rearrange("p (j c) -> p j c", c=8),
                            zxj[:].rearrange("p (j c) -> p j c", c=8),
                            rep_tile(zmx[:, 0:8], NCORES, 8), OP.subtract)
            env.fin(e)
        env.end()
        env.begin()
        @B.scalar
        def _(e):
            e.wait_ge(sph, env.w)
            e.activation(out=zxj[:], in_=zxj[:], func=AF.Exp)
            env.fin(e)
        env.end()
        env.begin()
        @B.vector
        def _(e):
            e.wait_ge(sph, env.w)
            e.tensor_reduce(zsm[:].rearrange("p (a c) -> p a c", a=1),
                            zxj[:].rearrange("p (j c) -> p c j", c=8), AX, OP.add) if False else None
            ap3 = bass.AP(tensor=zxj[:].tensor, offset=zxj[:].offset,
                          ap=[[zxj[:].ap[0][0], 128], [1, 8], [8, NCORES]])
            e.tensor_reduce(zsm[:], ap3, AX, OP.add)
            env.fin(e)
        env.end()
        env.begin()
        @B.scalar
        def _(e):
            e.wait_ge(sph, env.w)
            e.activation(out=wj[:], in_=zsm[:], func=AF.Ln)
            env.fin(e)
        env.end()
        env.begin()
        @B.vector
        def _(e):
            e.wait_ge(sph, env.w)
            e.tensor_add(zprev[:], zmx[:], wj[:])
            e.drain()
            e.memset(hkR[:], 0.0)
            e.memset(hvR[:], 0.0)
            env.fin(e)
        env.end()
        env.begin()
        @B.scalar
        def _(e):
            e.wait_ge(sph, env.w)
            e.activation(out=zsm[:], in_=zsm[:], func=AF.Ln)
            e.drain()
            e.activation(out=zsm[:], in_=zsm[:], func=AF.Exp, scale=-1.0)
            env.fin(e)
        env.end()
        for j in range(NCORES):
            env.begin()
            @B.vector
            def _(e, j=j):
                e.wait_ge(sph, env.w)
                e.tensor_mul(wj[:], zxj[:, 8 * j:8 * (j + 1)], zsm[:])
                e.drain()
                e.tensor_scalar(wj[:], wj[:], pmS[:, j:j + 1], None, OP.mult)
                e.drain()
                for it in range(8):
                    e.scalar_tensor_tensor(hkR[:, DK_B * it:DK_B * (it + 1)],
                                           hkj[j][:, DK_B * it:DK_B * (it + 1)],
                                           wj[:, it:it + 1], hkR[:, DK_B * it:DK_B * (it + 1)],
                                           OP.mult, OP.add)
                for it in range(8):
                    e.scalar_tensor_tensor(hvR[:, DV_B * it:DV_B * (it + 1)],
                                           hvj[j][:, DV_B * it:DV_B * (it + 1)],
                                           wj[:, it:it + 1], hvR[:, DV_B * it:DV_B * (it + 1)],
                                           OP.mult, OP.add)
                env.fin(e)
            env.end()
        esP.close()

        esC = contextlib.ExitStack()
        ovT = [esC.enter_context(nc.sbuf_tensor(f"ovT{k}", [128, T], BF16)) for k in range(16)]
        esC2 = contextlib.ExitStack()
        sbC = lambda n, s, d: esC2.enter_context(nc.sbuf_tensor(n, s, d))
        sTc = [sbC(f"sTc{i}", [128, 1024], F32) for i in range(2)]
        qTc = [sbC(f"qTc{i}", [128, 1024], F32) for i in range(2)]
        kTc = [sbC(f"kTc{i}", [128, 1024], F32) for i in range(2)]
        ktc = [sbC(f"ktc{i}", [128, 1024], F32) for i in range(2)]
        vtc = [sbC(f"vtc{i}", [128, 2048], F32) for i in range(2)]
        gtc = [sbC(f"gtc{i}", [128, 2048], F32) for i in range(2)]
        ptil = sbC("ptil", [128, 1024], F32)
        ptk = sbC("ptk", [128, 1024], F32)
        ctf = sbC("ctf", [128, 1024], F32)
        cttok = sbC("cttok", [128, 1024], F32)
        rc = sbC("rc", [128, 1024], F32)
        okb = sbC("okb", [128, 1024], F32)
        pvb = sbC("pvb", [128, 1024], F32)
        pvnT = sbC("pvnT", [128, 1024], F32)
        pvnAT = sbC("pvnAT", [128, 1024], F32)
        hkA = sbC("hkA", [128, 8 * DK_B], F32)
        hkD = sbC("hkD", [128, 8 * DK_B], F32)
        hkOst = sbC("hkOst", [64, 512], F32)
        qkb = [sbC(f"qkb{i}", [128, 128], F32) for i in range(2)]
        wmk = qkb
        mxc = sbC("mxc", [128, 8], F32)
        rref = sbC("rref", [128, 8], F32)
        nrf = sbC("nrf", [128, 8], F32)
        acol = sbC("acol", [128, 8], F32)
        cend = sbC("cend", [128, 8], F32)
        rcend = sbC("rcend", [128, 8], F32)
        arc = sbC("arc", [128, 8], F32)
        segm = sbC("segm", [128, 16], F32)
        segr = sbC("segr", [128, 16], F32)
        ovb = sbC("ovb", [128, 2048], F32)
        sqb = sbC("sqb", [128, 1024], F32)

        def colsl(dr, c, nch):
            return bass.AP(tensor=dr.tensor, offset=dr.offset + Q * c,
                           ap=[[T, 128], [T * 128, nch], [1, Q]])

        env.begin()
        @B.sync
        def _(e):
            e.wait_ge(sph, env.w)
            e.dma_start(out=sTc[0][:], in_=colsl(sT, 0, 8)).then_inc(sph, 16)
            e.dma_start(out=qTc[0][:], in_=colsl(qT, 0, 8)).then_inc(sph, 16)
            e.dma_start(out=kTc[0][:], in_=colsl(kT, 0, 8)).then_inc(sph, 16)
            e.dma_start(out=ktc[0][:], in_=ktok[0:128, :]).then_inc(sph, 16)
            e.dma_start(out=vtc[0][:], in_=vtok[0:128, :]).then_inc(sph, 16)
            e.dma_start(out=gtc[0][:], in_=gtok[0:128, :]).then_inc(sph, 16)
            env.inc(96)
        env.end()

        for c in range(NQ):
            par = c % 2
            env.begin()
            if c + 1 < NQ:
                @B.sync
                def _(e, c=c):
                    e.wait_ge(sph, env.w)
                    nx = (c + 1) % 2
                    e.dma_start(out=sTc[nx][:], in_=colsl(sT, c + 1, 8)).then_inc(sph, 16)
                    e.dma_start(out=qTc[nx][:], in_=colsl(qT, c + 1, 8)).then_inc(sph, 16)
                    e.dma_start(out=kTc[nx][:], in_=colsl(kT, c + 1, 8)).then_inc(sph, 16)
                    e.dma_start(out=ktc[nx][:], in_=ktok[128 * (c + 1):128 * (c + 2), :]).then_inc(sph, 16)
                    e.dma_start(out=vtc[nx][:], in_=vtok[128 * (c + 1):128 * (c + 2), :]).then_inc(sph, 16)
                    e.dma_start(out=gtc[nx][:], in_=gtok[128 * (c + 1):128 * (c + 2), :]).then_inc(sph, 16)
                    env.inc(96)
            @B.vector
            def _(e, c=c, par=par):
                e.wait_ge(sph, env.w)
                for it in range(8):
                    e.reduce_max(mxc[:, it:it + 1], sTc[par][:, Q * it:Q * (it + 1)], AX)
                e.drain()
                e.tensor_max(rref[:], zprev[:], mxc[:])
                e.drain()
                e.tensor_scalar_mul(nrf[:], rref[:], -1.0)
                e.drain()
                e.tensor_sub(acol[:], zprev[:], rref[:])
                env.fin(e)
            env.end()
            env.begin()
            @B.scalar
            def _(e, c=c, par=par):
                e.wait_ge(sph, env.w)
                e.activation(out=acol[:], in_=acol[:], func=AF.Exp)
                e.drain()
                for it in range(8):
                    e.activation(out=ptil[:, Q * it:Q * (it + 1)], in_=sTc[par][:, Q * it:Q * (it + 1)],
                                 func=AF.Exp, bias=nrf[:, it:it + 1])
                    e.drain()
                env.fin(e)
            env.end()
            env.begin()
            @B.vector
            def _(e, c=c):
                e.wait_ge(sph, env.w)
                for it in range(8):
                    e.tensor_scalar(hkA[:, DK_B * it:DK_B * (it + 1)],
                                    hkR[:, DK_B * it:DK_B * (it + 1)], acol[:, it:it + 1], None, OP.mult)
                e.drain()
                for it in range(8):
                    e.tensor_tensor_scan(ctf[:, Q * it:Q * (it + 1)], ptil[:, Q * it:Q * (it + 1)],
                                         ptil[:, Q * it:Q * (it + 1)], acol[:, it:it + 1],
                                         OP.add, OP.bypass)
                env.fin(e)
            env.end()
            env.begin()
            @B.tensor
            def _(e, c=c):
                e.wait_ge(sph, env.w)
                for it in range(8):
                    e.transpose(psB1[:, 128 * it:128 * (it + 1)], ptil[:, Q * it:Q * (it + 1)], idf_s[:])
                env.fin(e)
            @B.vector
            def _(e, c=c):
                e.wait_ge(sph, env.w)
                for it in range(8):
                    e.tensor_copy(cend[:, it:it + 1], ctf[:, Q * it + Q - 1:Q * it + Q])
                env.fin(e)
            env.end()
            env.begin()
            @B.scalar
            def _(e, c=c):
                e.wait_ge(sph, env.w)
                e.copy(ptk[:], psB1[:])
                env.fin(e)
            @B.tensor
            def _(e, c=c):
                e.wait_ge(sph, env.w)
                for it in range(4):
                    e.transpose(psA1[:, 256 * it:256 * it + 128], ctf[:, 256 * it:256 * it + 128], idf_s[:])
                    e.transpose(psA1[:, 256 * it + 128:256 * it + 256], ctf[:, 256 * it + 128:256 * it + 256], idf_s[:])
                env.fin(e)
            env.end()
            env.begin()
            @B.vector
            def _(e, c=c):
                e.wait_ge(sph, env.w)
                e.tensor_copy(cttok[:], psA1[:])
                env.fin(e)
            @B.tensor
            def _(e, c=c):
                e.wait_ge(sph, env.w)
                for h in range(H_B):
                    bb = (h % 2) * 64
                    ps_ = psC1 if h % 2 == 0 else psC2
                    cc0 = (h // 2) * 64
                    e.transpose(ps_[0:64, cc0:cc0 + 64],
                                hkA[bb:bb + 64, (h // 2) * DK_B:(h // 2) * DK_B + 64],
                                idf_s[bb:bb + 64, 0:64])
                env.fin(e)
            env.end()
            env.begin()
            @B.vector
            def _(e, c=c):
                e.wait_ge(sph, env.w)
                for h in range(0, H_B, 2):
                    e.tensor_copy(hkD[0:64, (h // 2) * DK_B:(h // 2) * DK_B + 64],
                                  psC1[0:64, (h // 2) * 64:(h // 2) * 64 + 64])
                e.drain()
                e.tensor_copy(hkOst[0:64, :], psC2[0:64, 0:512])
                env.fin(e)
            env.end()
            env.begin()
            @B.sync
            def _(e, c=c):
                e.wait_ge(sph, env.w)
                e.dma_start(out=hkD[64:128, 0:512], in_=hkOst[0:64, :]).then_inc(sph, 16)
                env.inc(16)
            env.end()
            env.begin()
            @B.scalar
            def _(e, c=c):
                e.wait_ge(sph, env.w)
                e.activation(out=rc[:], in_=cttok[:], func=AF.Ln)
                e.drain()
                e.activation(out=rc[:], in_=rc[:], func=AF.Exp, scale=-1.0)
                env.fin(e)
            env.end()
            env.begin()
            qb = c * H_B
            @B.tensor
            def _(e, c=c, par=par, qb=qb):
                e.wait_ge(sph, env.w)
                def okmm(h):
                    bb = (h % 2) * 64
                    e.matmul(psA1[:, 64 * h:64 * (h + 1)], qkb[h % 2][:],
                             ptk[:, (h // 2) * 128 + bb:(h // 2) * 128 + bb + 64],
                             start=True, stop=True)
                    e.matmul(psA2[:, 64 * h:64 * (h + 1)],
                             qTc[par][bb:bb + 64, (h // 2) * 128:(h // 2) * 128 + 128],
                             hkD[bb:bb + 64, (h // 2) * DK_B:(h // 2) * DK_B + 64],
                             start=True, stop=True).then_inc(sOK, 1)
                for h in range(H_B):
                    bb = (h % 2) * 64
                    ps_ = psC1 if h % 2 == 0 else psC2
                    if h >= 2:
                        e.wait_ge(sMK, qb + h - 1)
                    e.matmul(ps_[:, 0:128], kTc[par][bb:bb + 64, (h // 2) * 128:(h // 2) * 128 + 128],
                             qTc[par][bb:bb + 64, (h // 2) * 128:(h // 2) * 128 + 128],
                             start=True, stop=True).then_inc(sQK, 1)
                    if h >= 1:
                        e.wait_ge(sMK, qb + h)
                        okmm(h - 1)
                e.wait_ge(sMK, qb + H_B)
                okmm(H_B - 1)
                env.fin(e)
            @B.vector
            def _(e, c=c, qb=qb):
                e.wait_ge(sph, env.w)
                for h in range(H_B):
                    ps_ = psC1 if h % 2 == 0 else psC2
                    e.wait_ge(sQK, qb + h + 1)
                    if h >= 2:
                        e.wait_ge(sOK, qb + h - 1)
                    e.tensor_tensor(qkb[h % 2][:], ps_[:, 0:128], utri_s[:], OP.mult)
                    e.drain()
                    e.nop().then_inc(sMK, 1)
                env.fin(e)
            env.end()
            env.begin()
            @B.vector
            def _(e, c=c):
                e.wait_ge(sph, env.w)
                e.tensor_copy(cttok[:], psA2[:])
                env.fin(e)
            env.end()
            env.begin()
            @B.vector
            def _(e, c=c):
                e.wait_ge(sph, env.w)
                e.tensor_tensor(okb[:], psA1[:], cttok[:], OP.add)
                e.drain()
                e.tensor_mul(okb[:], okb[:], rc[:])
                e.drain()
                e.tensor_scalar_mul(okb[:], okb[:], SCL_B)
                e.drain()
                e.tensor_reduce(segm[:], okb[:].rearrange("p (h m) -> p h m", m=M_B), AX, OP.max)
                e.drain()
                e.tensor_tensor(okb[:].rearrange("p (h m) -> p h m", m=M_B),
                                okb[:].rearrange("p (h m) -> p h m", m=M_B),
                                rep_cols(segm[:, 0:16], 16, M_B), OP.subtract)
                env.fin(e)
            env.end()
            env.begin()
            @B.scalar
            def _(e, c=c):
                e.wait_ge(sph, env.w)
                e.activation(out=pvb[:], in_=okb[:], func=AF.Exp)
                env.fin(e)
            env.end()
            env.begin()
            @B.vector
            def _(e, c=c):
                e.wait_ge(sph, env.w)
                e.tensor_reduce(segr[:], pvb[:].rearrange("p (h m) -> p h m", m=M_B), AX, OP.add)
                env.fin(e)
            env.end()
            env.begin()
            @B.scalar
            def _(e, c=c):
                e.wait_ge(sph, env.w)
                e.activation(out=segr[:], in_=segr[:], func=AF.Ln)
                e.drain()
                e.activation(out=segr[:], in_=segr[:], func=AF.Exp, scale=-1.0)
                env.fin(e)
            env.end()
            env.begin()
            @B.vector
            def _(e, c=c):
                e.wait_ge(sph, env.w)
                e.tensor_tensor(pvb[:].rearrange("p (h m) -> p h m", m=M_B),
                                pvb[:].rearrange("p (h m) -> p h m", m=M_B),
                                rep_cols(segr[:, 0:16], 16, M_B), OP.mult)
                e.drain()
                e.tensor_mul(pvb[:], pvb[:], rc[:])
                env.fin(e)
            env.end()
            env.begin()
            @B.tensor
            def _(e, c=c):
                e.wait_ge(sph, env.w)
                for it in range(8):
                    e.transpose(psB1[:, 128 * it:128 * (it + 1)],
                                pvb[:, 128 * it:128 * (it + 1)], idf_s[:])
                env.fin(e)
            env.end()
            env.begin()
            @B.scalar
            def _(e, c=c):
                e.wait_ge(sph, env.w)
                e.copy(pvnT[:], psB1[:])
                env.fin(e)
            env.end()
            env.begin()
            @B.vector
            def _(e, c=c):
                e.wait_ge(sph, env.w)
                e.tensor_tensor(pvnAT[:].rearrange("p (i t) -> p i t", t=Q),
                                pvnT[:].rearrange("p (i t) -> p i t", t=Q),
                                rep_cols(acol[:, 0:8], 8, Q), OP.mult)
                env.fin(e)
            env.end()
            env.begin()
            qb = c * H_B
            @B.tensor
            def _(e, c=c, par=par, qb=qb):
                e.wait_ge(sph, env.w)
                def ovmm(h):
                    bb = (h % 2) * 64
                    psOV = psA1 if h < 8 else psA2
                    cc0 = (h % 8) * 128
                    e.matmul(psOV[:, cc0:cc0 + 128], wmk[h % 2][:],
                             vtc[par][:, h * DV_B:(h + 1) * DV_B], start=True, stop=False)
                    e.matmul(psOV[:, cc0:cc0 + 128],
                             pvnAT[bb:bb + 64, (h // 2) * 128:(h // 2) * 128 + 128],
                             hvR[bb:bb + 64, (h // 2) * DV_B:(h // 2) * DV_B + 128],
                             start=False, stop=True).then_inc(sOV, 1)
                for h in range(H_B):
                    bb = (h % 2) * 64
                    ps_ = psC1 if h % 2 == 0 else psC2
                    if h >= 2:
                        e.wait_ge(sWM, qb + h - 1)
                    e.matmul(ps_[:, 0:128], ptil[bb:bb + 64, (h // 2) * 128:(h // 2) * 128 + 128],
                             pvnT[bb:bb + 64, (h // 2) * 128:(h // 2) * 128 + 128],
                             start=True, stop=True).then_inc(sW2, 1)
                    if h >= 1:
                        e.wait_ge(sWM, qb + h)
                        ovmm(h - 1)
                e.wait_ge(sWM, qb + H_B)
                ovmm(H_B - 1)
                env.fin(e)
            @B.vector
            def _(e, c=c, qb=qb):
                e.wait_ge(sph, env.w)
                for h in range(H_B):
                    ps_ = psC1 if h % 2 == 0 else psC2
                    e.wait_ge(sW2, qb + h + 1)
                    if h >= 2:
                        e.wait_ge(sOV, qb + h - 1)
                    e.tensor_tensor(wmk[h % 2][:], ps_[:, 0:128], utri_s[:], OP.mult)
                    e.drain()
                    e.nop().then_inc(sWM, 1)
                env.fin(e)
            env.end()
            env.begin()
            @B.vector
            def _(e, c=c):
                e.wait_ge(sph, env.w)
                e.tensor_copy(ovb[:, 0:1024], psA1[:])
                e.drain()
                e.tensor_copy(ovb[:, 1024:2048], psA2[:])
                env.fin(e)
            env.end()
            for hf in range(2):
                env.begin()
                @B.scalar
                def _(e, c=c, hf=hf):
                    e.wait_ge(sph, env.w)
                    e.activation(out=sqb[:], in_=ovb[:, 1024 * hf:1024 * (hf + 1)], func=AF.Square)
                    env.fin(e)
                env.end()
                env.begin()
                @B.vector
                def _(e, c=c, hf=hf):
                    e.wait_ge(sph, env.w)
                    e.tensor_reduce(segm[:, 8 * hf:8 * (hf + 1)],
                                    sqb[:].rearrange("p (h v) -> p h v", v=DV_B), AX, OP.add)
                    env.fin(e)
                env.end()
            env.begin()
            @B.vector
            def _(e, c=c):
                e.wait_ge(sph, env.w)
                e.tensor_scalar_mul(segm[:], segm[:], 1.0 / DV_B)
                env.fin(e)
            env.end()
            env.begin()
            @B.scalar
            def _(e, c=c):
                e.wait_ge(sph, env.w)
                e.activation(out=segm[:], in_=segm[:], func=AF.Ln, bias=1e-5)
                e.drain()
                e.activation(out=segm[:], in_=segm[:], func=AF.Exp, scale=-0.5)
                env.fin(e)
            env.end()
            env.begin()
            @B.vector
            def _(e, c=c):
                e.wait_ge(sph, env.w)
                e.tensor_tensor(ovb[:].rearrange("p (h v) -> p h v", v=DV_B),
                                ovb[:].rearrange("p (h v) -> p h v", v=DV_B),
                                rep_cols(segm[:, 0:16], 16, DV_B), OP.mult)
                e.drain()
                e.tensor_mul(ovb[:], ovb[:], gnwbc[:])
                env.fin(e)
            env.end()
            for hf in range(2):
                env.begin()
                @B.scalar
                def _(e, c=c, par=par, hf=hf):
                    e.wait_ge(sph, env.w)
                    e.activation(out=sqb[:], in_=gtc[par][:, 1024 * hf:1024 * (hf + 1)], func=AF.Sigmoid)
                    env.fin(e)
                env.end()
                env.begin()
                @B.vector
                def _(e, c=c, par=par, hf=hf):
                    e.wait_ge(sph, env.w)
                    e.tensor_mul(sqb[:], sqb[:], gtc[par][:, 1024 * hf:1024 * (hf + 1)])
                    e.drain()
                    e.tensor_mul(ovb[:, 1024 * hf:1024 * (hf + 1)],
                                 ovb[:, 1024 * hf:1024 * (hf + 1)], sqb[:])
                    env.fin(e)
                env.end()
            for half in range(2):
                env.begin()
                @B.tensor
                def _(e, c=c, half=half):
                    e.wait_ge(sph, env.w)
                    for u in range(8):
                        e.transpose(psB1[:, 128 * u:128 * (u + 1)],
                                    ovb[:, 1024 * half + 128 * u:1024 * half + 128 * (u + 1)], idf_s[:])
                    env.fin(e)
                env.end()
                env.begin()
                @B.scalar
                def _(e, c=c, half=half):
                    e.wait_ge(sph, env.w)
                    for u in range(8):
                        k = half * 8 + u
                        e.copy(ovT[k][:, Q * c:Q * (c + 1)], psB1[:, 128 * u:128 * (u + 1)])
                    env.fin(e)
                env.end()
            env.begin()
            @B.tensor
            def _(e, c=c, par=par):
                e.wait_ge(sph, env.w)
                for h in range(H_B):
                    bb = (h % 2) * 64
                    lhs = ptk[:, (h // 2) * 128 + bb:(h // 2) * 128 + bb + 64]
                    e.matmul(psC1[bb:bb + 64, ((h // 2) % 4) * 64:((h // 2) % 4) * 64 + 64] if False else
                             psC1[bb:bb + 64, 64 * ((h // 2) % 4):64 * ((h // 2) % 4) + 64],
                             lhs, ktc[par][:, h * DK_B:(h + 1) * DK_B],
                             start=True, stop=True) if False else None
                for h in range(H_B):
                    bb = (h % 2) * 64
                    e.matmul(psC1[bb:bb + 64, (h // 2) * 64:(h // 2) * 64 + 64],
                             ptk[:, (h // 2) * 128 + bb:(h // 2) * 128 + bb + 64],
                             ktc[par][:, h * DK_B:(h + 1) * DK_B], start=True, stop=True)
                for h in range(H_B):
                    bb = (h % 2) * 64
                    psOV = psA1 if h < 8 else psA2
                    cc0 = (h % 8) * 128
                    e.matmul(psOV[:, cc0:cc0 + 128] if False else psA1[bb:bb + 64, 0:1] ,
                             ptk[:, 0:1], ktc[par][:, 0:1], start=True, stop=True) if False else None
                for h in range(H_B):
                    bb = (h % 2) * 64
                    psx = psA1 if h < 8 else psA2
                    e.matmul(psx[bb:bb + 64, (h // 2) * DV_B % 1024:(h // 2) * DV_B % 1024 + DV_B],
                             ptk[:, (h // 2) * 128 + bb:(h // 2) * 128 + bb + 64],
                             vtc[par][:, h * DV_B:(h + 1) * DV_B], start=True, stop=True)
                env.fin(e)
            env.end()
            env.begin()
            @B.scalar
            def _(e, c=c):
                e.wait_ge(sph, env.w)
                e.activation(out=rcend[:], in_=cend[:], func=AF.Ln)
                e.drain()
                e.activation(out=rcend[:], in_=rcend[:], func=AF.Exp, scale=-1.0)
                env.fin(e)
            env.end()
            env.begin()
            @B.vector
            def _(e, c=c):
                e.wait_ge(sph, env.w)
                e.tensor_mul(arc[:], acol[:], rcend[:])
                e.drain()
                for it in range(8):
                    e.tensor_scalar(hkR[:, DK_B * it:DK_B * (it + 1)],
                                    hkR[:, DK_B * it:DK_B * (it + 1)], arc[:, it:it + 1], None, OP.mult)
                e.drain()
                for it in range(8):
                    e.scalar_tensor_tensor(hkR[:, DK_B * it:DK_B * (it + 1)],
                                           psC1[:, 64 * it:64 * (it + 1)], rcend[:, it:it + 1],
                                           hkR[:, DK_B * it:DK_B * (it + 1)], OP.mult, OP.add)
                e.drain()
                for it in range(8):
                    e.tensor_scalar(hvR[:, DV_B * it:DV_B * (it + 1)],
                                    hvR[:, DV_B * it:DV_B * (it + 1)], arc[:, it:it + 1], None, OP.mult)
                e.drain()
                for it in range(8):
                    psx = psA1 if it < 4 else psA2
                    e.scalar_tensor_tensor(hvR[:, DV_B * it:DV_B * (it + 1)],
                                           psx[:, it * 128:it * 128 + 128],
                                           rcend[:, it:it + 1],
                                           hvR[:, DV_B * it:DV_B * (it + 1)], OP.mult, OP.add)
                e.drain()
                e.tensor_copy(zprev[:], rref[:])
                env.fin(e)
            env.end()
            env.begin()
            @B.scalar
            def _(e, c=c):
                e.wait_ge(sph, env.w)
                e.activation(out=cend[:], in_=cend[:], func=AF.Ln)
                env.fin(e)
            env.end()
            env.begin()
            @B.vector
            def _(e, c=c):
                e.wait_ge(sph, env.w)
                e.tensor_add(zprev[:], zprev[:], cend[:])
                env.fin(e)
            env.end()
            if DEBUG_L4:
                env.begin()
                @B.sync
                def _(e, c=c):
                    e.wait_ge(sph, env.w)
                    e.dma_start(out=dbg_z[c, :, :], in_=zprev[:]).then_inc(sph, 16)
                    e.dma_start(out=dbg_hk[c, :, :], in_=hkR[:]).then_inc(sph, 16)
                    e.dma_start(out=dbg_hv[c, :, :], in_=hvR[:]).then_inc(sph, 16)
                    env.inc(48)
                env.end()

        if DEBUG_L4:
            env.begin()
            @B.sync
            def _(e):
                e.wait_ge(sph, env.w)
                for k in range(16):
                    e.dma_start(out=dbg_ov[128 * k:128 * (k + 1), :], in_=ovT[k][:]).then_inc(sph, 16)
                env.inc(256)
            env.end()
        psG.close()
        esC2.close()

        emit_out_ln_scores(env, nc, Wo2, lambda k, c0, cwid: ovT[k][:, c0:c0 + cwid],
                           R2T, R3T, xlnT_o, xlntok_o, scores_o, vals_o, hist_o,
                           wg, idb_s, onesr, free_rhs=esC.close)
    return nc


def l4_inputs(inputs, core, l3b_outs, l3ball, l3outs):
    zm = np.zeros((128, 8 * NCORES), np.float32)
    for j in range(NCORES):
        if not (j < core and j // 4 == core // 4):
            zm[:, 8 * j:8 * (j + 1)] = -2.0e30
    return {
        "R2T": l3outs["R2T"],
        "qT": l3b_outs["qT_o"], "kT": l3b_outs["kT_o"], "sT": l3b_outs["sT_o"],
        "ktok": l3b_outs["ktok_o"], "vtok": l3b_outs["vtok_o"], "gtok": l3b_outs["gtok_o"],
        "hk_all": np.stack([l3ball[j]["hk_o"] for j in range(NCORES)]),
        "hv_all": np.stack([l3ball[j]["hv_o"] for j in range(NCORES)]),
        "zc_all": np.stack([l3ball[j]["zc_o"] for j in range(NCORES)]),
        "zmask": zm,
        "pm01": np.tile((np.array([[1.0 if (j < core and j // 4 == core // 4) else 0.0
                                     for j in range(NCORES)]], np.float32)), (128, 1)),
        "Wo2": np.asarray(inputs["L0_attn_b_Wo"]),
        "gnwt": np.tile(np.asarray(inputs["L0_attn_b_g_norm_weight"]), H_B),
        "wg": np.asarray(inputs["L0_ffn_w_gate"]),
        "idb": np.eye(128, dtype=bfnp),
        "idf": np.eye(128, dtype=np.float32),
        "utri": np.triu(np.ones((128, 128), np.float32)),
    }



def kernel(**inputs):
    cores = list(range(NCORES))
    nc1 = build_L1()
    res1 = run_bass_kernel_spmd(nc1, [l1_inputs(inputs, c) for c in cores], core_ids=cores).results
    nc2 = build_L2()
    res2 = run_bass_kernel_spmd(nc2, [l2_inputs(inputs, c, res1[c], res1) for c in cores], core_ids=cores).results
    nc3 = build_L3()
    res3 = run_bass_kernel_spmd(nc3, [l3_inputs(inputs, c, res2[c], res2) for c in cores], core_ids=cores).results
    nc3b = build_L3b()
    res3b = run_bass_kernel_spmd(nc3b, [l3b_inputs(inputs, c, res3[c]) for c in cores], core_ids=cores).results
    nc4 = build_L4()
    res4 = run_bass_kernel_spmd(nc4, [l4_inputs(inputs, c, res3b[c], res3b, res3[c]) for c in cores], core_ids=cores).results
    m4 = [dict(R1T=res4[c]["R3T"], xlnT_o=res4[c]["xlnT_o"], xlntok_o=res4[c]["xlntok_o"],
               scores_o=res4[c]["scores_o"], vals_o=res4[c]["vals_o"], hist_o=res4[c]["hist_o"])
          for c in cores]
    res5 = run_bass_kernel_spmd(nc3, [l3_inputs(inputs, c, m4[c], m4) for c in cores], core_ids=cores).results
    out = np.empty((2, 8192, D), np.float32)
    for c in cores:
        b, t0 = c // 4, (c % 4) * T
        out[b, t0:t0 + T, :] = res5[c]["R2T"].T
    return out
